# version 03
# baseline (speedup 1.0000x reference)
import numpy as np
import ml_dtypes
from contextlib import ExitStack

import concourse.bass as bass
import concourse.mybir as mybir
from concourse.bass_utils import run_bass_kernel_spmd

F32 = mybir.dt.float32
BF16 = mybir.dt.bfloat16
U8 = mybir.dt.uint8
AF = mybir.ActivationFunctionType
ALU = mybir.AluOpType
AX = mybir.AxisListType
NPBF = ml_dtypes.bfloat16

B = 2
T = 4096
D = 1024
H, DK, DV = 8, 64, 128
R = 64
F = 4096
E = 8
EH = 2048
C = 128
NCH = T // C
TOKT = 512
NTA = T // TOKT
DECAY, THR = 0.9, 0.5
EPS6, EPS5 = 1e-6, 1e-5

PHASES = "ABCD"
TEST_TAPS = ()
A_LIMIT = 999
CONV_ON = True
B_LIMIT = 999
B_STAGE = 99
C_LIMIT = 999
D_LIMIT = 999
C_STAGE = 99

CC_EPS6 = 0; CC_EPS5 = 1; CC_NHALF = 2; CC_HALF = 3; CC_NONE = 4; CC_I1024 = 5
CC_I128 = 6
CC_TINY = 7
NCONST = 8


class Prog:
    ENGS = ("sync", "gpsimd", "tensor", "vector", "scalar")

    def __init__(self, nc, ctx):
        self.nc = nc
        self.ctx = ctx
        self.items = {e: [] for e in self.ENGS}
        self.sems = {}
        self.cnt = {}
        self.qsems = set()
        self.ready = {}
        self.free = {}
        self._wmax = {e: {} for e in self.ENGS}

    def sem(self, name, queue=False):
        self.sems[name] = self.ctx.enter_context(self.nc.semaphore(name))
        self.cnt[name] = 0
        if queue:
            self.qsems.add(name)
        return name

    def op(self, eng, fn, inc=None, n=1):
        if inc is not None:
            self.cnt[inc] += n
        self.items[eng].append(("op", fn, inc, n))
        return self.cnt[inc] if inc is not None else None

    def wait(self, eng, sem, val=None):
        v = self.cnt[sem] if val is None else val
        if v <= 0:
            return
        wm = self._wmax[eng]
        if wm.get(sem, -1) >= v:
            return
        wm[sem] = v
        self.items[eng].append(("wait", sem, v))

    def dma(self, eng, out, in_, inc, **kw):
        return self.op(eng, (lambda e, o=out, i=in_, kw=kw: e.dma_start(out=o, in_=i, **kw)), inc=inc, n=16)

    def val(self, sem):
        return self.cnt[sem]

    def _resolve(self, s_, v_):
        return self.cnt[s_] if s_ in self.qsems else v_

    def req_free(self, eng, buf):
        for s_, v_ in self.free.pop(buf, ()):
            self.wait(eng, s_, self._resolve(s_, v_))

    def req_ready(self, eng, buf):
        if buf in self.ready:
            s_, v_ = self.ready[buf]
            self.wait(eng, s_, self._resolve(s_, v_))

    def set_ready(self, buf, sem):
        self.ready[buf] = (sem, self.cnt[sem])

    def set_free(self, buf, sem):
        lst = self.free.setdefault(buf, [])
        for i, (s_, _) in enumerate(lst):
            if s_ == sem:
                lst[i] = (sem, self.cnt[sem])
                break
        else:
            lst.append((sem, self.cnt[sem]))

    def verify(self):
        pos = {e: 0 for e in self.ENGS}
        cnt = {s_: 0 for s_ in self.cnt}
        progress = True
        while progress:
            progress = False
            for e in self.ENGS:
                items = self.items[e]
                while pos[e] < len(items):
                    it = items[pos[e]]
                    if it[0] == "op":
                        if it[2] is not None:
                            cnt[it[2]] += it[3]
                        pos[e] += 1
                        progress = True
                    else:
                        _, s_, v_ = it
                        if cnt[s_] >= v_:
                            pos[e] += 1
                            progress = True
                        else:
                            break
        stuck = {e: pos[e] for e in self.ENGS if pos[e] < len(self.items[e])}
        if stuck:
            msg = []
            for e, i in stuck.items():
                it = self.items[e][i]
                msg.append(f"{e}@{i}/{len(self.items[e])}: wait {it[1]} >= {it[2]} (have {cnt[it[1]]})")
            raise AssertionError("DEADLOCK: " + "; ".join(msg))

    def playback(self, block):
        for eng in self.ENGS:
            items = self.items[eng]

            def run(e, items=items):
                for it in items:
                    if it[0] == "op":
                        _, fn, incname, n = it
                        instr = fn(e)
                        if incname is not None:
                            instr.then_inc(self.sems[incname], n)
                    else:
                        _, s, v = it
                        e.wait_ge(self.sems[s], v)
            getattr(block, eng)(run)


def host_inputs(full):
    g = lambda n: np.asarray(full["L0_attn_" + n])
    gf = lambda n: np.asarray(full["L0_ffn_" + n])
    d = {}
    for n in ("wq", "wk", "wv", "w_spike", "w_au", "w_ad", "w_beta", "w_out",
              "w_u1", "w_u2", "w_ff1", "w_ff3", "w_ff2"):
        d[n] = np.ascontiguousarray(g(n), np.float32)
    d["moe_W1"] = np.ascontiguousarray(gf("W1"), np.float32)
    d["moe_W2"] = np.ascontiguousarray(gf("W2"), np.float32)
    d["moe_Wg"] = np.ascontiguousarray(gf("Wg"), np.float32)
    for n, key in (("w_norm_in", "wn"), ("conv_b", "cb"), ("b_u2", "bu2"),
                   ("w_ffn", "wffn")):
        d[key] = np.ascontiguousarray(g(n).reshape(8, 128).T, np.float32)
    d["conv_wp"] = np.ascontiguousarray(
        g("conv_w").reshape(8, 128, 4).transpose(1, 0, 2).reshape(128, 32), np.float32)
    d["b_au_p"] = np.ascontiguousarray(g("b_au").reshape(R, 1), np.float32)
    d["b_ad_p"] = np.ascontiguousarray(g("b_ad").reshape(4, 128).T, np.float32)
    d["b_u1_p"] = np.ascontiguousarray(g("b_u1").reshape(R, 1), np.float32)
    asp = g("alpha_spike")[0]
    d["aspk"] = np.ascontiguousarray(
        asp.reshape(4, 2, 64).transpose(1, 2, 0).reshape(128, 4), np.float32)
    bsp = g("beta_spike")[0]
    bb_bd = np.zeros((128, 8), np.float32)
    for hp in range(4):
        for j in range(2):
            bb_bd[j * 64:(j + 1) * 64, 2 * hp + j] = bsp[2 * hp + j]
    d["bspk_bd"] = bb_bd
    ones_bd = np.zeros((128, 2), np.float32)
    ones_bd[0:64, 0] = 1.0
    ones_bd[64:128, 1] = 1.0
    d["ones_bd"] = ones_bd
    d["b_beta_p"] = np.ascontiguousarray(g("b_beta").reshape(4, 2).T, np.float32)
    d["b_beta_col"] = np.ascontiguousarray(g("b_beta").reshape(8, 1), np.float32)
    d["w_hn_b"] = np.ascontiguousarray(
        np.broadcast_to(g("w_hn").reshape(1, 8 * 128), (128, 8 * 128)), np.float32)
    d["b1_p"] = np.ascontiguousarray(
        gf("b1").reshape(E, 16, 128).transpose(2, 0, 1).reshape(128, E * 16), np.float32)
    d["b2_lhsT"] = np.ascontiguousarray(gf("b2"), np.float32).astype(NPBF)
    d["bg_p"] = np.ascontiguousarray(gf("bg").reshape(E, 1), np.float32)
    sel2 = np.zeros((2, 128), np.float32)
    sel2[0, 0:64] = 1.0
    sel2[1, 64:128] = 1.0
    d["sel2"] = sel2
    d["nsel2"] = -sel2
    qsel = np.zeros((4, 128, 8), np.float32)
    for s in range(4):
        qsel[s, 0:64, 2 * s] = 1.0
        qsel[s, 64:128, 2 * s + 1] = 1.0
    d["qknorm_sel"] = qsel
    nbs = np.zeros((4, 8, 128), np.float32)
    for hp in range(4):
        nbs[hp, 2 * hp, 0:64] = 1.0
        nbs[hp, 2 * hp + 1, 64:128] = 1.0
    d["normbcast_sel"] = nbs
    idx = np.arange(128)
    d["mask_rt"] = (idx[None, :] > idx[:, None]).astype(np.uint8)
    d["mask_tr"] = (idx[:, None] > idx[None, :]).astype(np.uint8)
    d["mask_rt_incl"] = (idx[None, :] >= idx[:, None]).astype(np.uint8)
    for n in ("mask_rt", "mask_tr", "mask_rt_incl"):
        d[n + "4"] = np.ascontiguousarray(np.tile(d[n], (1, 4)))
    whn = g("w_hn")
    wb_ = np.zeros((16, 128), np.float32)
    for hp in range(4):
        for b_ in range(2):
            for h2 in range(2):
                wb_[(hp * 2 + b_) * 2 + h2] = whn[2 * hp + h2]
    d["whn_bh"] = np.ascontiguousarray(np.broadcast_to(wb_.reshape(1, 16 * 128), (128, 16 * 128)))
    d["ident_bf"] = np.eye(128, dtype=NPBF)
    d["ident_f32"] = np.eye(128, dtype=np.float32)
    d["ones1_f32"] = np.ones((1, 128), np.float32)
    d["ones1_bf"] = np.ones((1, 128), NPBF)
    d["onescol_f32"] = np.ones((128, 1), np.float32)
    cst = np.zeros((128, NCONST), np.float32)
    cst[:, CC_EPS6] = EPS6
    cst[:, CC_EPS5] = EPS5
    cst[:, CC_NHALF] = -0.5
    cst[:, CC_HALF] = 0.5
    cst[:, CC_NONE] = -1.0
    cst[:, CC_I1024] = 1.0 / 1024.0
    cst[:, CC_I128] = 1.0 / 128.0
    cst[:, CC_TINY] = 1e-30
    d["consts"] = cst

    x = np.asarray(full["L0_attn_x"], np.float32)
    percore = []
    for c in range(8):
        pc = dict(d)
        pc["x"] = np.ascontiguousarray(x[2 * c:2 * c + 2])
        percore.append(pc)
    return percore


def build():
    nc = bass.Bass("TRN2", target_bir_lowering=False)
    I = lambda n, s, dt=F32: nc.dram_tensor(n, s, dt, kind="ExternalInput").ap()
    O = lambda n, s, dt=F32: nc.dram_tensor(n, s, dt, kind="ExternalOutput").ap()

    def SCR(n, s, dt):
        kind = "ExternalOutput" if n in TEST_TAPS else "Internal"
        return nc.dram_tensor(n, s, dt, kind=kind).ap()

    xin = I("x", [B, T, D])
    w = {}
    for n, sh in (("wq", [D, 512]), ("wk", [D, 512]), ("wv", [D, 1024]),
                  ("w_spike", [D, 512]), ("w_au", [D, R]), ("w_ad", [R, 512]),
                  ("w_beta", [D, 8]), ("w_out", [D, D]), ("w_u1", [D, R]),
                  ("w_u2", [R, D]), ("w_ff1", [D, F]), ("w_ff3", [D, F]),
                  ("w_ff2", [F, D]), ("moe_W1", [E, D, EH]), ("moe_W2", [E, EH, D]),
                  ("moe_Wg", [D, E]), ("wn", [128, 8]), ("cb", [128, 8]),
                  ("bu2", [128, 8]), ("wffn", [128, 8]), ("conv_wp", [128, 32]),
                  ("b_au_p", [R, 1]), ("b_ad_p", [128, 4]), ("b_u1_p", [R, 1]),
                  ("aspk", [128, 4]), ("b_beta_p", [2, 4]), ("b_beta_col", [8, 1]),
                  ("w_hn_b", [128, 1024]), ("b1_p", [128, E * 16]), ("bg_p", [E, 1]),
                  ("qknorm_sel", [4, 128, 8]), ("normbcast_sel", [4, 8, 128]),
                  ("ident_f32", [128, 128]), ("ones1_f32", [1, 128]),
                  ("onescol_f32", [128, 1]), ("consts", [128, NCONST])):
        w[n] = I(n, sh)
    for n, sh in (("bspk_bd", [128, 8]), ("ones_bd", [128, 2]), ("sel2", [2, 128]),
                  ("nsel2", [2, 128])):
        w[n] = I(n, sh)
    for n, sh in (("ident_bf", [128, 128]), ("ones1_bf", [1, 128]),
                  ("b2_lhsT", [E, D])):
        w[n] = I(n, sh, BF16)
    for n in ("mask_rt", "mask_tr", "mask_rt_incl"):
        w[n] = I(n, [128, 128], U8)
        w[n + "4"] = I(n + "4", [128, 512], U8)
    w["whn_bh"] = I("whn_bh", [128, 2048])

    out = O("out", [B, T, D])

    x_fm = SCR("x_fm", [B, 8, 128, T], F32)
    q_sp = SCR("q_sp", [B, 4, 128, T], BF16)
    k_sp = SCR("k_sp", [B, 4, 128, T], BF16)
    v_sp = SCR("v_sp", [B, 8, 128, T], BF16)
    sd_sp = SCR("sd_sp", [B, 4, 128, T], F32)
    ab_sp = SCR("ab_sp", [B, 4, 128, T], F32)
    bb_sp = SCR("bb_sp", [B, 8, T], F32)
    y_sp = SCR("y_sp", [B, H, T, DV], BF16)
    z1_sp = SCR("z1_sp", [B, 8, 128, T], F32)
    coef_sp = SCR("coef_sp", [B, E, T], BF16)
    w1b = SCR("w1b", [E, D, EH], BF16)
    w2b = SCR("w2b", [E, EH, D], BF16)
    ff1b = SCR("ff1b", [D, F], BF16)
    ff3b = SCR("ff3b", [D, F], BF16)
    ff2b = SCR("ff2b", [F, D], BF16)
    bdbg = SCR("bdbg", [24, 128, 2048], F32)

    with ExitStack() as ctx:
        p = Prog(nc, ctx)
        sb = lambda n, s, dt=F32: ctx.enter_context(nc.sbuf_tensor(n, s, dt))[:]
        psum = lambda n, s, dt=F32: ctx.enter_context(nc.psum_tensor(n, s, dt))[:]
        dsems = (["qS", "qW", "c0", "c1", "st", "dxf", "dbbst", "dynb"] +
                 [f"dx{i}" for i in range(2)] + [f"devb{i}" for i in range(2)] +
                 [f"devf{i}" for i in range(2)] +
                 [f"d{nm}{i}" for nm in ("qc", "kc", "sdc", "abc", "bbr", "vtm",
                                          "yc", "xcb", "cf", "zf") for i in range(2)])
        for s_ in dsems:
            p.sem(s_, queue=True)
        for s_ in ("t", "v", "s", "g"):
            p.sem(s_)

        banks = [psum(f"bank{i}", [128, 512]) for i in range(8)]

        def barrier():
            for e in Prog.ENGS:
                for s2 in ("t", "v", "s", "g"):
                    p.wait(e, s2)
                for qs in sorted(p.qsems):
                    p.wait(e, qs)

        cst = sb("cst", [128, NCONST])
        masks = {n: sb(n + "_t", [128, 128], U8) for n in ("mask_rt", "mask_tr", "mask_rt_incl")}
        masks4 = {n: sb(n + "4_t", [128, 512], U8) for n in ("mask_rt", "mask_tr", "mask_rt_incl")}
        whnbhB = sb("whnbhB", [128, 2048])
        identb = sb("identb", [128, 128], BF16)
        identf = sb("identf", [128, 128])
        ones1f = sb("ones1f", [1, 128])
        ones1b = sb("ones1b", [1, 128], BF16)
        onescolf = sb("onescolf", [128, 1])
        sel2b = sb("sel2b", [2, 128])
        nsel2b = sb("nsel2b", [2, 128])
        onesbd = sb("onesbd", [128, 2])
        bspkbd = sb("bspkbd", [128, 8])
        wnB = sb("wnB", [128, 8])
        cbB = sb("cbB", [128, 8])
        bu2B = sb("bu2B", [128, 8])
        wffnB = sb("wffnB", [128, 8])
        convw = sb("convw", [128, 32])
        bauB = sb("bauB", [R, 1])
        badB = sb("badB", [128, 4])
        bu1B = sb("bu1B", [R, 1])
        aspkB = sb("aspkB", [128, 4])
        bbetaB = sb("bbetaB", [2, 4])
        bbetacB = sb("bbetacB", [8, 1])
        whnB = sb("whnB", [128, 1024])
        b1B = sb("b1B", [128, E * 16])
        b2B = sb("b2B", [E, D], BF16)
        bgB = sb("bgB", [E, 1])
        qkselB = sb("qkselB", [128, 32])
        nbselB = sb("nbselB", [8, 512])

        actx = ExitStack()
        sbA = lambda n, sh, dt=F32: actx.enter_context(nc.sbuf_tensor(n, sh, dt, side="right"))[:]
        wA = {}
        for n, cols in (("wq", 512), ("wk", 512), ("wv", 1024), ("w_spike", 512),
                        ("w_au", R), ("w_beta", 8)):
            wA[n] = sbA(n + "_b", [128, 8 * cols], BF16)
        wadB = sbA("wad_b", [R, 512], BF16)

        for ap_t, src in ((cst, w["consts"]), (identf, w["ident_f32"]),
                          (ones1f, w["ones1_f32"]), (onescolf, w["onescol_f32"]),
                          (wnB, w["wn"]), (cbB, w["cb"]), (bu2B, w["bu2"]),
                          (wffnB, w["wffn"]), (convw, w["conv_wp"]), (bauB, w["b_au_p"]),
                          (badB, w["b_ad_p"]), (bu1B, w["b_u1_p"]), (aspkB, w["aspk"]),
                          (bbetaB, w["b_beta_p"]), (bbetacB, w["b_beta_col"]),
                          (whnB, w["w_hn_b"]), (b1B, w["b1_p"]), (bgB, w["bg_p"]),
                          (masks["mask_rt"], w["mask_rt"]), (masks["mask_tr"], w["mask_tr"]),
                          (masks["mask_rt_incl"], w["mask_rt_incl"]),
                          (masks4["mask_rt"], w["mask_rt4"]), (masks4["mask_tr"], w["mask_tr4"]),
                          (masks4["mask_rt_incl"], w["mask_rt_incl4"]), (whnbhB, w["whn_bh"]),
                          (identb, w["ident_bf"]), (ones1b, w["ones1_bf"]),
                          (sel2b, w["sel2"]), (nsel2b, w["nsel2"]),
                          (onesbd, w["ones_bd"]), (bspkbd, w["bspk_bd"]),
                          (b2B, w["b2_lhsT"])):
            p.dma("sync", ap_t, src, "qS")
        for s_ in range(4):
            p.dma("sync", qkselB[:, 8 * s_:8 * s_ + 8], w["qknorm_sel"][s_], "qS")
            p.dma("sync", nbselB[:, 128 * s_:128 * s_ + 128], w["normbcast_sel"][s_], "qS")
        for n, cols in (("wq", 512), ("wk", 512), ("wv", 1024), ("w_spike", 512),
                        ("w_au", R), ("w_beta", 8)):
            for ks in range(8):
                p.dma("gpsimd", wA[n][:, ks * cols:(ks + 1) * cols],
                      w[n][ks * 128:(ks + 1) * 128, :], "qW")
        p.dma("gpsimd", wadB, w["w_ad"], "qW")
        for eng in ("tensor", "vector", "scalar", "gpsimd"):
            p.wait(eng, "qS")
        p.wait("tensor", "qW")

        cvt = [sb(f"cvt{i}", [128, 4096], BF16) for i in range(2)]
        nconv = [0]

        def conv_blk(dst, src, nrows):
            for r0 in range(0, nrows, 128):
                i = nconv[0] % 2
                cs = "c0" if i == 0 else "c1"
                p.wait("gpsimd", cs, p.val(cs))
                ncols = dst.shape[-1]
                p.dma("gpsimd", cvt[i][:, 0:ncols], src[r0:r0 + 128, :], cs)
                p.wait("gpsimd", cs, p.val(cs))
                p.dma("gpsimd", dst[r0:r0 + 128, :], cvt[i][:, 0:ncols], cs)
                nconv[0] += 1
        if CONV_ON:
            for e in range(E):
                conv_blk(w1b[e], w["moe_W1"][e], D)
                conv_blk(w2b[e], w["moe_W2"][e], EH)
            conv_blk(ff1b, w["w_ff1"], D)
            conv_blk(ff3b, w["w_ff3"], D)
            conv_blk(ff2b, w["w_ff2"], F)
        conv_done = (p.val("c0"), p.val("c1"))

        if "A" in PHASES:
            sb_ = sb
            sb = sbA
            xtok = [sb(f"xtok{i}", [128, 4 * 1024]) for i in range(2)]
            xf = sb("xf", [128, 8 * 512])
            sqt = sb("sqt", [128, 512])
            h0 = sb("h0", [128, 8 * 515])
            cacc = [sb(f"cacc{i}", [128, 512]) for i in range(2)]
            hB = sb("hB", [128, 8 * 512], BF16)
            qf = sb("qf", [128, 512])
            evb = [sb(f"evb{i}", [128, 512], BF16) for i in range(2)]
            evf = [sb(f"evf{i}", [128, 512]) for i in range(2)]
            auB = sb("auB", [R, 512], BF16)
            rowf = sb("rowf", [8, 512])
            rmsrow = sb("rmsrow", [1, 512])
            bbst = sb("bbst", [8, 512])
            psT = [banks[0][:, 0:128], banks[1][:, 0:128]]
            psP = banks[2:5]
            psR = banks[5][0:1, :]
            psN = banks[6][0:8, :]
            psBC = banks[7]
            sb = sb_

            for b in range(B):
                for ti in range(NTA):
                    tix = b * NTA + ti
                    if ti >= A_LIMIT:
                        continue
                    xb = xtok[tix % 2]
                    xbn = f"xtok{tix % 2}"
                    p.req_free("sync", xbn)
                    for j in range(4):
                        p.dma("sync", xb[:, 1024 * j:1024 * (j + 1)],
                              xin[b, ti * 512 + 128 * j: ti * 512 + 128 * (j + 1), :], f"dx{tix % 2}")
                    p.set_ready(xbn, f"dx{tix % 2}")
                    p.req_ready("tensor", xbn)
                    p.req_free("scalar", "xf")
                    for s_ in range(8):
                        for j in range(4):
                            half = (s_ * 4 + j) % 2
                            p.req_free("tensor", f"psT{half}")
                            p.op("tensor", (lambda t, s=s_, j=j, h=half, xb=xb:
                                  t.transpose(psT[h], xb[:, 1024 * j + 128 * s: 1024 * j + 128 * (s + 1)],
                                              identf)), inc="t")
                            p.set_ready(f"psT{half}", "t")
                            p.req_ready("scalar", f"psT{half}")
                            p.op("scalar", (lambda sc, s=s_, j=j, h=half:
                                  sc.copy(xf[:, 512 * s + 128 * j: 512 * s + 128 * (j + 1)],
                                          psT[h])), inc="s")
                            p.set_free(f"psT{half}", "s")
                    p.set_free(xbn, "t")
                    p.set_ready("xf", "s")
                    p.req_ready("gpsimd", "xf")
                    for s_ in range(8):
                        p.dma("gpsimd", x_fm[b, s_, :, ti * 512:(ti + 1) * 512],
                              xf[:, 512 * s_:512 * (s_ + 1)], "dxf")
                    p.set_free("xf", "dxf")
                    p.req_ready("vector", "xf")
                    p.req_free("tensor", "psR")
                    for s_ in range(8):
                        p.req_free("vector", "sqt")
                        p.op("vector", (lambda v, s=s_: v.tensor_mul(
                            sqt, xf[:, 512 * s:512 * (s + 1)], xf[:, 512 * s:512 * (s + 1)])), inc="v")
                        p.set_ready("sqt", "v")
                        p.req_ready("tensor", "sqt")
                        p.op("tensor", (lambda t, s=s_: t.matmul(
                            psR, onescolf, sqt, start=(s == 0), stop=(s == 7))), inc="t")
                        p.set_free("sqt", "t")
                    p.set_ready("psR", "t")
                    p.req_ready("scalar", "psR")
                    p.req_free("scalar", "rmsrow")
                    p.op("scalar", lambda sc: sc.activation(
                        rmsrow, psR, AF.Ln, bias=cst[0:1, CC_EPS6:CC_EPS6 + 1],
                        scale=cst[0:1, CC_I1024:CC_I1024 + 1]), inc="s")
                    p.set_free("psR", "s")
                    p.wait("scalar", "s", p.val("s"))
                    p.op("scalar", lambda sc: sc.activation(
                        rmsrow, rmsrow, AF.Exp, scale=cst[0:1, CC_NHALF:CC_NHALF + 1]), inc="s")
                    p.set_ready("rmsrow", "s")
                    p.req_ready("tensor", "rmsrow")
                    p.req_free("tensor", "psBC")
                    p.op("tensor", lambda t: t.matmul(psBC, ones1f, rmsrow, start=True, stop=True), inc="t")
                    p.set_free("rmsrow", "t")
                    p.set_ready("psBC", "t")
                    p.req_ready("vector", "psBC")
                    for s_ in range(8):
                        if ti == 0:
                            p.op("vector", (lambda v, s=s_: v.memset(h0[:, 515 * s:515 * s + 3], 0.0)), inc="v")
                        else:
                            p.op("vector", (lambda v, s=s_: v.tensor_copy(
                                h0[:, 515 * s:515 * s + 3], h0[:, 515 * s + 512:515 * s + 515])), inc="v")
                        p.wait("vector", "v", p.val("v"))
                        p.op("vector", (lambda v, s=s_: v.scalar_tensor_tensor(
                            h0[:, 515 * s + 3:515 * s + 515], psBC, wnB[:, s:s + 1],
                            xf[:, 512 * s:512 * (s + 1)], ALU.mult, ALU.mult)), inc="v")
                        p.wait("vector", "v", p.val("v"))
                    p.set_free("psBC", "v")
                    p.set_free("xf", "v")
                    for s_ in range(8):
                        base = 515 * s_
                        p.req_free("vector", "cacc0")
                        p.op("vector", (lambda v, b_=base, s=s_: v.tensor_scalar(
                            cacc[0], h0[:, b_:b_ + 512], convw[:, 4 * s:4 * s + 1], None, ALU.mult)), inc="v")
                        p.wait("vector", "v", p.val("v"))
                        p.req_free("vector", "cacc1")
                        p.op("vector", (lambda v, b_=base, s=s_: v.scalar_tensor_tensor(
                            cacc[1], h0[:, b_ + 1:b_ + 513], convw[:, 4 * s + 1:4 * s + 2],
                            cacc[0], ALU.mult, ALU.add)), inc="v")
                        p.wait("vector", "v", p.val("v"))
                        p.op("vector", (lambda v, b_=base, s=s_: v.scalar_tensor_tensor(
                            cacc[0], h0[:, b_ + 2:b_ + 514], convw[:, 4 * s + 2:4 * s + 3],
                            cacc[1], ALU.mult, ALU.add)), inc="v")
                        p.wait("vector", "v", p.val("v"))
                        p.op("vector", (lambda v, b_=base, s=s_: v.scalar_tensor_tensor(
                            cacc[1], h0[:, b_ + 3:b_ + 515], convw[:, 4 * s + 3:4 * s + 4],
                            cacc[0], ALU.mult, ALU.add)), inc="v")
                        p.set_free("cacc0", "v")
                        p.set_ready("cacc1", "v")
                        p.req_ready("scalar", "cacc1")
                        if s_ == 0:
                            p.req_free("scalar", "hB")
                        p.op("scalar", (lambda sc, s=s_: sc.activation(
                            hB[:, 512 * s:512 * (s + 1)], cacc[1], AF.Silu,
                            bias=cbB[:, s:s + 1])), inc="s")
                        p.set_free("cacc1", "s")
                    p.set_ready("hB", "s")

                    outsl = ([("q", m) for m in range(4)] + [("k", m) for m in range(4)] +
                             [("v", m) for m in range(8)] + [("sd", m) for m in range(4)] +
                             [("au", 0), ("bb", 0)])
                    p.req_ready("tensor", "hB")
                    bank_i = [0]

                    def proj_mm(wt, m, mw, cols):
                        bk = bank_i[0] % 3
                        bank_i[0] += 1
                        p.req_free("tensor", f"psP{bk}")
                        for ks in range(8):
                            p.op("tensor", (lambda t, ks=ks, wt=wt, m=m, bk=bk, cols=cols, mw=mw:
                                  t.matmul(psP[bk][0:mw, :],
                                           wA[wt][:, ks * cols + m * 128: ks * cols + m * 128 + mw],
                                           hB[:, 512 * ks:512 * (ks + 1)],
                                           start=(ks == 0), stop=(ks == 7))), inc="t")
                        p.set_ready(f"psP{bk}", "t")
                        return bk

                    for name, m in outsl:
                        if name in ("q", "k"):
                            wt = "wq" if name == "q" else "wk"
                            bk = proj_mm(wt, m, 128, 512)
                            p.req_ready("scalar", f"psP{bk}")
                            p.req_free("scalar", "qf")
                            p.op("scalar", (lambda sc, bk=bk: sc.copy(qf, psP[bk])), inc="s")
                            p.set_free(f"psP{bk}", "s")
                            p.set_ready("qf", "s")
                            p.req_ready("vector", "qf")
                            p.req_free("vector", "sqt")
                            p.op("vector", lambda v: v.tensor_mul(sqt, qf, qf), inc="v")
                            p.set_ready("sqt", "v")
                            p.req_ready("tensor", "sqt")
                            p.req_free("tensor", "psN")
                            p.op("tensor", (lambda t, m=m: t.matmul(
                                psN, qkselB[:, 8 * m:8 * (m + 1)], sqt, start=True, stop=True)), inc="t")
                            p.set_free("sqt", "t")
                            p.set_ready("psN", "t")
                            p.req_ready("scalar", "psN")
                            p.req_free("scalar", "rowf")
                            p.op("scalar", lambda sc: sc.activation(rowf, psN, AF.Ln,
                                                                     bias=cst[0:8, CC_TINY:CC_TINY + 1]), inc="s")
                            p.set_free("psN", "s")
                            for fn in (lambda sc: sc.activation(rowf, rowf, AF.Exp,
                                                                scale=cst[0:8, CC_HALF:CC_HALF + 1]),
                                       lambda sc: sc.activation(rowf, rowf, AF.Ln,
                                                                bias=cst[0:8, CC_EPS6:CC_EPS6 + 1]),
                                       lambda sc: sc.activation(rowf, rowf, AF.Exp,
                                                                scale=cst[0:8, CC_NONE:CC_NONE + 1])):
                                p.wait("scalar", "s", p.val("s"))
                                p.op("scalar", fn, inc="s")
                            p.set_ready("rowf", "s")
                            p.req_ready("tensor", "rowf")
                            p.req_free("tensor", "psBC")
                            p.op("tensor", (lambda t, m=m: t.matmul(
                                psBC, nbselB[:, 128 * m:128 * (m + 1)], rowf, start=True, stop=True)), inc="t")
                            p.set_free("rowf", "t")
                            p.set_ready("psBC", "t")
                            ei = 0 if name == "q" else 1
                            sp = q_sp if name == "q" else k_sp
                            p.req_ready("vector", "psBC")
                            p.req_free("vector", f"evb{ei}")
                            p.op("vector", (lambda v, ei=ei: v.tensor_mul(evb[ei], qf, psBC)), inc="v")
                            p.set_free("psBC", "v")
                            p.set_free("qf", "v")
                            p.set_ready(f"evb{ei}", "v")
                            p.req_ready("gpsimd", f"evb{ei}")
                            p.dma("gpsimd", sp[b, m, :, ti * 512:(ti + 1) * 512], evb[ei], f"devb{ei}")
                            p.set_free(f"evb{ei}", f"devb{ei}")
                        elif name == "v":
                            bk = proj_mm("wv", m, 128, 1024)
                            ei = m % 2
                            p.req_ready("scalar", f"psP{bk}")
                            p.req_free("scalar", f"evb{ei}")
                            p.op("scalar", (lambda sc, bk=bk, ei=ei: sc.copy(evb[ei], psP[bk])), inc="s")
                            p.set_free(f"psP{bk}", "s")
                            p.set_ready(f"evb{ei}", "s")
                            p.req_ready("gpsimd", f"evb{ei}")
                            p.dma("gpsimd", v_sp[b, m, :, ti * 512:(ti + 1) * 512], evb[ei], f"devb{ei}")
                            p.set_free(f"evb{ei}", f"devb{ei}")
                        elif name == "sd":
                            bk = proj_mm("w_spike", m, 128, 512)
                            ei = m % 2
                            p.req_ready("scalar", f"psP{bk}")
                            p.req_free("scalar", f"evf{ei}")
                            p.op("scalar", (lambda sc, bk=bk, ei=ei: sc.copy(evf[ei], psP[bk])), inc="s")
                            p.set_free(f"psP{bk}", "s")
                            p.set_ready(f"evf{ei}", "s")
                            p.req_ready("gpsimd", f"evf{ei}")
                            p.dma("gpsimd", sd_sp[b, m, :, ti * 512:(ti + 1) * 512], evf[ei], f"devf{ei}")
                            p.set_free(f"evf{ei}", f"devf{ei}")
                        elif name == "au":
                            bk = proj_mm("w_au", 0, R, R)
                            p.req_ready("scalar", f"psP{bk}")
                            p.req_free("scalar", "auB")
                            p.op("scalar", (lambda sc, bk=bk: sc.activation(
                                auB, psP[bk][0:R, :], AF.Silu, bias=bauB)), inc="s")
                            p.set_free(f"psP{bk}", "s")
                            p.set_ready("auB", "s")
                            p.req_ready("tensor", "auB")
                            for m2 in range(4):
                                bk2 = bank_i[0] % 3
                                bank_i[0] += 1
                                p.req_free("tensor", f"psP{bk2}")
                                p.op("tensor", (lambda t, m2=m2, bk2=bk2: t.matmul(
                                    psP[bk2], wadB[:, m2 * 128:(m2 + 1) * 128], auB,
                                    start=True, stop=True)), inc="t")
                                p.set_ready(f"psP{bk2}", "t")
                                ei = m2 % 2
                                p.req_ready("scalar", f"psP{bk2}")
                                p.req_free("scalar", f"evf{ei}")
                                p.op("scalar", (lambda sc, bk2=bk2, m2=m2, ei=ei: sc.activation(
                                    evf[ei], psP[bk2], AF.Identity, bias=badB[:, m2:m2 + 1])), inc="s")
                                p.set_free(f"psP{bk2}", "s")
                                p.set_ready(f"evf{ei}", "s")
                                p.req_ready("gpsimd", f"evf{ei}")
                                p.dma("gpsimd", ab_sp[b, m2, :, ti * 512:(ti + 1) * 512], evf[ei], f"devf{ei}")
                                p.set_free(f"evf{ei}", f"devf{ei}")
                            p.set_free("auB", "t")
                        else:
                            bk = proj_mm("w_beta", 0, 8, 8)
                            p.req_ready("scalar", f"psP{bk}")
                            p.req_free("scalar", "bbst")
                            p.op("scalar", (lambda sc, bk=bk: sc.activation(
                                bbst, psP[bk][0:8, :], AF.Identity, bias=bbetacB)), inc="s")
                            p.set_free(f"psP{bk}", "s")
                            p.set_ready("bbst", "s")
                            p.req_ready("gpsimd", "bbst")
                            p.dma("gpsimd", bb_sp[b, :, ti * 512:(ti + 1) * 512], bbst, "dbbst")
                            p.set_free("bbst", "dbbst")
                    p.set_free("hB", "t")

        actx.close()
        barrier()

        bctx = ExitStack()
        if "B" in PHASES:
            sbB = lambda n, sh, dt=F32: bctx.enter_context(nc.sbuf_tensor(n, sh, dt, side="right"))[:]
            qc = [sbB(f"qc{i}", [128, 1024], BF16) for i in range(2)]
            kc = [sbB(f"kc{i}", [128, 1024], BF16) for i in range(2)]
            sdc = [sbB(f"sdc{i}", [128, 1024]) for i in range(2)]
            abc = [sbB(f"abc{i}", [128, 1024]) for i in range(2)]
            vtm = [sbB(f"vtm{i}", [128, 2048], BF16) for i in range(2)]
            bbr = [sbB(f"bbr{i}", [2, 1024]) for i in range(2)]
            spk = sbB("spk", [128, 1024])
            memt = sbB("memt", [128, 8])
            m1t = sbB("m1t", [128, 8])
            alf = sbB("alf", [128, 1024])
            alf2 = sbB("alf2", [128, 1024])
            Graw = sbB("Graw", [128, 1024])
            Gi = sbB("Gi", [128, 1024])
            rhot = sbB("rhot", [128, 8])
            rhoi = sbB("rhoi", [128, 8])
            rhog = sbB("rhog", [128, 8])
            onesB = sbB("onesB", [128, 64])
            zt4 = sbB("zt4", [128, 512])
            actr = sbB("actr", [2, 1024])
            ssr = sbB("ssr", [2, 1024])
            brow = sbB("brow", [2, 1024])
            abcast = sbB("abcast", [128, 1024])
            bbcn = sbB("bbcn", [128, 1024])
            bcolT = sbB("bcolT", [128, 16])
            af = sbB("af", [128, 1024])
            abuf = sbB("abuf", [128, 1024], BF16)
            qtb = sbB("qtb", [128, 1024], BF16)
            bhn = sbB("bhn", [128, 1024], BF16)
            tmpf = sbB("tmpf", [128, 1024])
            ahf = sbB("ahf", [128, 1024])
            aTb = sbB("aTb", [128, 1024], BF16)
            XS = [sbB(f"XS{i}", [128, 2048], BF16) for i in range(2)]
            XTS = [sbB(f"XTS{i}", [128, 2048], BF16) for i in range(2)]
            Ub = [sbB(f"Ub{i}", [128, 2048], BF16) for i in range(2)]
            g1t = sbB("g1t", [128, 2048], BF16)
            Sb = [sbB(f"Sb{i}", [128, 1024]) for i in range(2)]
            S0pb = sbB("S0pb", [128, 1024], BF16)
            qtz = sbB("qtz", [128, 2048], BF16)
            yf = sbB("yf", [128, 2048])
            sqy = sbB("sqy", [128, 512])
            ysum = sbB("ysum", [128, 16])
            yrst = sbB("yrst", [128, 16])
            ynb = sbB("ynb", [128, 2048], BF16)

            p.op("vector", lambda v: v.memset(memt, 0.0), inc="v")
            p.op("vector", lambda v: v.memset(onesB, 1.0), inc="v")
            p.op("vector", lambda v: v.memset(zt4, 0.0), inc="v")
            p.op("vector", lambda v: v.memset(Sb[0], 0.0), inc="v")
            p.wait("vector", "v", p.val("v"))
            init_v = p.val("v")
            p.wait("tensor", "v", init_v)
            p.wait("scalar", "v", init_v)
            p.wait("gpsimd", "v", init_v)

            def b_loads(ci):
                pb = ci % 2
                c0, c1 = ci * C, (ci + 1) * C
                for nm, buf, sp in (("qc", qc[pb], q_sp), ("kc", kc[pb], k_sp),
                                    ("sdc", sdc[pb], sd_sp), ("abc", abc[pb], ab_sp)):
                    p.req_free("gpsimd", f"{nm}{pb}")
                    bufr = buf.rearrange("p (hp b t) -> p hp b t", hp=4, b=2, t=C)
                    for b_ in range(2):
                        p.dma("gpsimd", bufr[:, :, b_, :],
                              sp[b_, :, :, c0:c1].rearrange("hp p t -> p hp t"), f"d{nm}{pb}")
                    p.set_ready(f"{nm}{pb}", f"d{nm}{pb}")
                p.req_free("gpsimd", f"bbr{pb}")
                bbrr = bbr[pb].rearrange("h2 (hp b t) -> h2 hp b t", hp=4, b=2, t=C)
                for b_ in range(2):
                    p.dma("gpsimd", bbrr[:, :, b_, :],
                          bb_sp[b_, :, c0:c1].rearrange("(hp h2) t -> h2 hp t", h2=2), f"dbbr{pb}")
                p.set_ready(f"bbr{pb}", f"dbbr{pb}")
                p.req_free("sync", f"vtm{pb}")
                for g in range(8):
                    hp, b_ = g // 2, g % 2
                    for h2 in range(2):
                        vi = g * 2 + h2
                        p.dma("sync", vtm[pb][:, vi * 128:(vi + 1) * 128],
                              v_sp[b_, 2 * hp + h2, :, c0:c1], f"dvtm{pb}", transpose=True)
                p.set_ready(f"vtm{pb}", f"dvtm{pb}")

            b_loads(0)
            for ci in range(min(NCH, B_LIMIT)):
                pb = ci % 2
                sc_, sn = Sb[pb], Sb[1 - pb]
                p.req_ready("vector", f"sdc{pb}")
                p.req_free("vector", "spk")
                for t_ in range(C):
                    p.op("vector", (lambda v, t=t_, pb=pb: v.scalar_tensor_tensor(
                        m1t, memt, DECAY, sdc[pb][:, t:1024:128], ALU.mult, ALU.add)), inc="v")
                    p.wait("vector", "v", p.val("v"))
                    p.op("vector", (lambda v, t=t_: v.tensor_single_scalar(
                        spk[:, t:1024:128], m1t, THR, ALU.is_gt)), inc="v")
                    p.wait("vector", "v", p.val("v"))
                    p.op("vector", (lambda v, t=t_: v.scalar_tensor_tensor(
                        memt, spk[:, t:1024:128], -THR, m1t, ALU.mult, ALU.add)), inc="v")
                    p.wait("vector", "v", p.val("v"))
                p.set_free(f"sdc{pb}", "v")
                p.set_ready("spk", "v")
                if ci + 1 < min(NCH, B_LIMIT):
                    b_loads(ci + 1)
                if B_STAGE < 1:
                    continue
                p.req_ready("vector", f"abc{pb}")
                for hp in range(4):
                    p.op("vector", (lambda v, hp=hp: v.tensor_scalar(
                        alf[:, hp * 256:(hp + 1) * 256], spk[:, hp * 256:(hp + 1) * 256],
                        aspkB[:, hp:hp + 1], None, ALU.mult)), inc="v")
                p.wait("vector", "v", p.val("v"))
                p.op("vector", (lambda v, pb=pb: v.tensor_add(alf2, alf, abc[pb])), inc="v")
                p.set_ready("alf2", "v")
                p.set_free(f"abc{pb}", "v")
                p.req_ready("scalar", "alf2")
                p.op("scalar", lambda sc: sc.activation(alf, alf2, AF.Sigmoid), inc="s")
                p.set_ready("alfsig", "s")
                if B_STAGE < 2:
                    continue
                p.req_ready("tensor", "spk")
                for half in range(2):
                    p.req_free("tensor", "bk6")
                    for hp in (2 * half, 2 * half + 1):
                        p.op("tensor", (lambda t, hp=hp, half=half: t.matmul(
                            banks[6][0:2, (hp - 2 * half) * 256:(hp - 2 * half + 1) * 256],
                            bspkbd[:, 2 * hp:2 * hp + 2],
                            spk[:, hp * 256:(hp + 1) * 256], start=True, stop=True)), inc="t")
                    p.set_ready("bk6", "t")
                    p.req_ready("scalar", "bk6")
                    p.op("scalar", (lambda sc, half=half: sc.copy(
                        ssr[:, half * 512:(half + 1) * 512], banks[6][0:2, :])), inc="s")
                    p.set_free("bk6", "s")
                    p.req_free("tensor", "bk7")
                    p.op("tensor", (lambda t, half=half: t.matmul(
                        banks[7][0:2, :], onesbd, spk[:, half * 512:(half + 1) * 512],
                        start=True, stop=True)), inc="t")
                    p.set_ready("bk7", "t")
                    p.req_ready("scalar", "bk7")
                    p.op("scalar", (lambda sc, half=half: sc.copy(
                        actr[:, half * 512:(half + 1) * 512], banks[7][0:2, :])), inc="s")
                    p.set_free("bk7", "s")
                p.set_ready("rows", "s")
                p.req_ready("vector", "rows")
                p.req_ready("vector", f"bbr{pb}")
                p.op("vector", (lambda v, pb=pb: v.tensor_add(brow, ssr, bbr[pb])), inc="v")
                p.set_free(f"bbr{pb}", "v")
                p.wait("vector", "v", p.val("v"))
                p.op("vector", lambda v: v.tensor_single_scalar(ssr, actr, 0.0, ALU.is_gt), inc="v")
                p.set_ready("actrow", "v")
                p.req_ready("scalar", "actrow")
                p.op("scalar", lambda sc: sc.activation(brow, brow, AF.Sigmoid), inc="s")
                p.wait("scalar", "s", p.val("s"))
                p.set_ready("brow_sig", "s")
                p.req_ready("vector", "brow_sig")
                p.op("vector", lambda v: v.tensor_mul(brow, brow, ssr), inc="v")
                p.set_ready("brow", "v")
                for half in range(2):
                    p.req_ready("tensor", "brow" if half == 0 else "brow")
                    p.req_free("tensor", "bk7")
                    p.op("tensor", (lambda t, half=half: t.matmul(
                        banks[7], sel2b, ssr[:, half * 512:(half + 1) * 512],
                        start=True, stop=True)), inc="t")
                    p.set_ready("bk7", "t")
                    p.req_ready("scalar", "bk7")
                    p.op("scalar", (lambda sc, half=half: sc.copy(
                        abcast[:, half * 512:(half + 1) * 512], banks[7])), inc="s")
                    p.set_free("bk7", "s")
                    p.req_free("tensor", "bk6")
                    p.op("tensor", (lambda t, half=half: t.matmul(
                        banks[6], nsel2b, brow[:, half * 512:(half + 1) * 512],
                        start=True, stop=True)), inc="t")
                    p.set_ready("bk6", "t")
                    p.req_ready("scalar", "bk6")
                    p.op("scalar", (lambda sc, half=half: sc.copy(
                        bbcn[:, half * 512:(half + 1) * 512], banks[6])), inc="s")
                    p.set_free("bk6", "s")
                p.set_ready("bcasts", "s")
                p.req_ready("vector", "alfsig")
                p.req_ready("vector", "bcasts")
                p.op("vector", lambda v: v.tensor_scalar(alf2, alf, -1.0, None, ALU.add), inc="v")
                p.wait("vector", "v", p.val("v"))
                p.op("vector", lambda v: v.tensor_mul(alf2, alf2, abcast), inc="v")
                p.wait("vector", "v", p.val("v"))
                p.op("vector", lambda v: v.tensor_scalar(alf2, alf2, 1.0, None, ALU.add), inc="v")
                p.wait("vector", "v", p.val("v"))
                if B_STAGE < 3:
                    continue
                p.req_free("vector", "Graw")
                for g in range(8):
                    for hf in range(2):
                        p.op("vector", (lambda v, g=g, hf=hf: v.tensor_tensor_scan(
                            Graw[:, g * 128 + hf * 64: g * 128 + (hf + 1) * 64],
                            alf2[:, g * 128 + hf * 64: g * 128 + (hf + 1) * 64],
                            onesB, 1.0, ALU.mult, ALU.mult)), inc="v")
                p.wait("vector", "v", p.val("v"))
                p.op("vector", lambda v: v.tensor_scalar(rhot, Graw[:, 63:1024:128], 1e-35, None, ALU.max), inc="v")
                p.wait("vector", "v", p.val("v"))
                p.op("vector", lambda v: v.reciprocal(rhoi, rhot), inc="v")
                p.wait("vector", "v", p.val("v"))
                for g in range(8):
                    p.op("vector", (lambda v, g=g: v.tensor_scalar(
                        Graw[:, g * 128:g * 128 + 64], Graw[:, g * 128:g * 128 + 64],
                        rhoi[:, g:g + 1], None, ALU.mult)), inc="v")
                p.wait("vector", "v", p.val("v"))
                p.op("vector", lambda v: v.tensor_scalar(Graw, Graw, 1e-36, 1e36, ALU.max, ALU.min), inc="v")
                p.wait("vector", "v", p.val("v"))
                p.set_ready("Graw", "v")
                p.req_free("vector", "Gi")
                p.op("vector", lambda v: v.reciprocal(Gi, Graw), inc="v")
                p.wait("vector", "v", p.val("v"))
                p.set_ready("Gi", "v")
                p.op("vector", lambda v: v.tensor_mul(rhog, rhot, Graw[:, 127:1024:128]), inc="v")
                p.set_ready("rhog", "v")
                p.req_ready("vector", "Gi")
                p.req_ready("vector", f"kc{pb}")
                p.req_ready("vector", f"qc{pb}")
                p.req_free("vector", "af")
                p.req_free("vector", "abuf")
                p.req_free("vector", "qtb")
                p.req_free("vector", "bhn")
                p.req_free("vector", "ahfw")
                p.op("vector", (lambda v, pb=pb: v.tensor_mul(af, kc[pb], Gi)), inc="v")
                p.wait("vector", "v", p.val("v"))
                p.op("vector", lambda v: v.tensor_copy(abuf, af), inc="v")
                p.set_ready("abuf", "v")
                p.op("vector", (lambda v, pb=pb: v.tensor_mul(qtb, qc[pb], Graw)), inc="v")
                p.set_ready("qtb", "v")
                p.set_free(f"qc{pb}", "v")
                p.op("vector", (lambda v, pb=pb: v.tensor_mul(tmpf, kc[pb], Graw)), inc="v")
                p.set_free(f"kc{pb}", "v")
                p.wait("vector", "v", p.val("v"))
                p.op("vector", lambda v: v.tensor_mul(bhn, tmpf, bbcn), inc="v")
                p.set_ready("bhn", "v")
                for g in range(8):
                    p.op("vector", (lambda v, g=g: v.tensor_scalar(
                        ahf[:, g * 128:(g + 1) * 128], af[:, g * 128:(g + 1) * 128],
                        Graw[:, g * 128 + 127:g * 128 + 128], None, ALU.mult)), inc="v")
                p.set_ready("ahf", "v")
                p.set_free("af", "v")
                p.set_free("Gi", "v")
                p.req_ready("vector", "Sb")
                p.req_free("vector", "S0pb")
                for g in range(8):
                    p.op("vector", (lambda v, g=g, sc_=sc_: v.tensor_scalar(
                        S0pb[:, g * 128:(g + 1) * 128], sc_[:, g * 128:(g + 1) * 128],
                        rhot[:, g:g + 1], None, ALU.mult)), inc="v")
                p.set_ready("S0pb", "v")
                if B_STAGE < 4:
                    continue
                p.req_ready("tensor", "brow")
                for g in range(8):
                    bki = 2 + (g % 2)
                    p.req_free("tensor", f"bk{bki}")
                    p.op("tensor", (lambda t, g=g, bki=bki: t.transpose(
                        banks[bki][:, 0:2], brow[0:2, g * 128:(g + 1) * 128], identf[0:2, 0:2])), inc="t")
                    p.set_ready(f"bk{bki}", "t")
                    p.req_ready("scalar", f"bk{bki}")
                    p.op("scalar", (lambda sc, g=g, bki=bki: sc.copy(
                        bcolT[:, 2 * g:2 * g + 2], banks[bki][:, 0:2])), inc="s")
                    p.set_free(f"bk{bki}", "s")
                p.set_ready("bcolT", "s")
                p.req_ready("tensor", "ahf")
                p.req_free("scalar", "aTb")
                for g in range(8):
                    bki = 4 + (g % 2)
                    p.req_free("tensor", f"bk{bki}")
                    p.op("tensor", (lambda t, g=g, bki=bki: t.transpose(
                        banks[bki][:, 0:128], ahf[:, g * 128:(g + 1) * 128], identf)), inc="t")
                    p.set_ready(f"bk{bki}", "t")
                    p.req_ready("scalar", f"bk{bki}")
                    p.op("scalar", (lambda sc, g=g, bki=bki: sc.copy(
                        aTb[:, g * 128:(g + 1) * 128], banks[bki][:, 0:128])), inc="s")
                    p.set_free(f"bk{bki}", "s")
                p.set_ready("aTb", "s")
                p.set_free("ahfw", "t")

                def bh_slices(bi):
                    h2, g = bi // 8, bi % 8
                    rs = slice(h2 * 64, (h2 + 1) * 64)
                    csl = slice(g * 128, (g + 1) * 128)
                    return g, h2, rs, csl

                if B_STAGE < 5:
                    continue
                p.req_ready("tensor", "abuf")
                p.req_ready("tensor", "bhn")
                for bh in range(16):
                    g, h2, rs, csl = bh_slices(bh)
                    q4, qi = bh // 4, bh % 4
                    gb = g * 128
                    if qi == 0:
                        p.req_free("tensor", f"bk{q4}")
                    p.op("tensor", (lambda t, rs=rs, gb=gb, q4=q4, qi=qi: t.matmul(
                        banks[q4][0:64, qi * 128:(qi + 1) * 128],
                        abuf[rs, gb:gb + 64], bhn[rs, gb:gb + 128],
                        start=True, stop=True)), inc="t")
                for bh in range(16):
                    g, h2, rs, csl = bh_slices(bh)
                    q4, qi = bh // 4, bh % 4
                    gb = g * 128
                    p.op("tensor", (lambda t, rs=rs, gb=gb, q4=q4, qi=qi: t.matmul(
                        banks[q4][64:128, qi * 128 + 64:(qi + 1) * 128],
                        abuf[rs, gb + 64:gb + 128], bhn[rs, gb + 64:gb + 128],
                        start=True, stop=True)), inc="t")
                    if qi == 3:
                        p.set_ready(f"bk{q4}", "t")
                for q4 in range(4):
                    p.req_ready("vector", f"bk{q4}")
                    p.req_free("vector", "XS0")
                    for qi in range(4):
                        p.op("vector", (lambda v, q4=q4, qi=qi: v.memset(
                            banks[q4][64:128, qi * 128:qi * 128 + 64], 0.0)), inc="v")
                    p.op("vector", (lambda v, q4=q4: v.memset(
                        XS[0][:, q4 * 512:(q4 + 1) * 512], 0.0)), inc="v")
                    p.wait("vector", "v", p.val("v"))
                    p.op("vector", (lambda v, q4=q4: v.copy_predicated(
                        XS[0][:, q4 * 512:(q4 + 1) * 512], masks4["mask_rt"], banks[q4])), inc="v")
                    p.set_free(f"bk{q4}", "v")
                p.set_ready("XS0", "v")
                p.set_ready("XS0@v", "v")
                for bh in range(16):
                    g, h2, rs, csl = bh_slices(bh)
                    q4, qi = bh // 4, bh % 4
                    gb = g * 128
                    if qi == 0:
                        p.req_free("tensor", f"bk{4 + q4}")
                    p.op("tensor", (lambda t, rs=rs, gb=gb, q4=q4, qi=qi: t.matmul(
                        banks[4 + q4][0:64, qi * 128:qi * 128 + 64],
                        bhn[rs, gb:gb + 64], abuf[rs, gb:gb + 64],
                        start=True, stop=True)), inc="t")
                for bh in range(16):
                    g, h2, rs, csl = bh_slices(bh)
                    q4, qi = bh // 4, bh % 4
                    gb = g * 128
                    p.op("tensor", (lambda t, rs=rs, gb=gb, q4=q4, qi=qi: t.matmul(
                        banks[4 + q4][64:128, qi * 128:(qi + 1) * 128],
                        bhn[rs, gb + 64:gb + 128], abuf[rs, gb:gb + 128],
                        start=True, stop=True)), inc="t")
                    if qi == 3:
                        p.set_ready(f"bk{4 + q4}", "t")
                for q4 in range(4):
                    p.req_ready("vector", f"bk{4 + q4}")
                    for qi in range(4):
                        p.op("vector", (lambda v, q4=q4, qi=qi: v.memset(
                            banks[4 + q4][0:64, qi * 128 + 64:(qi + 1) * 128], 0.0)), inc="v")
                    p.op("vector", (lambda v, q4=q4: v.memset(
                        XTS[0][:, q4 * 512:(q4 + 1) * 512], 0.0)), inc="v")
                    p.wait("vector", "v", p.val("v"))
                    p.op("vector", (lambda v, q4=q4: v.copy_predicated(
                        XTS[0][:, q4 * 512:(q4 + 1) * 512], masks4["mask_tr"], banks[4 + q4])), inc="v")
                    p.set_free(f"bk{4 + q4}", "v")
                p.set_ready("XTS0", "v")
                p.set_ready("XTS0@v", "v")
                if B_STAGE < 6:
                    continue
                p.req_ready("tensor", "S0pb")
                for bh in range(16):
                    g, h2, rs, csl = bh_slices(bh)
                    q4, qi = bh // 4, bh % 4
                    if qi == 0:
                        p.req_free("tensor", f"bk{q4}")
                    p.op("tensor", (lambda t, rs=rs, csl=csl, q4=q4, qi=qi: t.matmul(
                        banks[q4][:, qi * 128:(qi + 1) * 128], bhn[rs, csl], S0pb[rs, csl],
                        start=True, stop=True)), inc="t")
                    if qi == 3:
                        p.set_ready(f"bk{q4}", "t")
                p.req_ready("vector", "vtm%d" % pb)
                p.req_ready("vector", "bcolT")
                for bh in range(16):
                    g, h2, rs, csl = bh_slices(bh)
                    q4, qi = bh // 4, bh % 4
                    vi = g * 2 + h2
                    if qi == 0:
                        p.req_ready("vector", f"bk{q4}")
                    p.op("vector", (lambda v, bh=bh, vi=vi, q4=q4, qi=qi, pb=pb: v.scalar_tensor_tensor(
                        Ub[0][:, bh * 128:(bh + 1) * 128], vtm[pb][:, vi * 128:(vi + 1) * 128],
                        bcolT[:, vi:vi + 1], banks[q4][:, qi * 128:(qi + 1) * 128],
                        ALU.mult, ALU.add)), inc="v")
                    if qi == 3:
                        p.set_free(f"bk{q4}", "v")
                p.set_ready("U0", "v")
                p.set_ready("U0@v", "v")
                p.set_free(f"vtm{pb}", "v")
                if B_STAGE < 7:
                    continue
                ucur = 0
                xpar = 0
                for j in range(7):
                    gA = 0 if j % 2 == 0 else 4
                    gB = 4 - gA
                    p.req_ready("tensor", f"U{ucur}")
                    p.req_ready("tensor", f"U{ucur}@v")
                    p.req_ready("tensor", f"XS{xpar}")
                    p.req_ready("tensor", f"XS{xpar}@v")
                    for bh in range(16):
                        q4, qi = bh // 4, bh % 4
                        if qi == 0:
                            p.req_free("tensor", f"bk{gA + q4}")
                        p.op("tensor", (lambda t, bh=bh, q4=q4, qi=qi, gA=gA, xp=xpar, uc=ucur: t.matmul(
                            banks[gA + q4][:, qi * 128:(qi + 1) * 128], identb,
                            Ub[uc][:, bh * 128:(bh + 1) * 128],
                            start=True, stop=False)), inc="t")
                        p.op("tensor", (lambda t, bh=bh, q4=q4, qi=qi, gA=gA, xp=xpar, uc=ucur: t.matmul(
                            banks[gA + q4][:, qi * 128:(qi + 1) * 128],
                            XS[xp][:, bh * 128:(bh + 1) * 128],
                            Ub[uc][:, bh * 128:(bh + 1) * 128],
                            start=False, stop=True)), inc="t")
                        if qi == 3:
                            p.set_ready(f"bk{gA + q4}", "t")
                    for q4 in range(4):
                        eng = "scalar" if q4 % 2 == 0 else "vector"
                        es = "s" if q4 % 2 == 0 else "v"
                        p.req_ready(eng, f"bk{gA + q4}")
                        p.op(eng, (lambda e, q4=q4, gA=gA, uc=ucur: (
                            e.copy(Ub[1 - uc][:, q4 * 512:(q4 + 1) * 512], banks[gA + q4])
                            if hasattr(e, "copy") else
                            e.tensor_copy(Ub[1 - uc][:, q4 * 512:(q4 + 1) * 512], banks[gA + q4]))), inc=es)
                        p.set_free(f"bk{gA + q4}", es)
                    p.set_ready(f"U{1 - ucur}", "s")
                    p.set_ready(f"U{1 - ucur}@v", "v")
                    ucur = 1 - ucur
                    if j < 6:
                        p.req_ready("tensor", f"XTS{xpar}")
                        p.req_ready("tensor", f"XTS{xpar}@v")
                        for bh in range(16):
                            q4, qi = bh // 4, bh % 4
                            if qi == 0:
                                p.req_free("tensor", f"bk{gB + q4}")
                            p.op("tensor", (lambda t, bh=bh, q4=q4, qi=qi, gB=gB, xp=xpar: t.matmul(
                                banks[gB + q4][:, qi * 128:(qi + 1) * 128],
                                XTS[xp][:, bh * 128:(bh + 1) * 128],
                                XS[xp][:, bh * 128:(bh + 1) * 128],
                                start=True, stop=True)), inc="t")
                            if qi == 3:
                                p.set_ready(f"bk{gB + q4}", "t")
                        for q4 in range(4):
                            eng = "scalar" if q4 % 2 == 1 else "vector"
                            es = "s" if q4 % 2 == 1 else "v"
                            p.req_ready(eng, f"bk{gB + q4}")
                            p.req_free(eng, f"XS{1 - xpar}w{q4}" if False else f"XS{1 - xpar}")
                            p.op(eng, (lambda e, q4=q4, gB=gB, xp=xpar: (
                                e.copy(XS[1 - xp][:, q4 * 512:(q4 + 1) * 512], banks[gB + q4])
                                if hasattr(e, "copy") else
                                e.tensor_copy(XS[1 - xp][:, q4 * 512:(q4 + 1) * 512], banks[gB + q4]))), inc=es)
                            p.set_free(f"bk{gB + q4}", es)
                        p.set_ready(f"XS{1 - xpar}", "s")
                        p.set_ready(f"XS{1 - xpar}@v", "v")
                        for bh in range(16):
                            q4, qi = bh // 4, bh % 4
                            if qi == 0:
                                p.req_free("tensor", f"bk{gA + q4}")
                            p.op("tensor", (lambda t, bh=bh, q4=q4, qi=qi, gA=gA, xp=xpar: t.matmul(
                                banks[gA + q4][:, qi * 128:(qi + 1) * 128],
                                XS[xp][:, bh * 128:(bh + 1) * 128],
                                XTS[xp][:, bh * 128:(bh + 1) * 128],
                                start=True, stop=True)), inc="t")
                            if qi == 3:
                                p.set_ready(f"bk{gA + q4}", "t")
                        for q4 in range(4):
                            eng = "scalar" if q4 % 2 == 0 else "vector"
                            es = "s" if q4 % 2 == 0 else "v"
                            p.req_ready(eng, f"bk{gA + q4}")
                            p.req_free(eng, f"XTS{1 - xpar}")
                            p.op(eng, (lambda e, q4=q4, gA=gA, xp=xpar: (
                                e.copy(XTS[1 - xp][:, q4 * 512:(q4 + 1) * 512], banks[gA + q4])
                                if hasattr(e, "copy") else
                                e.tensor_copy(XTS[1 - xp][:, q4 * 512:(q4 + 1) * 512], banks[gA + q4]))), inc=es)
                            p.set_free(f"bk{gA + q4}", es)
                        p.set_ready(f"XTS{1 - xpar}", "s")
                        p.set_ready(f"XTS{1 - xpar}@v", "v")
                        p.set_free(f"XS{xpar}", "t")
                        p.set_free(f"XTS{xpar}", "t")
                        xpar = 1 - xpar
                if B_STAGE < 8:
                    continue
                p.req_ready("tensor", "qtb")
                for bh in range(16):
                    g, h2, rs, csl = bh_slices(bh)
                    q4, qi = bh // 4, bh % 4
                    gb = g * 128
                    if qi == 0:
                        p.req_free("tensor", f"bk{q4}")
                    p.op("tensor", (lambda t, rs=rs, gb=gb, q4=q4, qi=qi: t.matmul(
                        banks[q4][0:64, qi * 128:(qi + 1) * 128],
                        abuf[rs, gb:gb + 64], qtb[rs, gb:gb + 128],
                        start=True, stop=True)), inc="t")
                for bh in range(16):
                    g, h2, rs, csl = bh_slices(bh)
                    q4, qi = bh // 4, bh % 4
                    gb = g * 128
                    p.op("tensor", (lambda t, rs=rs, gb=gb, q4=q4, qi=qi: t.matmul(
                        banks[q4][64:128, qi * 128 + 64:(qi + 1) * 128],
                        abuf[rs, gb + 64:gb + 128], qtb[rs, gb + 64:gb + 128],
                        start=True, stop=True)), inc="t")
                    if qi == 3:
                        p.set_ready(f"bk{q4}", "t")
                for q4 in range(4):
                    p.req_ready("vector", f"bk{q4}")
                    p.req_free("vector", "g1t")
                    for qi in range(4):
                        p.op("vector", (lambda v, q4=q4, qi=qi: v.memset(
                            banks[q4][64:128, qi * 128:qi * 128 + 64], 0.0)), inc="v")
                    p.op("vector", (lambda v, q4=q4: v.memset(
                        g1t[:, q4 * 512:(q4 + 1) * 512], 0.0)), inc="v")
                    p.wait("vector", "v", p.val("v"))
                    p.op("vector", (lambda v, q4=q4: v.copy_predicated(
                        g1t[:, q4 * 512:(q4 + 1) * 512], masks4["mask_rt_incl"], banks[q4])), inc="v")
                    p.set_free(f"bk{q4}", "v")
                p.set_ready("g1t", "v")
                if B_STAGE < 9:
                    continue
                p.req_ready("vector", "qtb")
                p.req_free("vector", "qtz")
                p.op("vector", lambda v: v.tensor_copy(qtz[0:64, 0:1024], qtb[0:64, :]), inc="v")
                p.op("vector", lambda v: v.memset(qtz[64:128, 0:1024], 0.0), inc="v")
                p.op("vector", lambda v: v.memset(qtz[0:64, 1024:2048], 0.0), inc="v")
                p.op("vector", lambda v: v.tensor_copy(qtz[64:128, 1024:2048], qtb[64:128, :]), inc="v")
                p.set_ready("qtz", "v")
                p.req_ready("tensor", "qtz")
                p.req_ready("tensor", "g1t")
                p.req_ready("tensor", f"U{ucur}")
                p.req_ready("tensor", f"U{ucur}@v")
                for bh in range(16):
                    g, h2, rs, csl = bh_slices(bh)
                    q4, qi = bh // 4, bh % 4
                    if qi == 0:
                        p.req_free("tensor", f"bk{4 + q4}")
                    p.op("tensor", (lambda t, h2=h2, csl=csl, q4=q4, qi=qi: t.matmul(
                        banks[4 + q4][:, qi * 128:(qi + 1) * 128],
                        qtz[:, h2 * 1024 + csl.start:h2 * 1024 + csl.stop],
                        S0pb[:, csl], start=True, stop=False)), inc="t")
                    p.op("tensor", (lambda t, bh=bh, q4=q4, qi=qi, uc=ucur: t.matmul(
                        banks[4 + q4][:, qi * 128:(qi + 1) * 128],
                        g1t[:, bh * 128:(bh + 1) * 128],
                        Ub[uc][:, bh * 128:(bh + 1) * 128],
                        start=False, stop=True)), inc="t")
                    if qi == 3:
                        p.set_ready(f"bk{4 + q4}", "t")
                p.set_free("qtz", "t")
                for q4 in range(4):
                    p.req_ready("scalar", f"bk{4 + q4}")
                    p.req_free("scalar", "yf")
                    p.op("scalar", (lambda sc, q4=q4: sc.copy(
                        yf[:, q4 * 512:(q4 + 1) * 512], banks[4 + q4])), inc="s")
                    p.set_free(f"bk{4 + q4}", "s")
                p.set_ready("yf", "s")
                p.req_ready("scalar", "yf")
                for q4 in range(4):
                    p.req_free("scalar", "sqy")
                    p.op("scalar", (lambda sc, q4=q4: sc.square(
                        sqy, yf[:, q4 * 512:(q4 + 1) * 512])), inc="s")
                    p.set_ready("sqy", "s")
                    p.req_ready("vector", "sqy")
                    for qi in range(4):
                        p.op("vector", (lambda v, q4=q4, qi=qi: v.reduce_sum(
                            ysum[:, q4 * 4 + qi:q4 * 4 + qi + 1],
                            sqy[:, qi * 128:(qi + 1) * 128], axis=AX.X)), inc="v")
                    p.set_free("sqy", "v")
                p.set_ready("ysum", "v")
                p.req_ready("scalar", "ysum")
                p.op("scalar", lambda sc: sc.activation(yrst, ysum, AF.Ln,
                                                        bias=cst[:, CC_EPS6:CC_EPS6 + 1],
                                                        scale=cst[:, CC_I128:CC_I128 + 1]), inc="s")
                p.wait("scalar", "s", p.val("s"))
                p.op("scalar", lambda sc: sc.activation(yrst, yrst, AF.Exp,
                                                        scale=cst[:, CC_NHALF:CC_NHALF + 1]), inc="s")
                p.set_ready("yrst", "s")
                p.req_ready("vector", "yrst")
                p.req_free("vector", "ynb")
                for bh in range(16):
                    g, h2, rs, csl = bh_slices(bh)
                    vi = g * 2 + h2
                    p.op("vector", (lambda v, bh=bh, vi=vi: v.scalar_tensor_tensor(
                        ynb[:, vi * 128:(vi + 1) * 128], yf[:, bh * 128:(bh + 1) * 128],
                        yrst[:, bh:bh + 1], whnbhB[:, vi * 128:(vi + 1) * 128],
                        ALU.mult, ALU.mult)), inc="v")
                p.set_free("yf", "v")
                p.set_ready("ynb", "v")
                p.req_ready("gpsimd", "ynb")
                c0 = ci * C
                for hp in range(4):
                    for b_ in range(2):
                        g_ = hp * 2 + b_
                        p.dma("gpsimd",
                              y_sp[b_, 2 * hp:2 * hp + 2, c0:c0 + C, :].rearrange("h2 t dv -> t h2 dv"),
                              ynb[:, g_ * 256:(g_ + 1) * 256].rearrange("t (h2 dv) -> t h2 dv", h2=2),
                              "dynb")
                p.set_free("ynb", "dynb")
                if B_STAGE < 10:
                    continue
                p.req_ready("tensor", "aTb")
                for bh in range(16):
                    g, h2, rs, csl = bh_slices(bh)
                    bkix = g // 4
                    if bh == 0 or bh == 4:
                        p.req_free("tensor", f"bk{bkix}")
                    p.op("tensor", (lambda t, g=g, h2=h2, bh=bh, bkix=bkix, uc=ucur: t.matmul(
                        banks[bkix][h2 * 64:(h2 + 1) * 64, (g % 4) * 128:((g % 4) + 1) * 128],
                        aTb[:, g * 128 + h2 * 64: g * 128 + h2 * 64 + 64],
                        Ub[uc][:, bh * 128:(bh + 1) * 128],
                        start=True, stop=True)), inc="t")
                    if bh == 11 or bh == 15:
                        p.set_ready(f"bk{bkix}", "t")
                p.req_ready("vector", "rhog")
                for g in range(8):
                    bkix = g // 4
                    if g % 4 == 0:
                        p.req_ready("vector", f"bk{bkix}")
                    p.op("vector", (lambda v, g=g, bkix=bkix, sc_=sc_, sn=sn: v.scalar_tensor_tensor(
                        sn[:, g * 128:(g + 1) * 128], sc_[:, g * 128:(g + 1) * 128],
                        rhog[:, g:g + 1], banks[bkix][:, (g % 4) * 128:((g % 4) + 1) * 128],
                        ALU.mult, ALU.add)), inc="v")
                    if g % 4 == 3:
                        p.set_free(f"bk{bkix}", "v")
                p.set_ready("Sb", "v")
                if ci == 0 and "bdbg" in TEST_TAPS:
                    dumps = [(spk, "v"), (alf2, "v"), (Graw, "v"), (Gi, "s"),
                             (brow, "v"), (bcolT, "s"), (abuf, "v"), (qtb, "v"),
                             (bhn, "v"), (Ub[ucur], "s"), (XS[0], "v"), (XTS[0], "v"),
                             (g1t, "v"), (S0pb, "v"), (aTb, "s"), (yf, "s"),
                             (ynb, "v"), (yrst, "s"), (rhot, "v"), (rhog, "v"),
                             (Sb[1 - pb], "v"), (vtm[pb], f"dvtm{pb}"), (abcast, "s"), (bbcn, "s")]
                    for di, (tile, dep) in enumerate(dumps):
                        if dep in ("v", "s", "t"):
                            p.wait("gpsimd", dep, p.val(dep))
                        else:
                            p.wait("gpsimd", dep)
                        sh = tile.shape
                        p.dma("gpsimd", bdbg[di, 0:sh[0], 0:sh[1]], tile, "st")
                p.set_free("spk", "t")
                p.set_free("spk", "v")
                for nm in ("abuf", "qtb", "bhn", "ahfw", "S0pb", "g1t"):
                    p.set_free(nm, "t")
                p.set_free("aTb", "t")

        bctx.close()
        barrier()

        CT3 = 256
        cctx = ExitStack()
        if "C" in PHASES:
            sbC = lambda n, sh, dt=F32: cctx.enter_context(nc.sbuf_tensor(n, sh, dt, side="right"))[:]
            wu1B = sbC("wu1B", [128, 8 * R], BF16)
            wu2B = sbC("wu2B", [R, D], BF16)
            woutB = sbC("woutB", [128, 8 * 1024], BF16)
            wgB = sbC("wgB", [128, 8 * 8], BF16)
            for ks in range(8):
                p.dma("gpsimd", wu1B[:, ks * R:(ks + 1) * R], w["w_u1"][ks * 128:(ks + 1) * 128, :], "qW")
                p.dma("gpsimd", woutB[:, ks * 1024:(ks + 1) * 1024], w["w_out"][ks * 128:(ks + 1) * 128, :], "qW")
                p.dma("gpsimd", wgB[:, ks * 8:(ks + 1) * 8], w["moe_Wg"][ks * 128:(ks + 1) * 128, :], "qW")
            p.dma("gpsimd", wu2B, w["w_u2"], "qW")
            for eng in ("tensor", "gpsimd"):
                p.wait(eng, "qW")
            p.wait("gpsimd", "c0")
            p.wait("gpsimd", "c1")
            p.wait("sync", "c0")
            p.wait("sync", "c1")

            xc = [sbC(f"xc{i}", [128, 8 * CT3]) for i in range(2)]
            xcb = [sbC(f"xcb{i}", [128, 8 * CT3], BF16) for i in range(2)]
            yc = [sbC(f"yc{i}", [128, 8 * CT3], BF16) for i in range(2)]
            ffp = [sbC(f"ffp{i}", [128, 8192], BF16) for i in range(2)]
            ff2m = [sbC(f"ff2m{i}", [128, 4096], BF16) for i in range(2)]
            f13full = sbC("f13full", [128, 8192], BF16)
            g1c = sbC("g1c", [R, CT3], BF16)
            gatec = sbC("gatec", [128, 8 * CT3])
            y2c = sbC("y2c", [128, 8 * CT3])
            y2b = sbC("y2b", [128, 8 * CT3], BF16)
            h2c = sbC("h2c", [128, 8 * CT3], BF16)
            f1c = sbC("f1c", [128, 4 * CT3], BF16)
            f13c = sbC("f13c", [128, 4 * CT3], BF16)
            sq1 = sbC("sq1", [128, CT3])
            rowc = sbC("rowc", [1, CT3])
            rowc2 = sbC("rowc2", [1, CT3])
            rowmu = sbC("rowmu", [1, CT3])
            zpre = sbC("zpre", [128, 8 * CT3])
            z1c = sbC("z1c", [128, 8 * CT3])
            z1cb = sbC("z1cb", [128, 8 * CT3], BF16)
            z1s = sbC("z1s", [128, CT3], BF16)
            lgc = sbC("lgc", [8, CT3])
            lgt = sbC("lgt", [128, 8])
            lgt2 = sbC("lgt2", [128, 8])
            prb = sbC("prb", [128, 8])
            rowm = sbC("rowm", [128, 1])
            rowsum = sbC("rowsum", [128, 1])
            rrec = sbC("rrec", [128, 1])
            coefc = sbC("coefc", [128, 16])
            coefT = sbC("coefT", [8, CT3], BF16)

            for b in range(B):
                for ti in range(T // CT3):
                    tix = b * (T // CT3) + ti
                    if ti >= C_LIMIT:
                        continue
                    pb = tix % 2
                    t0 = ti * CT3
                    p.req_free("sync", f"xc{pb}")
                    for s_ in range(8):
                        p.dma("sync", xc[pb][:, s_ * CT3:(s_ + 1) * CT3],
                              x_fm[b, s_, :, t0:t0 + CT3], f"dx{pb}")
                    p.set_ready(f"xc{pb}", f"dx{pb}")
                    p.req_free("gpsimd", f"xcb{pb}")
                    for s_ in range(8):
                        p.dma("gpsimd", xcb[pb][:, s_ * CT3:(s_ + 1) * CT3],
                              x_fm[b, s_, :, t0:t0 + CT3], f"dxcb{pb}")
                    p.set_ready(f"xcb{pb}", f"dxcb{pb}")
                    p.req_free("sync", f"yc{pb}")
                    for h_ in range(8):
                        p.dma("sync", yc[pb][:, h_ * CT3:(h_ + 1) * CT3],
                              y_sp[b, h_, t0:t0 + CT3, :], f"dyc{pb}", transpose=True)
                    p.set_ready(f"yc{pb}", f"dyc{pb}")
                    if C_STAGE < 1:
                        continue
                    p.req_ready("tensor", f"xcb{pb}")
                    p.req_free("tensor", "bk6")
                    for ks in range(8):
                        p.op("tensor", (lambda t, ks=ks, pb=pb: t.matmul(
                            banks[6][0:R, 0:CT3], wu1B[:, ks * R:(ks + 1) * R],
                            xcb[pb][:, ks * CT3:(ks + 1) * CT3],
                            start=(ks == 0), stop=(ks == 7))), inc="t")
                    p.set_free(f"xcb{pb}", "t")
                    p.set_ready("bk6", "t")
                    p.req_ready("scalar", "bk6")
                    p.req_free("scalar", "g1c")
                    p.op("scalar", lambda sc: sc.activation(g1c, banks[6][0:R, 0:CT3],
                                                            AF.Silu, bias=bu1B), inc="s")
                    p.set_free("bk6", "s")
                    p.set_ready("g1c", "s")
                    if C_STAGE < 1.5:
                        continue
                    p.req_ready("tensor", "g1c")
                    for m in range(8):
                        bki = 6 + (m % 2)
                        p.req_free("tensor", f"bk{bki}")
                        p.op("tensor", (lambda t, m=m, bki=bki: t.matmul(
                            banks[bki][:, 0:CT3],
                            wu2B[:, m * 128:(m + 1) * 128], g1c,
                            start=True, stop=True)), inc="t")
                        mv = p.val("t")
                        p.wait("scalar", "t", mv)
                        p.op("scalar", (lambda sc, m=m, bki=bki: sc.activation(
                            gatec[:, m * CT3:(m + 1) * CT3], banks[bki][:, 0:CT3],
                            AF.Sigmoid, bias=bu2B[:, m:m + 1])), inc="s")
                        p.set_free(f"bk{bki}", "s")
                    p.set_free("g1c", "t")
                    p.set_ready("gatec", "s")
                    if C_STAGE < 2:
                        continue
                    p.req_ready("tensor", f"yc{pb}")
                    p.req_ready("vector", "gatec")
                    p.req_ready("vector", f"xc{pb}")
                    for m in range(8):
                        p.req_free("tensor", f"bk{6 + (m % 2)}")
                        for ks in range(8):
                            p.op("tensor", (lambda t, m=m, ks=ks, pb=pb: t.matmul(
                                banks[6 + (m % 2)][:, 0:CT3],
                                woutB[:, ks * 1024 + m * 128: ks * 1024 + (m + 1) * 128],
                                yc[pb][:, ks * CT3:(ks + 1) * CT3],
                                start=(ks == 0), stop=(ks == 7))), inc="t")
                        p.set_ready(f"bk{6 + (m % 2)}", "t")
                        p.req_ready("vector", f"bk{6 + (m % 2)}")
                        p.op("vector", (lambda v, m=m: v.tensor_mul(
                            y2c[:, m * CT3:(m + 1) * CT3], banks[6 + (m % 2)][:, 0:CT3],
                            gatec[:, m * CT3:(m + 1) * CT3])), inc="v")
                        p.wait("vector", "v", p.val("v"))
                        p.op("vector", (lambda v, m=m, pb=pb: v.tensor_add(
                            y2c[:, m * CT3:(m + 1) * CT3], y2c[:, m * CT3:(m + 1) * CT3],
                            xc[pb][:, m * CT3:(m + 1) * CT3])), inc="v")
                        p.wait("vector", "v", p.val("v"))
                        p.op("vector", (lambda v, m=m: v.tensor_copy(
                            y2b[:, m * CT3:(m + 1) * CT3], y2c[:, m * CT3:(m + 1) * CT3])), inc="v")
                        p.set_free(f"bk{6 + (m % 2)}", "v")
                    p.set_ready("y2c", "v")
                    p.set_ready("y2b", "v")
                    p.set_free("gatec", "v")
                    if C_STAGE < 3:
                        continue
                    p.req_ready("vector", "y2c")
                    p.req_free("tensor", "bk6")
                    for s_ in range(8):
                        p.req_free("vector", "sq1")
                        p.op("vector", (lambda v, s=s_: v.tensor_mul(
                            sq1, y2c[:, s * CT3:(s + 1) * CT3], y2c[:, s * CT3:(s + 1) * CT3])), inc="v")
                        p.set_ready("sq1", "v")
                        p.req_ready("tensor", "sq1")
                        p.op("tensor", (lambda t, s=s_: t.matmul(
                            banks[6][0:1, 0:CT3], onescolf, sq1,
                            start=(s == 0), stop=(s == 7))), inc="t")
                        p.set_free("sq1", "t")
                    p.set_ready("bk6", "t")
                    p.req_ready("scalar", "bk6")
                    p.req_free("scalar", "rowc")
                    p.op("scalar", lambda sc: sc.activation(
                        rowc, banks[6][0:1, 0:CT3], AF.Ln, bias=cst[0:1, CC_EPS6:CC_EPS6 + 1],
                        scale=cst[0:1, CC_I1024:CC_I1024 + 1]), inc="s")
                    p.set_free("bk6", "s")
                    p.wait("scalar", "s", p.val("s"))
                    p.op("scalar", lambda sc: sc.activation(
                        rowc, rowc, AF.Exp, scale=cst[0:1, CC_NHALF:CC_NHALF + 1]), inc="s")
                    p.set_ready("rowc", "s")
                    p.req_ready("tensor", "rowc")
                    p.req_free("tensor", "bk7")
                    p.op("tensor", lambda t: t.matmul(banks[7][:, 0:CT3], ones1f, rowc,
                                                      start=True, stop=True), inc="t")
                    p.set_free("rowc", "t")
                    p.set_ready("bk7", "t")
                    p.req_ready("vector", "bk7")
                    p.req_free("vector", "h2c")
                    for s_ in range(8):
                        p.op("vector", (lambda v, s=s_: v.scalar_tensor_tensor(
                            h2c[:, s * CT3:(s + 1) * CT3], banks[7][:, 0:CT3], wffnB[:, s:s + 1],
                            y2c[:, s * CT3:(s + 1) * CT3], ALU.mult, ALU.mult)), inc="v")
                    p.set_free("bk7", "v")
                    p.set_ready("h2c", "v")
                    if C_STAGE < 4:
                        continue
                    FB1, FB3 = 0, 8 * 512
                    for fp in range(8):
                        fpb = fp % 2
                        p.req_free("gpsimd", f"ffp{fpb}")
                        for ks in range(8):
                            p.dma("gpsimd", ffp[fpb][:, FB1 + ks * 512:FB1 + (ks + 1) * 512],
                                  ff1b[ks * 128:(ks + 1) * 128, fp * 512:(fp + 1) * 512], f"dsdc{fpb}")
                            p.dma("gpsimd", ffp[fpb][:, FB3 + ks * 512:FB3 + (ks + 1) * 512],
                                  ff3b[ks * 128:(ks + 1) * 128, fp * 512:(fp + 1) * 512], f"dsdc{fpb}")
                        p.set_ready(f"ffp{fpb}", f"dsdc{fpb}")
                        p.req_ready("tensor", f"ffp{fpb}")
                        p.req_ready("tensor", "h2c")
                        p.req_free("tensor", "bk4")
                        p.req_free("tensor", "bk5")
                        for mo in range(4):
                            for ks in range(8):
                                p.op("tensor", (lambda t, mo=mo, ks=ks, fpb=fpb: t.matmul(
                                    banks[4 + mo // 2][:, (mo % 2) * CT3:(mo % 2 + 1) * CT3],
                                    ffp[fpb][:, FB1 + ks * 512 + mo * 128:FB1 + ks * 512 + (mo + 1) * 128],
                                    h2c[:, ks * CT3:(ks + 1) * CT3],
                                    start=(ks == 0), stop=(ks == 7))), inc="t")
                        p.set_ready("bk4", "t")
                        p.set_ready("bk5", "t")
                        p.req_ready("scalar", "bk4")
                        p.req_ready("scalar", "bk5")
                        p.req_free("scalar", "f1c")
                        for mo in range(4):
                            p.op("scalar", (lambda sc, mo=mo: sc.activation(
                                f1c[:, mo * CT3:(mo + 1) * CT3],
                                banks[4 + mo // 2][:, (mo % 2) * CT3:(mo % 2 + 1) * CT3],
                                AF.Silu)), inc="s")
                        p.set_free("bk4", "s")
                        p.set_free("bk5", "s")
                        p.set_ready("f1c", "s")
                        p.req_ready("tensor", "f1c")
                        p.req_free("tensor", "bk6")
                        p.req_free("tensor", "bk7")
                        for mo in range(4):
                            for ks in range(8):
                                p.op("tensor", (lambda t, mo=mo, ks=ks, fpb=fpb: t.matmul(
                                    banks[6 + mo // 2][:, (mo % 2) * CT3:(mo % 2 + 1) * CT3],
                                    ffp[fpb][:, FB3 + ks * 512 + mo * 128:FB3 + ks * 512 + (mo + 1) * 128],
                                    h2c[:, ks * CT3:(ks + 1) * CT3],
                                    start=(ks == 0), stop=(ks == 7))), inc="t")
                        p.set_ready("bk6", "t")
                        p.set_ready("bk7", "t")
                        p.req_ready("vector", "bk6")
                        p.req_ready("vector", "bk7")
                        p.req_ready("vector", "f1c")
                        if fp == 0:
                            p.req_free("vector", "f13full")
                        for mo in range(4):
                            p.op("vector", (lambda v, mo=mo, fp=fp: v.tensor_mul(
                                f13full[:, fp * 1024 + mo * CT3:fp * 1024 + (mo + 1) * CT3],
                                banks[6 + mo // 2][:, (mo % 2) * CT3:(mo % 2 + 1) * CT3],
                                f1c[:, mo * CT3:(mo + 1) * CT3])), inc="v")
                        p.set_free("bk6", "v")
                        p.set_free("bk7", "v")
                        p.set_free("f1c", "v")
                        p.set_free(f"ffp{fpb}", "t")
                    p.set_ready("f13full", "v")
                    p.req_ready("tensor", "f13full")
                    for m in range(8):
                        mpb = m % 2
                        p.req_free("gpsimd", f"ff2m{mpb}")
                        p.dma("gpsimd", ff2m[mpb].rearrange("p (ks c) -> p ks c", ks=32, c=128),
                              ff2b[:, m * 128:(m + 1) * 128].rearrange("(ks p) c -> p ks c", p=128),
                              f"dbbr{mpb}")
                        p.set_ready(f"ff2m{mpb}", f"dbbr{mpb}")
                        p.req_ready("tensor", f"ff2m{mpb}")
                        bki = m % 4
                        p.req_free("tensor", f"bk{bki}")
                        for ks2 in range(32):
                            p.op("tensor", (lambda t, m=m, ks2=ks2, bki=bki, mpb=mpb: t.matmul(
                                banks[bki][:, 0:CT3],
                                ff2m[mpb][:, ks2 * 128:(ks2 + 1) * 128],
                                f13full[:, ks2 * CT3:(ks2 + 1) * CT3],
                                start=(ks2 == 0), stop=(ks2 == 31))), inc="t")
                        p.set_ready(f"bkz{m}" if False else f"bk{bki}", "t")
                        p.set_free(f"ff2m{mpb}", "t")
                        p.req_ready("vector", f"bk{bki}")
                        if m == 0:
                            p.req_free("vector", "zpre")
                        p.op("vector", (lambda v, m=m, bki=bki: v.tensor_add(
                            zpre[:, m * CT3:(m + 1) * CT3], banks[bki][:, 0:CT3],
                            y2c[:, m * CT3:(m + 1) * CT3])), inc="v")
                        p.wait("vector", "v", p.val("v"))
                        p.op("vector", (lambda v, m=m, pb=pb: v.tensor_add(
                            zpre[:, m * CT3:(m + 1) * CT3], zpre[:, m * CT3:(m + 1) * CT3],
                            xc[pb][:, m * CT3:(m + 1) * CT3])), inc="v")
                        p.wait("vector", "v", p.val("v"))
                        p.set_free(f"bk{bki}", "v")
                    p.set_ready("zpre", "v")
                    p.set_free("f13full", "t")
                    p.set_free(f"xc{pb}", "v")
                    p.set_free("y2c", "v")
                    if C_STAGE < 5:
                        continue
                    p.req_ready("tensor", "zpre")
                    p.req_free("tensor", "bk6")
                    for s_ in range(8):
                        p.op("tensor", (lambda t, s=s_: t.matmul(
                            banks[6][0:1, 0:CT3], onescolf, zpre[:, s * CT3:(s + 1) * CT3],
                            start=(s == 0), stop=(s == 7))), inc="t")
                    p.set_ready("bk6mu", "t")
                    p.req_free("tensor", "bk7")
                    for s_ in range(8):
                        p.req_free("vector", "sq1")
                        p.op("vector", (lambda v, s=s_: v.tensor_mul(
                            sq1, zpre[:, s * CT3:(s + 1) * CT3], zpre[:, s * CT3:(s + 1) * CT3])), inc="v")
                        p.set_ready("sq1", "v")
                        p.req_ready("tensor", "sq1")
                        p.op("tensor", (lambda t, s=s_: t.matmul(
                            banks[7][0:1, 0:CT3], onescolf, sq1,
                            start=(s == 0), stop=(s == 7))), inc="t")
                        p.set_free("sq1", "t")
                    p.set_ready("bk7s", "t")
                    p.req_ready("vector", "bk6mu")
                    p.req_ready("vector", "bk7s")
                    p.req_free("vector", "rowmu")
                    p.op("vector", lambda v: v.tensor_scalar(rowmu, banks[6][0:1, 0:CT3],
                                                             1.0 / 1024.0, None, ALU.mult), inc="v")
                    p.wait("vector", "v", p.val("v"))
                    p.req_free("vector", "rowc2")
                    p.op("vector", lambda v: v.tensor_mul(rowc2, rowmu, rowmu), inc="v")
                    p.wait("vector", "v", p.val("v"))
                    p.op("vector", lambda v: v.scalar_tensor_tensor(
                        rowc2, banks[7][0:1, 0:CT3], 1.0 / 1024.0, rowc2, ALU.mult,
                        ALU.subtract), inc="v")
                    p.wait("vector", "v", p.val("v"))
                    p.set_free("bk6", "v")
                    p.set_free("bk7", "v")
                    p.set_ready("var", "v")
                    p.req_ready("scalar", "var")
                    p.op("scalar", lambda sc: sc.activation(rowc2, rowc2, AF.Ln,
                                                            bias=cst[0:1, CC_EPS5:CC_EPS5 + 1]), inc="s")
                    p.wait("scalar", "s", p.val("s"))
                    p.op("scalar", lambda sc: sc.activation(rowc2, rowc2, AF.Exp,
                                                            scale=cst[0:1, CC_NHALF:CC_NHALF + 1]), inc="s")
                    p.set_ready("rstd1", "s")
                    p.req_ready("vector", "rstd1")
                    p.op("vector", lambda v: v.tensor_mul(rowmu, rowmu, rowc2), inc="v")
                    p.set_ready("murstd", "v")
                    p.req_ready("tensor", "murstd")
                    p.req_free("tensor", "bk6")
                    p.op("tensor", lambda t: t.matmul(banks[6][:, 0:CT3], ones1f, rowc2,
                                                      start=True, stop=True), inc="t")
                    p.op("tensor", lambda t: t.matmul(banks[6][:, CT3:2 * CT3], ones1f, rowmu,
                                                      start=True, stop=True), inc="t")
                    p.set_ready("bk6", "t")
                    p.req_ready("vector", "bk6")
                    p.req_free("vector", "z1c")
                    for s_ in range(8):
                        p.op("vector", (lambda v, s=s_: v.tensor_mul(
                            z1c[:, s * CT3:(s + 1) * CT3], zpre[:, s * CT3:(s + 1) * CT3],
                            banks[6][:, 0:CT3])), inc="v")
                        p.wait("vector", "v", p.val("v"))
                        p.op("vector", (lambda v, s=s_: v.tensor_sub(
                            z1c[:, s * CT3:(s + 1) * CT3], z1c[:, s * CT3:(s + 1) * CT3],
                            banks[6][:, CT3:2 * CT3])), inc="v")
                        p.wait("vector", "v", p.val("v"))
                    p.set_free("bk6", "v")
                    p.set_free("zpre", "v")
                    p.set_ready("z1c", "v")
                    p.req_ready("gpsimd", "z1c")
                    for s_ in range(8):
                        p.dma("gpsimd", z1_sp[b, s_, :, t0:t0 + CT3],
                              z1c[:, s_ * CT3:(s_ + 1) * CT3], "dxf")
                    p.set_free("z1c", "dxf")
                    if C_STAGE < 6:
                        continue
                    p.req_free("vector", "z1cb")
                    for s_ in range(8):
                        p.op("vector", (lambda v, s=s_: v.tensor_copy(
                            z1cb[:, s * CT3:(s + 1) * CT3], z1c[:, s * CT3:(s + 1) * CT3])), inc="v")
                    p.set_ready("z1cb", "v")
                    p.req_ready("tensor", "z1cb")
                    p.req_free("tensor", "bk7")
                    for ks in range(8):
                        p.op("tensor", (lambda t, ks=ks: t.matmul(
                            banks[7][0:8, 0:CT3], wgB[:, ks * 8:(ks + 1) * 8],
                            z1cb[:, ks * CT3:(ks + 1) * CT3],
                            start=(ks == 0), stop=(ks == 7))), inc="t")
                    p.set_free("z1cb", "t")
                    p.set_ready("bk7", "t")
                    p.req_ready("scalar", "bk7")
                    p.req_free("scalar", "lgc")
                    p.op("scalar", lambda sc: sc.activation(lgc, banks[7][0:8, 0:CT3],
                                                            AF.Identity, bias=bgB), inc="s")
                    p.set_free("bk7", "s")
                    p.set_ready("lgc", "s")
                    for sub in range(2):
                        p.req_ready("tensor", "lgc")
                        p.req_free("tensor", "bk7")
                        p.op("tensor", (lambda t, sub=sub: t.transpose(
                            banks[7][:, 0:8], lgc[0:8, sub * 128:(sub + 1) * 128],
                            identf[0:8, 0:8])), inc="t")
                        p.set_ready("bk7", "t")
                        p.req_ready("vector", "bk7")
                        p.req_free("vector", "lgt")
                        p.op("vector", (lambda v: v.tensor_copy(lgt, banks[7][:, 0:8])), inc="v")
                        p.set_free("bk7", "v")
                        p.wait("vector", "v", p.val("v"))
                        p.op("vector", lambda v: v.reduce_max(rowm, lgt, axis=AX.X), inc="v")
                        p.wait("vector", "v", p.val("v"))
                        p.op("vector", lambda v: v.tensor_scalar(rowm, rowm, -1.0, None, ALU.mult), inc="v")
                        p.set_ready("negm", "v")
                        p.req_ready("scalar", "negm")
                        p.op("scalar", lambda sc: sc.activation(lgt2, lgt, AF.Exp,
                                                                bias=rowm[:, 0:1]), inc="s")
                        p.set_ready("expd", "s")
                        p.req_ready("vector", "expd")
                        p.op("vector", lambda v: v.reduce_sum(rowsum, lgt2, axis=AX.X), inc="v")
                        p.wait("vector", "v", p.val("v"))
                        p.op("vector", lambda v: v.reciprocal(rrec, rowsum), inc="v")
                        p.wait("vector", "v", p.val("v"))
                        p.op("vector", lambda v: v.tensor_scalar(prb, lgt2, rrec[:, 0:1], None, ALU.mult), inc="v")
                        p.wait("vector", "v", p.val("v"))
                        p.op("vector", lambda v: v.reduce_max(rowm, prb, axis=AX.X), inc="v")
                        p.wait("vector", "v", p.val("v"))
                        p.op("vector", lambda v: v.tensor_scalar(lgt2, prb, rowm[:, 0:1], None, ALU.is_ge), inc="v")
                        p.wait("vector", "v", p.val("v"))
                        if sub == 0:
                            p.req_free("vector", "coefc")
                        p.op("vector", (lambda v, sub=sub: v.tensor_mul(
                            coefc[:, sub * 8:(sub + 1) * 8], lgt2, prb)), inc="v")
                        p.wait("vector", "v", p.val("v"))
                        p.set_free("lgt", "v")
                    p.set_free("lgc", "t")
                    p.set_ready("coefc", "v")
                    p.set_free("coefc", "t")
                    for sub in range(2):
                        p.req_ready("tensor", "coefc")
                        p.req_free("tensor", "bk7")
                        p.op("tensor", (lambda t, sub=sub: t.transpose(
                            banks[7][0:8, 0:128], coefc[:, sub * 8:(sub + 1) * 8],
                            identf)), inc="t")
                        p.set_ready("bk7", "t")
                        p.req_ready("scalar", "bk7")
                        p.req_free("scalar", "coefT") if sub == 0 else None
                        p.op("scalar", (lambda sc, sub=sub: sc.copy(
                            coefT[:, sub * 128:(sub + 1) * 128], banks[7][0:8, 0:128])), inc="s")
                        p.set_free("bk7", "s")
                    p.set_ready("coefT", "s")
                    p.req_ready("gpsimd", "coefT")
                    p.dma("gpsimd", coef_sp[b, :, t0:t0 + CT3], coefT, "dbbst")
                    p.set_free("coefT", "dbbst")
                    p.set_free("y2b", "t")

        cctx.close()
        barrier()

        dctx = ExitStack()
        if "D" in PHASES:
            sbD = lambda n, sh, dt=F32: dctx.enter_context(nc.sbuf_tensor(n, sh, dt, side="right"))[:]
            Wb = [sbD(f"Wb{i}", [128, 8192], BF16) for i in range(2)]
            z1b = [sbD(f"z1b{i}", [128, 8 * 1024], BF16) for i in range(2)]
            zacc = sbD("zacc", [128, 8 * 1024])
            hsD = sbD("hsD", [128, 8 * 512], BF16)
            htmp = sbD("htmp", [128, 512], BF16)
            coefD1 = [sbD(f"coefD1{i}", [1, 8 * 1024], BF16) for i in range(2)]
            coefD8 = [sbD(f"coefD8{i}", [8, 1024], BF16) for i in range(2)]
            z1fD = sbD("z1fD", [128, 8 * 512])
            z2D = z1fD
            outst = sbD("outst", [128, 1024])
            sqD = sbD("sqD", [128, 512])
            rowD = sbD("rowD", [1, 512])
            rowD2 = sbD("rowD2", [1, 512])
            rowDmu = sbD("rowDmu", [1, 512])

            def wload(unit):
                ui = unit % 2
                rem = unit % 32
                e_, half, kind = rem // 4, (rem // 2) % 2, rem % 2
                p.req_free("gpsimd", f"Wb{ui}")
                if kind == 0:
                    for ks in range(8):
                        p.dma("gpsimd", Wb[ui][:, ks * 1024:(ks + 1) * 1024],
                              w1b[e_][ks * 128:(ks + 1) * 128, half * 1024:(half + 1) * 1024],
                              f"devb{ui}")
                else:
                    for ks in range(8):
                        p.dma("gpsimd", Wb[ui][:, ks * 1024:(ks + 1) * 1024],
                              w2b[e_][half * 1024 + ks * 128:half * 1024 + (ks + 1) * 128, :],
                              f"devb{ui}")
                p.set_ready(f"Wb{ui}", f"devb{ui}")

            NDT = T // 1024
            nunit = [0]
            wload(0)
            wload(1)
            for b in range(B):
                for td in range(NDT):
                    tl = b * NDT + td
                    if td >= D_LIMIT:
                        continue
                    pb = tl % 2
                    t0 = td * 1024
                    p.req_free("gpsimd", f"z1b{pb}")
                    for s_ in range(8):
                        p.dma("gpsimd", z1b[pb][:, s_ * 1024:(s_ + 1) * 1024],
                              z1_sp[b, s_, :, t0:t0 + 1024], f"dqc{pb}")
                    p.set_ready(f"z1b{pb}", f"dqc{pb}")
                    p.req_free("sync", f"coefD{pb}")
                    p.dma("sync", coefD1[pb][0:1, :].rearrange("o (e t) -> o e t", e=8, t=1024),
                          coef_sp[b:b + 1, :, t0:t0 + 1024], f"dcf{pb}")
                    p.dma("sync", coefD8[pb], coef_sp[b, :, t0:t0 + 1024], f"dcf{pb}")
                    p.set_ready(f"coefD{pb}", f"dcf{pb}")
                    p.req_free("vector", "zacc")
                    for s_ in range(8):
                        p.op("vector", (lambda v, s=s_: v.memset(zacc[:, s * 1024:(s + 1) * 1024], 0.0)), inc="v")
                    p.wait("vector", "v", p.val("v"))
                    p.set_ready("zacc0", "v")
                    for e_ in range(E):
                        for half in range(2):
                            u1 = tl * 32 + e_ * 4 + half * 2
                            b1i = u1 % 2
                            b2i = (u1 + 1) % 2
                            for sub in range(2):
                                s0 = sub * 512
                                p.req_ready("tensor", f"coefD{pb}")
                                p.req_free("tensor", "bk6")
                                p.op("tensor", (lambda t, e_=e_, s0=s0, pb=pb: t.matmul(
                                    banks[6][:, 0:512], ones1b,
                                    coefD1[pb][0:1, e_ * 1024 + s0:e_ * 1024 + s0 + 512],
                                    start=True, stop=True)), inc="t")
                                p.set_ready("bk6", "t")
                                p.req_ready("tensor", f"Wb{b1i}")
                                p.req_ready("tensor", f"z1b{pb}")
                                for mo in range(8):
                                    bki = 4 + (mo % 2)
                                    p.req_free("tensor", f"bk{bki}")
                                    for ks in range(8):
                                        p.op("tensor", (lambda t, mo=mo, ks=ks, bki=bki, b1i=b1i, pb=pb, s0=s0: t.matmul(
                                            banks[bki][:, 0:512],
                                            Wb[b1i][:, ks * 1024 + mo * 128:ks * 1024 + (mo + 1) * 128],
                                            z1b[pb][:, ks * 1024 + s0:ks * 1024 + s0 + 512],
                                            start=(ks == 0), stop=(ks == 7))), inc="t")
                                    p.set_ready(f"bk{bki}", "t")
                                    p.req_ready("scalar", f"bk{bki}")
                                    p.req_free("scalar", "htmp")
                                    p.op("scalar", (lambda sc, mo=mo, bki=bki, e_=e_, half=half: sc.activation(
                                        htmp, banks[bki][:, 0:512], AF.Gelu,
                                        bias=b1B[:, e_ * 16 + half * 8 + mo:e_ * 16 + half * 8 + mo + 1])), inc="s")
                                    p.set_free(f"bk{bki}", "s")
                                    p.set_ready("htmp", "s")
                                    p.req_ready("vector", "htmp")
                                    p.req_ready("vector", "bk6")
                                    p.req_free("vector", "hsD_" + str(mo))
                                    p.op("vector", (lambda v, mo=mo: v.tensor_mul(
                                        hsD[:, mo * 512:(mo + 1) * 512], htmp, banks[6][:, 0:512])), inc="v")
                                    p.set_free("htmp", "v")
                                    p.set_ready("hsD_" + str(mo), "v")
                                p.set_free("bk6", "v")
                                p.req_ready("tensor", f"Wb{b2i}")
                                first_u = (e_ == 0 and half == 0)
                                last_u = (e_ == E - 1 and half == 1)
                                for mg in range(2):
                                    for m in range(4):
                                        mm_ = mg * 4 + m
                                        p.req_free("tensor", f"bkw{sub}_{mm_}" if False else f"bk{m}")
                                        for ks2 in range(8):
                                            p.req_ready("tensor", "hsD_" + str(ks2))
                                            p.op("tensor", (lambda t, mm_=mm_, ks2=ks2, m=m, b2i=b2i: t.matmul(
                                                banks[m][:, 0:512],
                                                Wb[b2i][:, ks2 * 1024 + mm_ * 128:ks2 * 1024 + (mm_ + 1) * 128],
                                                hsD[:, ks2 * 512:(ks2 + 1) * 512],
                                                start=(ks2 == 0), stop=(ks2 == 7))), inc="t")
                                        p.set_ready(f"bk{m}", "t")
                                        p.req_ready("vector", f"bk{m}")
                                        p.req_ready("vector", "zacc0")
                                        p.op("vector", (lambda v, mm_=mm_, m=m, s0=s0: v.tensor_add(
                                            zacc[:, mm_ * 1024 + s0:mm_ * 1024 + s0 + 512],
                                            zacc[:, mm_ * 1024 + s0:mm_ * 1024 + s0 + 512],
                                            banks[m][:, 0:512])), inc="v")
                                        p.wait("vector", "v", p.val("v"))
                                        p.set_free(f"bk{m}", "v")
                            p.set_free(f"Wb{b1i}", "t")
                            p.set_free(f"Wb{b2i}", "t")
                            if u1 + 2 < 32 * NDT * B:
                                wload(u1 + 2)
                            if u1 + 3 < 32 * NDT * B:
                                wload(u1 + 3)
                    for mo in range(8):
                        p.set_free("hsD_" + str(mo), "t")
                    for sub in range(2):
                        s0 = sub * 512
                        p.req_free("sync", "z1fD")
                        for s_ in range(8):
                            p.dma("sync", z1fD[:, s_ * 512:(s_ + 1) * 512],
                                  z1_sp[b, s_, :, t0 + s0:t0 + s0 + 512], f"dzf{pb}")
                        p.set_ready("z1fD", f"dzf{pb}")
                        for m in range(8):
                            bki = 6 + (m % 2)
                            p.req_free("tensor", f"bk{bki}")
                            p.op("tensor", (lambda t, m=m, bki=bki, pb=pb, s0=s0: t.matmul(
                                banks[bki][:, 0:512], b2B[:, m * 128:(m + 1) * 128],
                                coefD8[pb][:, s0:s0 + 512], start=True, stop=True)), inc="t")
                            p.set_ready(f"bk{bki}", "t")
                            p.req_ready("vector", f"bk{bki}")
                            p.req_ready("vector", "z1fD")
                            p.op("vector", (lambda v, m=m, bki=bki, s0=s0: v.tensor_add(
                                zacc[:, m * 1024 + s0:m * 1024 + s0 + 512],
                                zacc[:, m * 1024 + s0:m * 1024 + s0 + 512],
                                banks[bki][:, 0:512])), inc="v")
                            p.wait("vector", "v", p.val("v"))
                            p.op("vector", (lambda v, m=m, s0=s0: v.tensor_add(
                                zacc[:, m * 1024 + s0:m * 1024 + s0 + 512],
                                zacc[:, m * 1024 + s0:m * 1024 + s0 + 512],
                                z1fD[:, m * 512:(m + 1) * 512])), inc="v")
                            p.wait("vector", "v", p.val("v"))
                            p.set_free(f"bk{bki}", "v")
                        p.set_ready("zpreD", "v")
                        p.req_ready("tensor", "zpreD")
                        p.req_free("tensor", "bk6")
                        for s_ in range(8):
                            p.op("tensor", (lambda t, s=s_, s0=s0: t.matmul(
                                banks[6][0:1, 0:512], onescolf,
                                zacc[:, s * 1024 + s0:s * 1024 + s0 + 512],
                                start=(s == 0), stop=(s == 7))), inc="t")
                        p.set_ready("bk6mu", "t")
                        p.req_free("tensor", "bk7")
                        for s_ in range(8):
                            p.req_free("vector", "sqD")
                            p.op("vector", (lambda v, s=s_, s0=s0: v.tensor_mul(
                                sqD, zacc[:, s * 1024 + s0:s * 1024 + s0 + 512],
                                zacc[:, s * 1024 + s0:s * 1024 + s0 + 512])), inc="v")
                            p.set_ready("sqD", "v")
                            p.req_ready("tensor", "sqD")
                            p.op("tensor", (lambda t, s=s_: t.matmul(
                                banks[7][0:1, 0:512], onescolf, sqD,
                                start=(s == 0), stop=(s == 7))), inc="t")
                            p.set_free("sqD", "t")
                        p.set_ready("bk7s", "t")
                        p.req_ready("vector", "bk6mu")
                        p.req_ready("vector", "bk7s")
                        p.req_free("vector", "rowDmu")
                        p.op("vector", lambda v: v.tensor_scalar(rowDmu, banks[6][0:1, 0:512],
                                                                 1.0 / 1024.0, None, ALU.mult), inc="v")
                        p.wait("vector", "v", p.val("v"))
                        p.req_free("vector", "rowD2")
                        p.op("vector", lambda v: v.tensor_mul(rowD2, rowDmu, rowDmu), inc="v")
                        p.wait("vector", "v", p.val("v"))
                        p.op("vector", lambda v: v.scalar_tensor_tensor(
                            rowD2, banks[7][0:1, 0:512], 1.0 / 1024.0, rowD2, ALU.mult,
                            ALU.subtract), inc="v")
                        p.wait("vector", "v", p.val("v"))
                        p.set_free("bk6", "v")
                        p.set_free("bk7", "v")
                        p.set_ready("varD", "v")
                        p.req_ready("scalar", "varD")
                        p.op("scalar", lambda sc: sc.activation(rowD2, rowD2, AF.Ln,
                                                                bias=cst[0:1, CC_EPS5:CC_EPS5 + 1]), inc="s")
                        p.wait("scalar", "s", p.val("s"))
                        p.op("scalar", lambda sc: sc.activation(rowD2, rowD2, AF.Exp,
                                                                scale=cst[0:1, CC_NHALF:CC_NHALF + 1]), inc="s")
                        p.set_ready("rstd2", "s")
                        p.req_ready("vector", "rstd2")
                        p.op("vector", lambda v: v.tensor_mul(rowDmu, rowDmu, rowD2), inc="v")
                        p.set_ready("murstd2", "v")
                        p.req_ready("tensor", "murstd2")
                        p.req_free("tensor", "bk6")
                        p.req_free("tensor", "bk7")
                        p.op("tensor", lambda t: t.matmul(banks[6][:, 0:512], ones1f, rowD2,
                                                          start=True, stop=True), inc="t")
                        p.op("tensor", lambda t: t.matmul(banks[7][:, 0:512],
                                                          ones1f, rowDmu, start=True, stop=True), inc="t")
                        p.set_ready("bk6", "t")
                        p.req_ready("vector", "bk6")
                        p.req_free("vector", "z2D")
                        for s_ in range(8):
                            p.op("vector", (lambda v, s=s_, s0=s0: v.tensor_mul(
                                z2D[:, s * 512:(s + 1) * 512],
                                zacc[:, s * 1024 + s0:s * 1024 + s0 + 512],
                                banks[6][:, 0:512])), inc="v")
                            p.wait("vector", "v", p.val("v"))
                            p.op("vector", (lambda v, s=s_: v.tensor_sub(
                                z2D[:, s * 512:(s + 1) * 512], z2D[:, s * 512:(s + 1) * 512],
                                banks[7][:, 0:512])), inc="v")
                            p.wait("vector", "v", p.val("v"))
                        p.set_free("bk6", "v")
                        p.set_free("bk7", "v")
                        p.set_ready("z2D", "v")
                        p.req_ready("tensor", "z2D")
                        for tb in range(4):
                            p.req_free("scalar", "outst")
                            for s_ in range(8):
                                bki = 4 + (s_ % 2)
                                p.req_free("tensor", f"bk{bki}")
                                p.op("tensor", (lambda t, s=s_, tb=tb, bki=bki: t.transpose(
                                    banks[bki][:, 0:128],
                                    z2D[:, s * 512 + tb * 128:s * 512 + (tb + 1) * 128],
                                    identf)), inc="t")
                                p.set_ready(f"bk{bki}", "t")
                                p.req_ready("scalar", f"bk{bki}")
                                p.op("scalar", (lambda sc, s=s_, bki=bki: sc.copy(
                                    outst[:, s * 128:(s + 1) * 128], banks[bki][:, 0:128])), inc="s")
                                p.set_free(f"bk{bki}", "s")
                            p.set_ready("outst", "s")
                            p.req_ready("gpsimd", "outst")
                            p.dma("gpsimd", out[b, t0 + s0 + tb * 128:t0 + s0 + (tb + 1) * 128, :],
                                  outst, "st")
                            p.set_free("outst", "st")
                        p.set_free("z1fD", "t")
                    p.set_free("zacc", "v")
                    p.set_free(f"z1b{pb}", "t")
                    p.set_free(f"coefD{pb}", "t")

        for qs in sorted(p.qsems):
            p.wait("sync", qs)

        p.verify()
        p.playback(ctx.enter_context(nc.Block()))
    return nc


def kernel(**inputs):
    percore = host_inputs(inputs)
    nc = build()
    res = run_bass_kernel_spmd(nc, percore, core_ids=list(range(8)))
    return np.concatenate([r["out"] for r in res.results], axis=0)


# version 04 (final)
# speedup vs baseline: 1.1245x; 1.1245x over previous
import numpy as np
import ml_dtypes
from contextlib import ExitStack

import concourse.bass as bass
import concourse.mybir as mybir
from concourse.bass_utils import run_bass_kernel_spmd

F32 = mybir.dt.float32
BF16 = mybir.dt.bfloat16
U8 = mybir.dt.uint8
AF = mybir.ActivationFunctionType
ALU = mybir.AluOpType
AX = mybir.AxisListType
NPBF = ml_dtypes.bfloat16

B = 2
T = 4096
D = 1024
H, DK, DV = 8, 64, 128
R = 64
F = 4096
E = 8
EH = 2048
C = 128
NCH = T // C
TOKT = 512
NTA = T // TOKT
DECAY, THR = 0.9, 0.5
EPS6, EPS5 = 1e-6, 1e-5

PHASES = "ABCD"
TEST_TAPS = ()
A_LIMIT = 999
CONV_ON = True
B_LIMIT = 999
B_STAGE = 99
C_LIMIT = 999
D_LIMIT = 999
C_STAGE = 99

CC_EPS6 = 0; CC_EPS5 = 1; CC_NHALF = 2; CC_HALF = 3; CC_NONE = 4; CC_I1024 = 5
CC_I128 = 6
CC_TINY = 7
NCONST = 8


class Prog:
    ENGS = ("sync", "gpsimd", "tensor", "vector", "scalar")

    def __init__(self, nc, ctx):
        self.nc = nc
        self.ctx = ctx
        self.items = {e: [] for e in self.ENGS}
        self.sems = {}
        self.cnt = {}
        self.qsems = set()
        self.ready = {}
        self.free = {}
        self._wmax = {e: {} for e in self.ENGS}

    def sem(self, name, queue=False):
        self.sems[name] = self.ctx.enter_context(self.nc.semaphore(name))
        self.cnt[name] = 0
        if queue:
            self.qsems.add(name)
        return name

    def op(self, eng, fn, inc=None, n=1):
        if inc is not None:
            self.cnt[inc] += n
        self.items[eng].append(("op", fn, inc, n))
        return self.cnt[inc] if inc is not None else None

    def wait(self, eng, sem, val=None):
        v = self.cnt[sem] if val is None else val
        if v <= 0:
            return
        wm = self._wmax[eng]
        if wm.get(sem, -1) >= v:
            return
        wm[sem] = v
        self.items[eng].append(("wait", sem, v))

    def dma(self, eng, out, in_, inc, **kw):
        return self.op(eng, (lambda e, o=out, i=in_, kw=kw: e.dma_start(out=o, in_=i, **kw)), inc=inc, n=16)

    def val(self, sem):
        return self.cnt[sem]

    def _resolve(self, s_, v_):
        return self.cnt[s_] if s_ in self.qsems else v_

    def req_free(self, eng, buf):
        for s_, v_ in self.free.pop(buf, ()):
            self.wait(eng, s_, self._resolve(s_, v_))

    def req_ready(self, eng, buf):
        if buf in self.ready:
            s_, v_ = self.ready[buf]
            self.wait(eng, s_, self._resolve(s_, v_))

    def set_ready(self, buf, sem):
        self.ready[buf] = (sem, self.cnt[sem])

    def set_free(self, buf, sem):
        lst = self.free.setdefault(buf, [])
        for i, (s_, _) in enumerate(lst):
            if s_ == sem:
                lst[i] = (sem, self.cnt[sem])
                break
        else:
            lst.append((sem, self.cnt[sem]))

    def verify(self):
        pos = {e: 0 for e in self.ENGS}
        cnt = {s_: 0 for s_ in self.cnt}
        progress = True
        while progress:
            progress = False
            for e in self.ENGS:
                items = self.items[e]
                while pos[e] < len(items):
                    it = items[pos[e]]
                    if it[0] == "op":
                        if it[2] is not None:
                            cnt[it[2]] += it[3]
                        pos[e] += 1
                        progress = True
                    else:
                        _, s_, v_ = it
                        if cnt[s_] >= v_:
                            pos[e] += 1
                            progress = True
                        else:
                            break
        stuck = {e: pos[e] for e in self.ENGS if pos[e] < len(self.items[e])}
        if stuck:
            msg = []
            for e, i in stuck.items():
                it = self.items[e][i]
                msg.append(f"{e}@{i}/{len(self.items[e])}: wait {it[1]} >= {it[2]} (have {cnt[it[1]]})")
            raise AssertionError("DEADLOCK: " + "; ".join(msg))

    def playback(self, block):
        for eng in self.ENGS:
            items = self.items[eng]

            def run(e, items=items):
                for it in items:
                    if it[0] == "op":
                        _, fn, incname, n = it
                        instr = fn(e)
                        if incname is not None:
                            instr.then_inc(self.sems[incname], n)
                    else:
                        _, s, v = it
                        e.wait_ge(self.sems[s], v)
            getattr(block, eng)(run)


def host_inputs(full):
    g = lambda n: np.asarray(full["L0_attn_" + n])
    gf = lambda n: np.asarray(full["L0_ffn_" + n])
    d = {}
    for n in ("wq", "wk", "wv", "w_spike", "w_au", "w_ad", "w_beta", "w_out",
              "w_u1", "w_u2", "w_ff1", "w_ff3", "w_ff2"):
        d[n] = np.ascontiguousarray(g(n), np.float32)
    d["moe_W1"] = np.ascontiguousarray(gf("W1"), np.float32)
    d["moe_W2"] = np.ascontiguousarray(gf("W2"), np.float32)
    d["moe_Wg"] = np.ascontiguousarray(gf("Wg"), np.float32)
    for n, key in (("w_norm_in", "wn"), ("conv_b", "cb"), ("b_u2", "bu2"),
                   ("w_ffn", "wffn")):
        d[key] = np.ascontiguousarray(g(n).reshape(8, 128).T, np.float32)
    d["conv_wp"] = np.ascontiguousarray(
        g("conv_w").reshape(8, 128, 4).transpose(1, 0, 2).reshape(128, 32), np.float32)
    d["b_au_p"] = np.ascontiguousarray(g("b_au").reshape(R, 1), np.float32)
    d["b_ad_p"] = np.ascontiguousarray(g("b_ad").reshape(4, 128).T, np.float32)
    d["b_u1_p"] = np.ascontiguousarray(g("b_u1").reshape(R, 1), np.float32)
    asp = g("alpha_spike")[0]
    d["aspk"] = np.ascontiguousarray(
        asp.reshape(4, 2, 64).transpose(1, 2, 0).reshape(128, 4), np.float32)
    bsp = g("beta_spike")[0]
    bb_bd = np.zeros((128, 8), np.float32)
    for hp in range(4):
        for j in range(2):
            bb_bd[j * 64:(j + 1) * 64, 2 * hp + j] = bsp[2 * hp + j]
    d["bspk_bd"] = bb_bd
    ones_bd = np.zeros((128, 2), np.float32)
    ones_bd[0:64, 0] = 1.0
    ones_bd[64:128, 1] = 1.0
    d["ones_bd"] = ones_bd
    d["b_beta_p"] = np.ascontiguousarray(g("b_beta").reshape(4, 2).T, np.float32)
    d["b_beta_col"] = np.ascontiguousarray(g("b_beta").reshape(8, 1), np.float32)
    d["w_hn_b"] = np.ascontiguousarray(
        np.broadcast_to(g("w_hn").reshape(1, 8 * 128), (128, 8 * 128)), np.float32)
    d["b1_p"] = np.ascontiguousarray(
        gf("b1").reshape(E, 16, 128).transpose(2, 0, 1).reshape(128, E * 16), np.float32)
    d["b2_lhsT"] = np.ascontiguousarray(gf("b2"), np.float32).astype(NPBF)
    d["bg_p"] = np.ascontiguousarray(gf("bg").reshape(E, 1), np.float32)
    sel2 = np.zeros((2, 128), np.float32)
    sel2[0, 0:64] = 1.0
    sel2[1, 64:128] = 1.0
    d["sel2"] = sel2
    d["nsel2"] = -sel2
    qsel = np.zeros((4, 128, 8), np.float32)
    for s in range(4):
        qsel[s, 0:64, 2 * s] = 1.0
        qsel[s, 64:128, 2 * s + 1] = 1.0
    d["qknorm_sel"] = qsel
    nbs = np.zeros((4, 8, 128), np.float32)
    for hp in range(4):
        nbs[hp, 2 * hp, 0:64] = 1.0
        nbs[hp, 2 * hp + 1, 64:128] = 1.0
    d["normbcast_sel"] = nbs
    idx = np.arange(128)
    d["mask_rt"] = (idx[None, :] > idx[:, None]).astype(np.uint8)
    d["mask_tr"] = (idx[:, None] > idx[None, :]).astype(np.uint8)
    d["mask_rt_incl"] = (idx[None, :] >= idx[:, None]).astype(np.uint8)
    for n in ("mask_rt", "mask_tr", "mask_rt_incl"):
        d[n + "4"] = np.ascontiguousarray(np.tile(d[n], (1, 4)))
    whn = g("w_hn")
    wb_ = np.zeros((16, 128), np.float32)
    for hp in range(4):
        for b_ in range(2):
            for h2 in range(2):
                wb_[(hp * 2 + b_) * 2 + h2] = whn[2 * hp + h2]
    d["whn_bh"] = np.ascontiguousarray(np.broadcast_to(wb_.reshape(1, 16 * 128), (128, 16 * 128)))
    d["ident_bf"] = np.eye(128, dtype=NPBF)
    d["ident_f32"] = np.eye(128, dtype=np.float32)
    d["ones1_f32"] = np.ones((1, 128), np.float32)
    d["ones1_bf"] = np.ones((1, 128), NPBF)
    d["onescol_f32"] = np.ones((128, 1), np.float32)
    cst = np.zeros((128, NCONST), np.float32)
    cst[:, CC_EPS6] = EPS6
    cst[:, CC_EPS5] = EPS5
    cst[:, CC_NHALF] = -0.5
    cst[:, CC_HALF] = 0.5
    cst[:, CC_NONE] = -1.0
    cst[:, CC_I1024] = 1.0 / 1024.0
    cst[:, CC_I128] = 1.0 / 128.0
    cst[:, CC_TINY] = 1e-30
    d["consts"] = cst

    x = np.asarray(full["L0_attn_x"], np.float32)
    percore = []
    for c in range(8):
        pc = dict(d)
        pc["x"] = np.ascontiguousarray(x[2 * c:2 * c + 2])
        percore.append(pc)
    return percore


def build():
    nc = bass.Bass("TRN2", target_bir_lowering=False)
    I = lambda n, s, dt=F32: nc.dram_tensor(n, s, dt, kind="ExternalInput").ap()
    O = lambda n, s, dt=F32: nc.dram_tensor(n, s, dt, kind="ExternalOutput").ap()

    def SCR(n, s, dt):
        kind = "ExternalOutput" if n in TEST_TAPS else "Internal"
        return nc.dram_tensor(n, s, dt, kind=kind).ap()

    xin = I("x", [B, T, D])
    w = {}
    for n, sh in (("wq", [D, 512]), ("wk", [D, 512]), ("wv", [D, 1024]),
                  ("w_spike", [D, 512]), ("w_au", [D, R]), ("w_ad", [R, 512]),
                  ("w_beta", [D, 8]), ("w_out", [D, D]), ("w_u1", [D, R]),
                  ("w_u2", [R, D]), ("w_ff1", [D, F]), ("w_ff3", [D, F]),
                  ("w_ff2", [F, D]), ("moe_W1", [E, D, EH]), ("moe_W2", [E, EH, D]),
                  ("moe_Wg", [D, E]), ("wn", [128, 8]), ("cb", [128, 8]),
                  ("bu2", [128, 8]), ("wffn", [128, 8]), ("conv_wp", [128, 32]),
                  ("b_au_p", [R, 1]), ("b_ad_p", [128, 4]), ("b_u1_p", [R, 1]),
                  ("aspk", [128, 4]), ("b_beta_p", [2, 4]), ("b_beta_col", [8, 1]),
                  ("w_hn_b", [128, 1024]), ("b1_p", [128, E * 16]), ("bg_p", [E, 1]),
                  ("qknorm_sel", [4, 128, 8]), ("normbcast_sel", [4, 8, 128]),
                  ("ident_f32", [128, 128]), ("ones1_f32", [1, 128]),
                  ("onescol_f32", [128, 1]), ("consts", [128, NCONST])):
        w[n] = I(n, sh)
    for n, sh in (("bspk_bd", [128, 8]), ("ones_bd", [128, 2]), ("sel2", [2, 128]),
                  ("nsel2", [2, 128])):
        w[n] = I(n, sh)
    for n, sh in (("ident_bf", [128, 128]), ("ones1_bf", [1, 128]),
                  ("b2_lhsT", [E, D])):
        w[n] = I(n, sh, BF16)
    for n in ("mask_rt", "mask_tr", "mask_rt_incl"):
        w[n] = I(n, [128, 128], U8)
        w[n + "4"] = I(n + "4", [128, 512], U8)
    w["whn_bh"] = I("whn_bh", [128, 2048])

    out = O("out", [B, T, D])

    x_fm = SCR("x_fm", [B, 8, 128, T], F32)
    q_sp = SCR("q_sp", [B, 4, 128, T], BF16)
    k_sp = SCR("k_sp", [B, 4, 128, T], BF16)
    v_sp = SCR("v_sp", [B, 8, 128, T], BF16)
    sd_sp = SCR("sd_sp", [B, 4, 128, T], F32)
    ab_sp = SCR("ab_sp", [B, 4, 128, T], F32)
    bb_sp = SCR("bb_sp", [B, 8, T], F32)
    y_sp = SCR("y_sp", [B, H, T, DV], BF16)
    z1_sp = SCR("z1_sp", [B, 8, 128, T], F32)
    coef_sp = SCR("coef_sp", [B, E, T], BF16)
    w1b = SCR("w1b", [E, D, EH], BF16)
    w2b = SCR("w2b", [E, EH, D], BF16)
    ff1b = SCR("ff1b", [D, F], BF16)
    ff3b = SCR("ff3b", [D, F], BF16)
    ff2b = SCR("ff2b", [F, D], BF16)
    bdbg = SCR("bdbg", [24, 128, 2048], F32)

    with ExitStack() as ctx:
        p = Prog(nc, ctx)
        sb = lambda n, s, dt=F32: ctx.enter_context(nc.sbuf_tensor(n, s, dt))[:]
        psum = lambda n, s, dt=F32: ctx.enter_context(nc.psum_tensor(n, s, dt))[:]
        dsems = (["qS", "qW", "c0", "c1", "st", "dxf", "dbbst", "dynb"] +
                 [f"dx{i}" for i in range(2)] + [f"devb{i}" for i in range(2)] +
                 [f"devf{i}" for i in range(2)] +
                 [f"d{nm}{i}" for nm in ("qc", "kc", "sdc", "abc", "bbr", "vtm",
                                          "yc", "xcb", "cf", "zf", "pan") for i in range(2)])
        for s_ in dsems:
            p.sem(s_, queue=True)
        for s_ in ("t", "v", "s", "g"):
            p.sem(s_)

        banks = [psum(f"bank{i}", [128, 512]) for i in range(8)]

        def barrier():
            for e in Prog.ENGS:
                for s2 in ("t", "v", "s", "g"):
                    p.wait(e, s2)
                for qs in sorted(p.qsems):
                    p.wait(e, qs)

        cst = sb("cst", [128, NCONST])
        masks = {n: sb(n + "_t", [128, 128], U8) for n in ("mask_rt", "mask_tr", "mask_rt_incl")}
        masks4 = {n: sb(n + "4_t", [128, 512], U8) for n in ("mask_rt", "mask_tr", "mask_rt_incl")}
        identb = sb("identb", [128, 128], BF16)
        identf = sb("identf", [128, 128])
        ones1f = sb("ones1f", [1, 128])
        ones1b = sb("ones1b", [1, 128], BF16)
        onescolf = sb("onescolf", [128, 1])
        sel2b = sb("sel2b", [2, 128])
        nsel2b = sb("nsel2b", [2, 128])
        onesbd = sb("onesbd", [128, 2])
        bspkbd = sb("bspkbd", [128, 8])
        wnB = sb("wnB", [128, 8])
        cbB = sb("cbB", [128, 8])
        bu2B = sb("bu2B", [128, 8])
        wffnB = sb("wffnB", [128, 8])
        convw = sb("convw", [128, 32])
        bauB = sb("bauB", [R, 1])
        badB = sb("badB", [128, 4])
        bu1B = sb("bu1B", [R, 1])
        aspkB = sb("aspkB", [128, 4])
        bbetaB = sb("bbetaB", [2, 4])
        bbetacB = sb("bbetacB", [8, 1])
        b1B = sb("b1B", [128, E * 16])
        b2B = sb("b2B", [E, D], BF16)
        bgB = sb("bgB", [E, 1])
        qkselB = sb("qkselB", [128, 32])
        nbselB = sb("nbselB", [8, 512])

        actx = ExitStack()
        sbA = lambda n, sh, dt=F32: actx.enter_context(nc.sbuf_tensor(n, sh, dt, side="right"))[:]
        wA = {}
        for n, cols in (("wq", 512), ("wk", 512), ("wv", 1024), ("w_spike", 512),
                        ("w_au", R), ("w_beta", 8)):
            wA[n] = sbA(n + "_b", [128, 8 * cols], BF16)
        wadB = sbA("wad_b", [R, 512], BF16)

        for ap_t, src in ((cst, w["consts"]), (identf, w["ident_f32"]),
                          (ones1f, w["ones1_f32"]), (onescolf, w["onescol_f32"]),
                          (wnB, w["wn"]), (cbB, w["cb"]), (bu2B, w["bu2"]),
                          (wffnB, w["wffn"]), (convw, w["conv_wp"]), (bauB, w["b_au_p"]),
                          (badB, w["b_ad_p"]), (bu1B, w["b_u1_p"]), (aspkB, w["aspk"]),
                          (bbetaB, w["b_beta_p"]), (bbetacB, w["b_beta_col"]),
                          (b1B, w["b1_p"]), (bgB, w["bg_p"]),
                          (masks["mask_rt"], w["mask_rt"]), (masks["mask_tr"], w["mask_tr"]),
                          (masks["mask_rt_incl"], w["mask_rt_incl"]),
                          (masks4["mask_rt"], w["mask_rt4"]), (masks4["mask_tr"], w["mask_tr4"]),
                          (masks4["mask_rt_incl"], w["mask_rt_incl4"]),
                          (identb, w["ident_bf"]), (ones1b, w["ones1_bf"]),
                          (sel2b, w["sel2"]), (nsel2b, w["nsel2"]),
                          (onesbd, w["ones_bd"]), (bspkbd, w["bspk_bd"]),
                          (b2B, w["b2_lhsT"])):
            p.dma("sync", ap_t, src, "qS")
        for s_ in range(4):
            p.dma("sync", qkselB[:, 8 * s_:8 * s_ + 8], w["qknorm_sel"][s_], "qS")
            p.dma("sync", nbselB[:, 128 * s_:128 * s_ + 128], w["normbcast_sel"][s_], "qS")
        for n, cols in (("wq", 512), ("wk", 512), ("wv", 1024), ("w_spike", 512),
                        ("w_au", R), ("w_beta", 8)):
            for ks in range(8):
                p.dma("gpsimd", wA[n][:, ks * cols:(ks + 1) * cols],
                      w[n][ks * 128:(ks + 1) * 128, :], "qW")
        p.dma("gpsimd", wadB, w["w_ad"], "qW")
        for eng in ("tensor", "vector", "scalar", "gpsimd"):
            p.wait(eng, "qS")
        p.wait("tensor", "qW")

        cvt = [sb(f"cvt{i}", [128, 2048], BF16) for i in range(2)]
        nconv = [0]

        def conv_blk(dst, src, nrows):
            ncols = dst.shape[-1]
            for r0 in range(0, nrows, 128):
                for c0 in range(0, ncols, 2048):
                    cw = min(2048, ncols - c0)
                    i = nconv[0] % 2
                    cs = "c0" if i == 0 else "c1"
                    p.wait("gpsimd", cs, p.val(cs))
                    p.dma("gpsimd", cvt[i][:, 0:cw], src[r0:r0 + 128, c0:c0 + cw], cs)
                    p.wait("gpsimd", cs, p.val(cs))
                    p.dma("gpsimd", dst[r0:r0 + 128, c0:c0 + cw], cvt[i][:, 0:cw], cs)
                    nconv[0] += 1
        if CONV_ON:
            for e in range(E):
                conv_blk(w1b[e], w["moe_W1"][e], D)
                conv_blk(w2b[e], w["moe_W2"][e], EH)
            conv_blk(ff1b, w["w_ff1"], D)
            conv_blk(ff3b, w["w_ff3"], D)
            conv_blk(ff2b, w["w_ff2"], F)
        conv_done = (p.val("c0"), p.val("c1"))

        if "A" in PHASES:
            sb_ = sb
            sb = sbA
            xtok = [sb(f"xtok{i}", [128, 4 * 1024]) for i in range(2)]
            xf = sb("xf", [128, 8 * 512])
            sqt = sb("sqt", [128, 512])
            h0 = sb("h0", [128, 8 * 515])
            cacc = [sb(f"cacc{i}", [128, 512]) for i in range(2)]
            hB = sb("hB", [128, 8 * 512], BF16)
            qf = sb("qf", [128, 512])
            evb = [sb(f"evb{i}", [128, 512], BF16) for i in range(2)]
            evf = [sb(f"evf{i}", [128, 512]) for i in range(2)]
            auB = sb("auB", [R, 512], BF16)
            rowf = sb("rowf", [8, 512])
            rmsrow = sb("rmsrow", [1, 512])
            bbst = sb("bbst", [8, 512])
            psT = [banks[0][:, 0:128], banks[1][:, 0:128]]
            psP = banks[2:5]
            psR = banks[5][0:1, :]
            psN = banks[6][0:8, :]
            psBC = banks[7]
            sb = sb_

            for b in range(B):
                for ti in range(NTA):
                    tix = b * NTA + ti
                    if ti >= A_LIMIT:
                        continue
                    xb = xtok[tix % 2]
                    xbn = f"xtok{tix % 2}"
                    p.req_free("sync", xbn)
                    for j in range(4):
                        p.dma("sync", xb[:, 1024 * j:1024 * (j + 1)],
                              xin[b, ti * 512 + 128 * j: ti * 512 + 128 * (j + 1), :], f"dx{tix % 2}")
                    p.set_ready(xbn, f"dx{tix % 2}")
                    p.req_ready("tensor", xbn)
                    p.req_free("scalar", "xf")
                    for s_ in range(8):
                        for j in range(4):
                            half = (s_ * 4 + j) % 2
                            p.req_free("tensor", f"psT{half}")
                            p.op("tensor", (lambda t, s=s_, j=j, h=half, xb=xb:
                                  t.transpose(psT[h], xb[:, 1024 * j + 128 * s: 1024 * j + 128 * (s + 1)],
                                              identf)), inc="t")
                            p.set_ready(f"psT{half}", "t")
                            p.req_ready("scalar", f"psT{half}")
                            p.op("scalar", (lambda sc, s=s_, j=j, h=half:
                                  sc.copy(xf[:, 512 * s + 128 * j: 512 * s + 128 * (j + 1)],
                                          psT[h])), inc="s")
                            p.set_free(f"psT{half}", "s")
                    p.set_free(xbn, "t")
                    p.set_ready("xf", "s")
                    p.req_ready("gpsimd", "xf")
                    for s_ in range(8):
                        p.dma("gpsimd", x_fm[b, s_, :, ti * 512:(ti + 1) * 512],
                              xf[:, 512 * s_:512 * (s_ + 1)], "dxf")
                    p.set_free("xf", "dxf")
                    p.req_ready("vector", "xf")
                    p.req_free("tensor", "psR")
                    for s_ in range(8):
                        p.req_free("vector", "sqt")
                        p.op("vector", (lambda v, s=s_: v.tensor_mul(
                            sqt, xf[:, 512 * s:512 * (s + 1)], xf[:, 512 * s:512 * (s + 1)])), inc="v")
                        p.set_ready("sqt", "v")
                        p.req_ready("tensor", "sqt")
                        p.op("tensor", (lambda t, s=s_: t.matmul(
                            psR, onescolf, sqt, start=(s == 0), stop=(s == 7))), inc="t")
                        p.set_free("sqt", "t")
                    p.set_ready("psR", "t")
                    p.req_ready("scalar", "psR")
                    p.req_free("scalar", "rmsrow")
                    p.op("scalar", lambda sc: sc.activation(
                        rmsrow, psR, AF.Ln, bias=cst[0:1, CC_EPS6:CC_EPS6 + 1],
                        scale=cst[0:1, CC_I1024:CC_I1024 + 1]), inc="s")
                    p.set_free("psR", "s")
                    p.wait("scalar", "s", p.val("s"))
                    p.op("scalar", lambda sc: sc.activation(
                        rmsrow, rmsrow, AF.Exp, scale=cst[0:1, CC_NHALF:CC_NHALF + 1]), inc="s")
                    p.set_ready("rmsrow", "s")
                    p.req_ready("tensor", "rmsrow")
                    p.req_free("tensor", "psBC")
                    p.op("tensor", lambda t: t.matmul(psBC, ones1f, rmsrow, start=True, stop=True), inc="t")
                    p.set_free("rmsrow", "t")
                    p.set_ready("psBC", "t")
                    p.req_ready("vector", "psBC")
                    for s_ in range(8):
                        if ti == 0:
                            p.op("vector", (lambda v, s=s_: v.memset(h0[:, 515 * s:515 * s + 3], 0.0)), inc="v")
                        else:
                            p.op("vector", (lambda v, s=s_: v.tensor_copy(
                                h0[:, 515 * s:515 * s + 3], h0[:, 515 * s + 512:515 * s + 515])), inc="v")
                        p.wait("vector", "v", p.val("v"))
                        p.op("vector", (lambda v, s=s_: v.scalar_tensor_tensor(
                            h0[:, 515 * s + 3:515 * s + 515], psBC, wnB[:, s:s + 1],
                            xf[:, 512 * s:512 * (s + 1)], ALU.mult, ALU.mult)), inc="v")
                        p.wait("vector", "v", p.val("v"))
                    p.set_free("psBC", "v")
                    p.set_free("xf", "v")
                    for s_ in range(8):
                        base = 515 * s_
                        p.req_free("vector", "cacc0")
                        p.op("vector", (lambda v, b_=base, s=s_: v.tensor_scalar(
                            cacc[0], h0[:, b_:b_ + 512], convw[:, 4 * s:4 * s + 1], None, ALU.mult)), inc="v")
                        p.wait("vector", "v", p.val("v"))
                        p.req_free("vector", "cacc1")
                        p.op("vector", (lambda v, b_=base, s=s_: v.scalar_tensor_tensor(
                            cacc[1], h0[:, b_ + 1:b_ + 513], convw[:, 4 * s + 1:4 * s + 2],
                            cacc[0], ALU.mult, ALU.add)), inc="v")
                        p.wait("vector", "v", p.val("v"))
                        p.op("vector", (lambda v, b_=base, s=s_: v.scalar_tensor_tensor(
                            cacc[0], h0[:, b_ + 2:b_ + 514], convw[:, 4 * s + 2:4 * s + 3],
                            cacc[1], ALU.mult, ALU.add)), inc="v")
                        p.wait("vector", "v", p.val("v"))
                        p.op("vector", (lambda v, b_=base, s=s_: v.scalar_tensor_tensor(
                            cacc[1], h0[:, b_ + 3:b_ + 515], convw[:, 4 * s + 3:4 * s + 4],
                            cacc[0], ALU.mult, ALU.add)), inc="v")
                        p.set_free("cacc0", "v")
                        p.set_ready("cacc1", "v")
                        p.req_ready("scalar", "cacc1")
                        if s_ == 0:
                            p.req_free("scalar", "hB")
                        p.op("scalar", (lambda sc, s=s_: sc.activation(
                            hB[:, 512 * s:512 * (s + 1)], cacc[1], AF.Silu,
                            bias=cbB[:, s:s + 1])), inc="s")
                        p.set_free("cacc1", "s")
                    p.set_ready("hB", "s")

                    outsl = ([("q", m) for m in range(4)] + [("k", m) for m in range(4)] +
                             [("v", m) for m in range(8)] + [("sd", m) for m in range(4)] +
                             [("au", 0), ("bb", 0)])
                    p.req_ready("tensor", "hB")
                    bank_i = [0]

                    def proj_mm(wt, m, mw, cols):
                        bk = bank_i[0] % 3
                        bank_i[0] += 1
                        p.req_free("tensor", f"psP{bk}")
                        for ks in range(8):
                            p.op("tensor", (lambda t, ks=ks, wt=wt, m=m, bk=bk, cols=cols, mw=mw:
                                  t.matmul(psP[bk][0:mw, :],
                                           wA[wt][:, ks * cols + m * 128: ks * cols + m * 128 + mw],
                                           hB[:, 512 * ks:512 * (ks + 1)],
                                           start=(ks == 0), stop=(ks == 7))), inc="t")
                        p.set_ready(f"psP{bk}", "t")
                        return bk

                    for name, m in outsl:
                        if name in ("q", "k"):
                            wt = "wq" if name == "q" else "wk"
                            bk = proj_mm(wt, m, 128, 512)
                            p.req_ready("scalar", f"psP{bk}")
                            p.req_free("scalar", "qf")
                            p.op("scalar", (lambda sc, bk=bk: sc.copy(qf, psP[bk])), inc="s")
                            p.set_free(f"psP{bk}", "s")
                            p.set_ready("qf", "s")
                            p.req_ready("vector", "qf")
                            p.req_free("vector", "sqt")
                            p.op("vector", lambda v: v.tensor_mul(sqt, qf, qf), inc="v")
                            p.set_ready("sqt", "v")
                            p.req_ready("tensor", "sqt")
                            p.req_free("tensor", "psN")
                            p.op("tensor", (lambda t, m=m: t.matmul(
                                psN, qkselB[:, 8 * m:8 * (m + 1)], sqt, start=True, stop=True)), inc="t")
                            p.set_free("sqt", "t")
                            p.set_ready("psN", "t")
                            p.req_ready("scalar", "psN")
                            p.req_free("scalar", "rowf")
                            p.op("scalar", lambda sc: sc.activation(rowf, psN, AF.Ln,
                                                                     bias=cst[0:8, CC_TINY:CC_TINY + 1]), inc="s")
                            p.set_free("psN", "s")
                            for fn in (lambda sc: sc.activation(rowf, rowf, AF.Exp,
                                                                scale=cst[0:8, CC_HALF:CC_HALF + 1]),
                                       lambda sc: sc.activation(rowf, rowf, AF.Ln,
                                                                bias=cst[0:8, CC_EPS6:CC_EPS6 + 1]),
                                       lambda sc: sc.activation(rowf, rowf, AF.Exp,
                                                                scale=cst[0:8, CC_NONE:CC_NONE + 1])):
                                p.wait("scalar", "s", p.val("s"))
                                p.op("scalar", fn, inc="s")
                            p.set_ready("rowf", "s")
                            p.req_ready("tensor", "rowf")
                            p.req_free("tensor", "psBC")
                            p.op("tensor", (lambda t, m=m: t.matmul(
                                psBC, nbselB[:, 128 * m:128 * (m + 1)], rowf, start=True, stop=True)), inc="t")
                            p.set_free("rowf", "t")
                            p.set_ready("psBC", "t")
                            ei = 0 if name == "q" else 1
                            sp = q_sp if name == "q" else k_sp
                            p.req_ready("vector", "psBC")
                            p.req_free("vector", f"evb{ei}")
                            p.op("vector", (lambda v, ei=ei: v.tensor_mul(evb[ei], qf, psBC)), inc="v")
                            p.set_free("psBC", "v")
                            p.set_free("qf", "v")
                            p.set_ready(f"evb{ei}", "v")
                            p.req_ready("gpsimd", f"evb{ei}")
                            p.dma("gpsimd", sp[b, m, :, ti * 512:(ti + 1) * 512], evb[ei], f"devb{ei}")
                            p.set_free(f"evb{ei}", f"devb{ei}")
                        elif name == "v":
                            bk = proj_mm("wv", m, 128, 1024)
                            ei = m % 2
                            p.req_ready("scalar", f"psP{bk}")
                            p.req_free("scalar", f"evb{ei}")
                            p.op("scalar", (lambda sc, bk=bk, ei=ei: sc.copy(evb[ei], psP[bk])), inc="s")
                            p.set_free(f"psP{bk}", "s")
                            p.set_ready(f"evb{ei}", "s")
                            p.req_ready("gpsimd", f"evb{ei}")
                            p.dma("gpsimd", v_sp[b, m, :, ti * 512:(ti + 1) * 512], evb[ei], f"devb{ei}")
                            p.set_free(f"evb{ei}", f"devb{ei}")
                        elif name == "sd":
                            bk = proj_mm("w_spike", m, 128, 512)
                            ei = m % 2
                            p.req_ready("scalar", f"psP{bk}")
                            p.req_free("scalar", f"evf{ei}")
                            p.op("scalar", (lambda sc, bk=bk, ei=ei: sc.copy(evf[ei], psP[bk])), inc="s")
                            p.set_free(f"psP{bk}", "s")
                            p.set_ready(f"evf{ei}", "s")
                            p.req_ready("gpsimd", f"evf{ei}")
                            p.dma("gpsimd", sd_sp[b, m, :, ti * 512:(ti + 1) * 512], evf[ei], f"devf{ei}")
                            p.set_free(f"evf{ei}", f"devf{ei}")
                        elif name == "au":
                            bk = proj_mm("w_au", 0, R, R)
                            p.req_ready("scalar", f"psP{bk}")
                            p.req_free("scalar", "auB")
                            p.op("scalar", (lambda sc, bk=bk: sc.activation(
                                auB, psP[bk][0:R, :], AF.Silu, bias=bauB)), inc="s")
                            p.set_free(f"psP{bk}", "s")
                            p.set_ready("auB", "s")
                            p.req_ready("tensor", "auB")
                            for m2 in range(4):
                                bk2 = bank_i[0] % 3
                                bank_i[0] += 1
                                p.req_free("tensor", f"psP{bk2}")
                                p.op("tensor", (lambda t, m2=m2, bk2=bk2: t.matmul(
                                    psP[bk2], wadB[:, m2 * 128:(m2 + 1) * 128], auB,
                                    start=True, stop=True)), inc="t")
                                p.set_ready(f"psP{bk2}", "t")
                                ei = m2 % 2
                                p.req_ready("scalar", f"psP{bk2}")
                                p.req_free("scalar", f"evf{ei}")
                                p.op("scalar", (lambda sc, bk2=bk2, m2=m2, ei=ei: sc.activation(
                                    evf[ei], psP[bk2], AF.Identity, bias=badB[:, m2:m2 + 1])), inc="s")
                                p.set_free(f"psP{bk2}", "s")
                                p.set_ready(f"evf{ei}", "s")
                                p.req_ready("gpsimd", f"evf{ei}")
                                p.dma("gpsimd", ab_sp[b, m2, :, ti * 512:(ti + 1) * 512], evf[ei], f"devf{ei}")
                                p.set_free(f"evf{ei}", f"devf{ei}")
                            p.set_free("auB", "t")
                        else:
                            bk = proj_mm("w_beta", 0, 8, 8)
                            p.req_ready("scalar", f"psP{bk}")
                            p.req_free("scalar", "bbst")
                            p.op("scalar", (lambda sc, bk=bk: sc.activation(
                                bbst, psP[bk][0:8, :], AF.Identity, bias=bbetacB)), inc="s")
                            p.set_free(f"psP{bk}", "s")
                            p.set_ready("bbst", "s")
                            p.req_ready("gpsimd", "bbst")
                            p.dma("gpsimd", bb_sp[b, :, ti * 512:(ti + 1) * 512], bbst, "dbbst")
                            p.set_free("bbst", "dbbst")
                    p.set_free("hB", "t")

        actx.close()
        barrier()

        bctx = ExitStack()
        if "B" in PHASES:
            sbB = lambda n, sh, dt=F32: bctx.enter_context(nc.sbuf_tensor(n, sh, dt, side="right"))[:]
            whnbhB = sbB("whnbhB", [128, 2048])
            p.dma("sync", whnbhB, w["whn_bh"], "qS")
            p.wait("vector", "qS")
            qc = [sbB(f"qc{i}", [128, 1024], BF16) for i in range(2)]
            kc = [sbB(f"kc{i}", [128, 1024], BF16) for i in range(2)]
            sdc = [sbB(f"sdc{i}", [128, 1024]) for i in range(2)]
            abc = [sbB(f"abc{i}", [128, 1024]) for i in range(2)]
            vtm = [sbB(f"vtm{i}", [128, 2048], BF16) for i in range(2)]
            bbr = [sbB(f"bbr{i}", [2, 1024]) for i in range(2)]
            spk = sbB("spk", [128, 1024])
            memt = sbB("memt", [128, 8])
            m1t = sbB("m1t", [128, 8])
            alf = sbB("alf", [128, 1024])
            alf2 = sbB("alf2", [128, 1024])
            Graw = sbB("Graw", [128, 1024])
            Gi = sbB("Gi", [128, 1024])
            rhot = sbB("rhot", [128, 8])
            rhoi = sbB("rhoi", [128, 8])
            rhog = sbB("rhog", [128, 8])
            onesB = sbB("onesB", [128, 64])
            zt4 = sbB("zt4", [128, 512])
            actr = sbB("actr", [2, 1024])
            ssr = sbB("ssr", [2, 1024])
            brow = sbB("brow", [2, 1024])
            abcast = sbB("abcast", [128, 1024])
            bbcn = sbB("bbcn", [128, 1024])
            bcolT = sbB("bcolT", [128, 16])
            af = sbB("af", [128, 1024])
            abuf = sbB("abuf", [128, 1024], BF16)
            qtb = sbB("qtb", [128, 1024], BF16)
            bhn = sbB("bhn", [128, 1024], BF16)
            tmpf = sbB("tmpf", [128, 1024])
            ahf = sbB("ahf", [128, 1024])
            aTb = sbB("aTb", [128, 1024], BF16)
            XS = [sbB(f"XS{i}", [128, 2048], BF16) for i in range(2)]
            XTS = [sbB(f"XTS{i}", [128, 2048], BF16) for i in range(2)]
            Ub = [sbB(f"Ub{i}", [128, 2048], BF16) for i in range(2)]
            g1t = sbB("g1t", [128, 2048], BF16)
            Sb = [sbB(f"Sb{i}", [128, 1024]) for i in range(2)]
            S0pb = sbB("S0pb", [128, 1024], BF16)
            qtz = sbB("qtz", [128, 2048], BF16)
            yf = sbB("yf", [128, 2048])
            sqy = sbB("sqy", [128, 512])
            ysum = sbB("ysum", [128, 16])
            yrst = sbB("yrst", [128, 16])
            ynb = sbB("ynb", [128, 2048], BF16)

            p.op("vector", lambda v: v.memset(memt, 0.0), inc="v")
            p.op("vector", lambda v: v.memset(onesB, 1.0), inc="v")
            p.op("vector", lambda v: v.memset(zt4, 0.0), inc="v")
            p.op("vector", lambda v: v.memset(Sb[0], 0.0), inc="v")
            p.wait("vector", "v", p.val("v"))
            init_v = p.val("v")
            p.wait("tensor", "v", init_v)
            p.wait("scalar", "v", init_v)
            p.wait("gpsimd", "v", init_v)

            def b_loads(ci):
                pb = ci % 2
                c0, c1 = ci * C, (ci + 1) * C
                for nm, buf, sp in (("qc", qc[pb], q_sp), ("kc", kc[pb], k_sp),
                                    ("sdc", sdc[pb], sd_sp), ("abc", abc[pb], ab_sp)):
                    p.req_free("gpsimd", f"{nm}{pb}")
                    bufr = buf.rearrange("p (hp b t) -> p hp b t", hp=4, b=2, t=C)
                    for b_ in range(2):
                        p.dma("gpsimd", bufr[:, :, b_, :],
                              sp[b_, :, :, c0:c1].rearrange("hp p t -> p hp t"), f"d{nm}{pb}")
                    p.set_ready(f"{nm}{pb}", f"d{nm}{pb}")
                p.req_free("gpsimd", f"bbr{pb}")
                bbrr = bbr[pb].rearrange("h2 (hp b t) -> h2 hp b t", hp=4, b=2, t=C)
                for b_ in range(2):
                    p.dma("gpsimd", bbrr[:, :, b_, :],
                          bb_sp[b_, :, c0:c1].rearrange("(hp h2) t -> h2 hp t", h2=2), f"dbbr{pb}")
                p.set_ready(f"bbr{pb}", f"dbbr{pb}")
                p.req_free("sync", f"vtm{pb}")
                for g in range(8):
                    hp, b_ = g // 2, g % 2
                    for h2 in range(2):
                        vi = g * 2 + h2
                        p.dma("sync", vtm[pb][:, vi * 128:(vi + 1) * 128],
                              v_sp[b_, 2 * hp + h2, :, c0:c1], f"dvtm{pb}", transpose=True)
                p.set_ready(f"vtm{pb}", f"dvtm{pb}")

            b_loads(0)
            for ci in range(min(NCH, B_LIMIT)):
                pb = ci % 2
                sc_, sn = Sb[pb], Sb[1 - pb]
                p.req_ready("vector", f"sdc{pb}")
                p.req_free("vector", "spk")
                for t_ in range(C):
                    p.op("vector", (lambda v, t=t_, pb=pb: v.scalar_tensor_tensor(
                        m1t, memt, DECAY, sdc[pb][:, t:1024:128], ALU.mult, ALU.add)), inc="v")
                    p.wait("vector", "v", p.val("v"))
                    p.op("vector", (lambda v, t=t_: v.tensor_single_scalar(
                        spk[:, t:1024:128], m1t, THR, ALU.is_gt)), inc="v")
                    p.wait("vector", "v", p.val("v"))
                    p.op("vector", (lambda v, t=t_: v.scalar_tensor_tensor(
                        memt, spk[:, t:1024:128], -THR, m1t, ALU.mult, ALU.add)), inc="v")
                    p.wait("vector", "v", p.val("v"))
                p.set_free(f"sdc{pb}", "v")
                p.set_ready("spk", "v")
                if ci + 1 < min(NCH, B_LIMIT):
                    b_loads(ci + 1)
                if B_STAGE < 1:
                    continue
                p.req_ready("vector", f"abc{pb}")
                for hp in range(4):
                    p.op("vector", (lambda v, hp=hp: v.tensor_scalar(
                        alf[:, hp * 256:(hp + 1) * 256], spk[:, hp * 256:(hp + 1) * 256],
                        aspkB[:, hp:hp + 1], None, ALU.mult)), inc="v")
                p.wait("vector", "v", p.val("v"))
                p.op("vector", (lambda v, pb=pb: v.tensor_add(alf2, alf, abc[pb])), inc="v")
                p.set_ready("alf2", "v")
                p.set_free(f"abc{pb}", "v")
                p.req_ready("scalar", "alf2")
                p.op("scalar", lambda sc: sc.activation(alf, alf2, AF.Sigmoid), inc="s")
                p.set_ready("alfsig", "s")
                if B_STAGE < 2:
                    continue
                p.req_ready("tensor", "spk")
                for half in range(2):
                    p.req_free("tensor", "bk6")
                    for hp in (2 * half, 2 * half + 1):
                        p.op("tensor", (lambda t, hp=hp, half=half: t.matmul(
                            banks[6][0:2, (hp - 2 * half) * 256:(hp - 2 * half + 1) * 256],
                            bspkbd[:, 2 * hp:2 * hp + 2],
                            spk[:, hp * 256:(hp + 1) * 256], start=True, stop=True)), inc="t")
                    p.set_ready("bk6", "t")
                    p.req_ready("scalar", "bk6")
                    p.op("scalar", (lambda sc, half=half: sc.copy(
                        ssr[:, half * 512:(half + 1) * 512], banks[6][0:2, :])), inc="s")
                    p.set_free("bk6", "s")
                    p.req_free("tensor", "bk7")
                    p.op("tensor", (lambda t, half=half: t.matmul(
                        banks[7][0:2, :], onesbd, spk[:, half * 512:(half + 1) * 512],
                        start=True, stop=True)), inc="t")
                    p.set_ready("bk7", "t")
                    p.req_ready("scalar", "bk7")
                    p.op("scalar", (lambda sc, half=half: sc.copy(
                        actr[:, half * 512:(half + 1) * 512], banks[7][0:2, :])), inc="s")
                    p.set_free("bk7", "s")
                p.set_ready("rows", "s")
                p.req_ready("vector", "rows")
                p.req_ready("vector", f"bbr{pb}")
                p.op("vector", (lambda v, pb=pb: v.tensor_add(brow, ssr, bbr[pb])), inc="v")
                p.set_free(f"bbr{pb}", "v")
                p.wait("vector", "v", p.val("v"))
                p.op("vector", lambda v: v.tensor_single_scalar(ssr, actr, 0.0, ALU.is_gt), inc="v")
                p.set_ready("actrow", "v")
                p.req_ready("scalar", "actrow")
                p.op("scalar", lambda sc: sc.activation(brow, brow, AF.Sigmoid), inc="s")
                p.wait("scalar", "s", p.val("s"))
                p.set_ready("brow_sig", "s")
                p.req_ready("vector", "brow_sig")
                p.op("vector", lambda v: v.tensor_mul(brow, brow, ssr), inc="v")
                p.set_ready("brow", "v")
                for half in range(2):
                    p.req_ready("tensor", "brow" if half == 0 else "brow")
                    p.req_free("tensor", "bk7")
                    p.op("tensor", (lambda t, half=half: t.matmul(
                        banks[7], sel2b, ssr[:, half * 512:(half + 1) * 512],
                        start=True, stop=True)), inc="t")
                    p.set_ready("bk7", "t")
                    p.req_ready("scalar", "bk7")
                    p.op("scalar", (lambda sc, half=half: sc.copy(
                        abcast[:, half * 512:(half + 1) * 512], banks[7])), inc="s")
                    p.set_free("bk7", "s")
                    p.req_free("tensor", "bk6")
                    p.op("tensor", (lambda t, half=half: t.matmul(
                        banks[6], nsel2b, brow[:, half * 512:(half + 1) * 512],
                        start=True, stop=True)), inc="t")
                    p.set_ready("bk6", "t")
                    p.req_ready("scalar", "bk6")
                    p.op("scalar", (lambda sc, half=half: sc.copy(
                        bbcn[:, half * 512:(half + 1) * 512], banks[6])), inc="s")
                    p.set_free("bk6", "s")
                p.set_ready("bcasts", "s")
                p.req_ready("vector", "alfsig")
                p.req_ready("vector", "bcasts")
                p.op("vector", lambda v: v.tensor_scalar(alf2, alf, -1.0, None, ALU.add), inc="v")
                p.wait("vector", "v", p.val("v"))
                p.op("vector", lambda v: v.tensor_mul(alf2, alf2, abcast), inc="v")
                p.wait("vector", "v", p.val("v"))
                p.op("vector", lambda v: v.tensor_scalar(alf2, alf2, 1.0, None, ALU.add), inc="v")
                p.wait("vector", "v", p.val("v"))
                if B_STAGE < 3:
                    continue
                p.req_free("vector", "Graw")
                for g in range(8):
                    for hf in range(2):
                        p.op("vector", (lambda v, g=g, hf=hf: v.tensor_tensor_scan(
                            Graw[:, g * 128 + hf * 64: g * 128 + (hf + 1) * 64],
                            alf2[:, g * 128 + hf * 64: g * 128 + (hf + 1) * 64],
                            onesB, 1.0, ALU.mult, ALU.mult)), inc="v")
                p.wait("vector", "v", p.val("v"))
                p.op("vector", lambda v: v.tensor_scalar(rhot, Graw[:, 63:1024:128], 1e-35, None, ALU.max), inc="v")
                p.wait("vector", "v", p.val("v"))
                p.op("vector", lambda v: v.reciprocal(rhoi, rhot), inc="v")
                p.wait("vector", "v", p.val("v"))
                for g in range(8):
                    p.op("vector", (lambda v, g=g: v.tensor_scalar(
                        Graw[:, g * 128:g * 128 + 64], Graw[:, g * 128:g * 128 + 64],
                        rhoi[:, g:g + 1], None, ALU.mult)), inc="v")
                p.wait("vector", "v", p.val("v"))
                p.op("vector", lambda v: v.tensor_scalar(Graw, Graw, 1e-36, 1e36, ALU.max, ALU.min), inc="v")
                p.wait("vector", "v", p.val("v"))
                p.set_ready("Graw", "v")
                p.req_free("vector", "Gi")
                p.op("vector", lambda v: v.reciprocal(Gi, Graw), inc="v")
                p.wait("vector", "v", p.val("v"))
                p.set_ready("Gi", "v")
                p.op("vector", lambda v: v.tensor_mul(rhog, rhot, Graw[:, 127:1024:128]), inc="v")
                p.set_ready("rhog", "v")
                p.req_ready("vector", "Gi")
                p.req_ready("vector", f"kc{pb}")
                p.req_ready("vector", f"qc{pb}")
                p.req_free("vector", "af")
                p.req_free("vector", "abuf")
                p.req_free("vector", "qtb")
                p.req_free("vector", "bhn")
                p.req_free("vector", "ahfw")
                p.op("vector", (lambda v, pb=pb: v.tensor_mul(af, kc[pb], Gi)), inc="v")
                p.wait("vector", "v", p.val("v"))
                p.op("vector", lambda v: v.tensor_copy(abuf, af), inc="v")
                p.set_ready("abuf", "v")
                p.op("vector", (lambda v, pb=pb: v.tensor_mul(qtb, qc[pb], Graw)), inc="v")
                p.set_ready("qtb", "v")
                p.set_free(f"qc{pb}", "v")
                p.op("vector", (lambda v, pb=pb: v.tensor_mul(tmpf, kc[pb], Graw)), inc="v")
                p.set_free(f"kc{pb}", "v")
                p.wait("vector", "v", p.val("v"))
                p.op("vector", lambda v: v.tensor_mul(bhn, tmpf, bbcn), inc="v")
                p.set_ready("bhn", "v")
                for g in range(8):
                    p.op("vector", (lambda v, g=g: v.tensor_scalar(
                        ahf[:, g * 128:(g + 1) * 128], af[:, g * 128:(g + 1) * 128],
                        Graw[:, g * 128 + 127:g * 128 + 128], None, ALU.mult)), inc="v")
                p.set_ready("ahf", "v")
                p.set_free("af", "v")
                p.set_free("Gi", "v")
                p.req_ready("vector", "Sb")
                p.req_free("vector", "S0pb")
                for g in range(8):
                    p.op("vector", (lambda v, g=g, sc_=sc_: v.tensor_scalar(
                        S0pb[:, g * 128:(g + 1) * 128], sc_[:, g * 128:(g + 1) * 128],
                        rhot[:, g:g + 1], None, ALU.mult)), inc="v")
                p.set_ready("S0pb", "v")
                if B_STAGE < 4:
                    continue
                p.req_ready("tensor", "brow")
                for g in range(8):
                    bki = 2 + (g % 2)
                    p.req_free("tensor", f"bk{bki}")
                    p.op("tensor", (lambda t, g=g, bki=bki: t.transpose(
                        banks[bki][:, 0:2], brow[0:2, g * 128:(g + 1) * 128], identf[0:2, 0:2])), inc="t")
                    p.set_ready(f"bk{bki}", "t")
                    p.req_ready("scalar", f"bk{bki}")
                    p.op("scalar", (lambda sc, g=g, bki=bki: sc.copy(
                        bcolT[:, 2 * g:2 * g + 2], banks[bki][:, 0:2])), inc="s")
                    p.set_free(f"bk{bki}", "s")
                p.set_ready("bcolT", "s")
                p.req_ready("tensor", "ahf")
                p.req_free("scalar", "aTb")
                for g in range(8):
                    bki = 4 + (g % 2)
                    p.req_free("tensor", f"bk{bki}")
                    p.op("tensor", (lambda t, g=g, bki=bki: t.transpose(
                        banks[bki][:, 0:128], ahf[:, g * 128:(g + 1) * 128], identf)), inc="t")
                    p.set_ready(f"bk{bki}", "t")
                    p.req_ready("scalar", f"bk{bki}")
                    p.op("scalar", (lambda sc, g=g, bki=bki: sc.copy(
                        aTb[:, g * 128:(g + 1) * 128], banks[bki][:, 0:128])), inc="s")
                    p.set_free(f"bk{bki}", "s")
                p.set_ready("aTb", "s")
                p.set_free("ahfw", "t")

                def bh_slices(bi):
                    h2, g = bi // 8, bi % 8
                    rs = slice(h2 * 64, (h2 + 1) * 64)
                    csl = slice(g * 128, (g + 1) * 128)
                    return g, h2, rs, csl

                if B_STAGE < 5:
                    continue
                p.req_ready("tensor", "abuf")
                p.req_ready("tensor", "bhn")
                for bh in range(16):
                    g, h2, rs, csl = bh_slices(bh)
                    q4, qi = bh // 4, bh % 4
                    gb = g * 128
                    if qi == 0:
                        p.req_free("tensor", f"bk{q4}")
                    p.op("tensor", (lambda t, rs=rs, gb=gb, q4=q4, qi=qi: t.matmul(
                        banks[q4][0:64, qi * 128:(qi + 1) * 128],
                        abuf[rs, gb:gb + 64], bhn[rs, gb:gb + 128],
                        start=True, stop=True)), inc="t")
                for bh in range(16):
                    g, h2, rs, csl = bh_slices(bh)
                    q4, qi = bh // 4, bh % 4
                    gb = g * 128
                    p.op("tensor", (lambda t, rs=rs, gb=gb, q4=q4, qi=qi: t.matmul(
                        banks[q4][64:128, qi * 128 + 64:(qi + 1) * 128],
                        abuf[rs, gb + 64:gb + 128], bhn[rs, gb + 64:gb + 128],
                        start=True, stop=True)), inc="t")
                    if qi == 3:
                        p.set_ready(f"bk{q4}", "t")
                for q4 in range(4):
                    p.req_ready("vector", f"bk{q4}")
                    p.req_free("vector", "XS0")
                    for qi in range(4):
                        p.op("vector", (lambda v, q4=q4, qi=qi: v.memset(
                            banks[q4][64:128, qi * 128:qi * 128 + 64], 0.0)), inc="v")
                    p.op("vector", (lambda v, q4=q4: v.memset(
                        XS[0][:, q4 * 512:(q4 + 1) * 512], 0.0)), inc="v")
                    p.wait("vector", "v", p.val("v"))
                    p.op("vector", (lambda v, q4=q4: v.copy_predicated(
                        XS[0][:, q4 * 512:(q4 + 1) * 512], masks4["mask_rt"], banks[q4])), inc="v")
                    p.set_free(f"bk{q4}", "v")
                p.set_ready("XS0", "v")
                p.set_ready("XS0@v", "v")
                for bh in range(16):
                    g, h2, rs, csl = bh_slices(bh)
                    q4, qi = bh // 4, bh % 4
                    gb = g * 128
                    if qi == 0:
                        p.req_free("tensor", f"bk{4 + q4}")
                    p.op("tensor", (lambda t, rs=rs, gb=gb, q4=q4, qi=qi: t.matmul(
                        banks[4 + q4][0:64, qi * 128:qi * 128 + 64],
                        bhn[rs, gb:gb + 64], abuf[rs, gb:gb + 64],
                        start=True, stop=True)), inc="t")
                for bh in range(16):
                    g, h2, rs, csl = bh_slices(bh)
                    q4, qi = bh // 4, bh % 4
                    gb = g * 128
                    p.op("tensor", (lambda t, rs=rs, gb=gb, q4=q4, qi=qi: t.matmul(
                        banks[4 + q4][64:128, qi * 128:(qi + 1) * 128],
                        bhn[rs, gb + 64:gb + 128], abuf[rs, gb:gb + 128],
                        start=True, stop=True)), inc="t")
                    if qi == 3:
                        p.set_ready(f"bk{4 + q4}", "t")
                for q4 in range(4):
                    p.req_ready("vector", f"bk{4 + q4}")
                    for qi in range(4):
                        p.op("vector", (lambda v, q4=q4, qi=qi: v.memset(
                            banks[4 + q4][0:64, qi * 128 + 64:(qi + 1) * 128], 0.0)), inc="v")
                    p.op("vector", (lambda v, q4=q4: v.memset(
                        XTS[0][:, q4 * 512:(q4 + 1) * 512], 0.0)), inc="v")
                    p.wait("vector", "v", p.val("v"))
                    p.op("vector", (lambda v, q4=q4: v.copy_predicated(
                        XTS[0][:, q4 * 512:(q4 + 1) * 512], masks4["mask_tr"], banks[4 + q4])), inc="v")
                    p.set_free(f"bk{4 + q4}", "v")
                p.set_ready("XTS0", "v")
                p.set_ready("XTS0@v", "v")
                if B_STAGE < 6:
                    continue
                p.req_ready("tensor", "S0pb")
                for bh in range(16):
                    g, h2, rs, csl = bh_slices(bh)
                    q4, qi = bh // 4, bh % 4
                    if qi == 0:
                        p.req_free("tensor", f"bk{q4}")
                    p.op("tensor", (lambda t, rs=rs, csl=csl, q4=q4, qi=qi: t.matmul(
                        banks[q4][:, qi * 128:(qi + 1) * 128], bhn[rs, csl], S0pb[rs, csl],
                        start=True, stop=True)), inc="t")
                    if qi == 3:
                        p.set_ready(f"bk{q4}", "t")
                p.req_ready("vector", "vtm%d" % pb)
                p.req_ready("vector", "bcolT")
                for bh in range(16):
                    g, h2, rs, csl = bh_slices(bh)
                    q4, qi = bh // 4, bh % 4
                    vi = g * 2 + h2
                    if qi == 0:
                        p.req_ready("vector", f"bk{q4}")
                    p.op("vector", (lambda v, bh=bh, vi=vi, q4=q4, qi=qi, pb=pb: v.scalar_tensor_tensor(
                        Ub[0][:, bh * 128:(bh + 1) * 128], vtm[pb][:, vi * 128:(vi + 1) * 128],
                        bcolT[:, vi:vi + 1], banks[q4][:, qi * 128:(qi + 1) * 128],
                        ALU.mult, ALU.add)), inc="v")
                    if qi == 3:
                        p.set_free(f"bk{q4}", "v")
                p.set_ready("U0", "v")
                p.set_ready("U0@v", "v")
                p.set_free(f"vtm{pb}", "v")
                if B_STAGE < 7:
                    continue
                ucur = 0
                xpar = 0
                for j in range(7):
                    gA = 0 if j % 2 == 0 else 4
                    gB = 4 - gA
                    p.req_ready("tensor", f"U{ucur}")
                    p.req_ready("tensor", f"U{ucur}@v")
                    p.req_ready("tensor", f"XS{xpar}")
                    p.req_ready("tensor", f"XS{xpar}@v")
                    for bh in range(16):
                        q4, qi = bh // 4, bh % 4
                        if qi == 0:
                            p.req_free("tensor", f"bk{gA + q4}")
                        p.op("tensor", (lambda t, bh=bh, q4=q4, qi=qi, gA=gA, xp=xpar, uc=ucur: t.matmul(
                            banks[gA + q4][:, qi * 128:(qi + 1) * 128], identb,
                            Ub[uc][:, bh * 128:(bh + 1) * 128],
                            start=True, stop=False)), inc="t")
                        p.op("tensor", (lambda t, bh=bh, q4=q4, qi=qi, gA=gA, xp=xpar, uc=ucur: t.matmul(
                            banks[gA + q4][:, qi * 128:(qi + 1) * 128],
                            XS[xp][:, bh * 128:(bh + 1) * 128],
                            Ub[uc][:, bh * 128:(bh + 1) * 128],
                            start=False, stop=True)), inc="t")
                        if qi == 3:
                            p.set_ready(f"bk{gA + q4}", "t")
                    for q4 in range(4):
                        eng = "scalar" if q4 % 2 == 0 else "vector"
                        es = "s" if q4 % 2 == 0 else "v"
                        p.req_ready(eng, f"bk{gA + q4}")
                        p.op(eng, (lambda e, q4=q4, gA=gA, uc=ucur: (
                            e.copy(Ub[1 - uc][:, q4 * 512:(q4 + 1) * 512], banks[gA + q4])
                            if hasattr(e, "copy") else
                            e.tensor_copy(Ub[1 - uc][:, q4 * 512:(q4 + 1) * 512], banks[gA + q4]))), inc=es)
                        p.set_free(f"bk{gA + q4}", es)
                    p.set_ready(f"U{1 - ucur}", "s")
                    p.set_ready(f"U{1 - ucur}@v", "v")
                    ucur = 1 - ucur
                    if j < 6:
                        p.req_ready("tensor", f"XTS{xpar}")
                        p.req_ready("tensor", f"XTS{xpar}@v")
                        for bh in range(16):
                            q4, qi = bh // 4, bh % 4
                            if qi == 0:
                                p.req_free("tensor", f"bk{gB + q4}")
                            p.op("tensor", (lambda t, bh=bh, q4=q4, qi=qi, gB=gB, xp=xpar: t.matmul(
                                banks[gB + q4][:, qi * 128:(qi + 1) * 128],
                                XTS[xp][:, bh * 128:(bh + 1) * 128],
                                XS[xp][:, bh * 128:(bh + 1) * 128],
                                start=True, stop=True)), inc="t")
                            if qi == 3:
                                p.set_ready(f"bk{gB + q4}", "t")
                        for q4 in range(4):
                            eng = "scalar" if q4 % 2 == 1 else "vector"
                            es = "s" if q4 % 2 == 1 else "v"
                            p.req_ready(eng, f"bk{gB + q4}")
                            p.req_free(eng, f"XS{1 - xpar}w{q4}" if False else f"XS{1 - xpar}")
                            p.op(eng, (lambda e, q4=q4, gB=gB, xp=xpar: (
                                e.copy(XS[1 - xp][:, q4 * 512:(q4 + 1) * 512], banks[gB + q4])
                                if hasattr(e, "copy") else
                                e.tensor_copy(XS[1 - xp][:, q4 * 512:(q4 + 1) * 512], banks[gB + q4]))), inc=es)
                            p.set_free(f"bk{gB + q4}", es)
                        p.set_ready(f"XS{1 - xpar}", "s")
                        p.set_ready(f"XS{1 - xpar}@v", "v")
                        for bh in range(16):
                            q4, qi = bh // 4, bh % 4
                            if qi == 0:
                                p.req_free("tensor", f"bk{gA + q4}")
                            p.op("tensor", (lambda t, bh=bh, q4=q4, qi=qi, gA=gA, xp=xpar: t.matmul(
                                banks[gA + q4][:, qi * 128:(qi + 1) * 128],
                                XS[xp][:, bh * 128:(bh + 1) * 128],
                                XTS[xp][:, bh * 128:(bh + 1) * 128],
                                start=True, stop=True)), inc="t")
                            if qi == 3:
                                p.set_ready(f"bk{gA + q4}", "t")
                        for q4 in range(4):
                            eng = "scalar" if q4 % 2 == 0 else "vector"
                            es = "s" if q4 % 2 == 0 else "v"
                            p.req_ready(eng, f"bk{gA + q4}")
                            p.req_free(eng, f"XTS{1 - xpar}")
                            p.op(eng, (lambda e, q4=q4, gA=gA, xp=xpar: (
                                e.copy(XTS[1 - xp][:, q4 * 512:(q4 + 1) * 512], banks[gA + q4])
                                if hasattr(e, "copy") else
                                e.tensor_copy(XTS[1 - xp][:, q4 * 512:(q4 + 1) * 512], banks[gA + q4]))), inc=es)
                            p.set_free(f"bk{gA + q4}", es)
                        p.set_ready(f"XTS{1 - xpar}", "s")
                        p.set_ready(f"XTS{1 - xpar}@v", "v")
                        p.set_free(f"XS{xpar}", "t")
                        p.set_free(f"XTS{xpar}", "t")
                        xpar = 1 - xpar
                if B_STAGE < 8:
                    continue
                p.req_ready("tensor", "qtb")
                for bh in range(16):
                    g, h2, rs, csl = bh_slices(bh)
                    q4, qi = bh // 4, bh % 4
                    gb = g * 128
                    if qi == 0:
                        p.req_free("tensor", f"bk{q4}")
                    p.op("tensor", (lambda t, rs=rs, gb=gb, q4=q4, qi=qi: t.matmul(
                        banks[q4][0:64, qi * 128:(qi + 1) * 128],
                        abuf[rs, gb:gb + 64], qtb[rs, gb:gb + 128],
                        start=True, stop=True)), inc="t")
                for bh in range(16):
                    g, h2, rs, csl = bh_slices(bh)
                    q4, qi = bh // 4, bh % 4
                    gb = g * 128
                    p.op("tensor", (lambda t, rs=rs, gb=gb, q4=q4, qi=qi: t.matmul(
                        banks[q4][64:128, qi * 128 + 64:(qi + 1) * 128],
                        abuf[rs, gb + 64:gb + 128], qtb[rs, gb + 64:gb + 128],
                        start=True, stop=True)), inc="t")
                    if qi == 3:
                        p.set_ready(f"bk{q4}", "t")
                for q4 in range(4):
                    p.req_ready("vector", f"bk{q4}")
                    p.req_free("vector", "g1t")
                    for qi in range(4):
                        p.op("vector", (lambda v, q4=q4, qi=qi: v.memset(
                            banks[q4][64:128, qi * 128:qi * 128 + 64], 0.0)), inc="v")
                    p.op("vector", (lambda v, q4=q4: v.memset(
                        g1t[:, q4 * 512:(q4 + 1) * 512], 0.0)), inc="v")
                    p.wait("vector", "v", p.val("v"))
                    p.op("vector", (lambda v, q4=q4: v.copy_predicated(
                        g1t[:, q4 * 512:(q4 + 1) * 512], masks4["mask_rt_incl"], banks[q4])), inc="v")
                    p.set_free(f"bk{q4}", "v")
                p.set_ready("g1t", "v")
                if B_STAGE < 9:
                    continue
                p.req_ready("vector", "qtb")
                p.req_free("vector", "qtz")
                p.op("vector", lambda v: v.tensor_copy(qtz[0:64, 0:1024], qtb[0:64, :]), inc="v")
                p.op("vector", lambda v: v.memset(qtz[64:128, 0:1024], 0.0), inc="v")
                p.op("vector", lambda v: v.memset(qtz[0:64, 1024:2048], 0.0), inc="v")
                p.op("vector", lambda v: v.tensor_copy(qtz[64:128, 1024:2048], qtb[64:128, :]), inc="v")
                p.set_ready("qtz", "v")
                p.req_ready("tensor", "qtz")
                p.req_ready("tensor", "g1t")
                p.req_ready("tensor", f"U{ucur}")
                p.req_ready("tensor", f"U{ucur}@v")
                for bh in range(16):
                    g, h2, rs, csl = bh_slices(bh)
                    q4, qi = bh // 4, bh % 4
                    if qi == 0:
                        p.req_free("tensor", f"bk{4 + q4}")
                    p.op("tensor", (lambda t, h2=h2, csl=csl, q4=q4, qi=qi: t.matmul(
                        banks[4 + q4][:, qi * 128:(qi + 1) * 128],
                        qtz[:, h2 * 1024 + csl.start:h2 * 1024 + csl.stop],
                        S0pb[:, csl], start=True, stop=False)), inc="t")
                    p.op("tensor", (lambda t, bh=bh, q4=q4, qi=qi, uc=ucur: t.matmul(
                        banks[4 + q4][:, qi * 128:(qi + 1) * 128],
                        g1t[:, bh * 128:(bh + 1) * 128],
                        Ub[uc][:, bh * 128:(bh + 1) * 128],
                        start=False, stop=True)), inc="t")
                    if qi == 3:
                        p.set_ready(f"bk{4 + q4}", "t")
                p.set_free("qtz", "t")
                for q4 in range(4):
                    p.req_ready("scalar", f"bk{4 + q4}")
                    p.req_free("scalar", "yf")
                    p.op("scalar", (lambda sc, q4=q4: sc.copy(
                        yf[:, q4 * 512:(q4 + 1) * 512], banks[4 + q4])), inc="s")
                    p.set_free(f"bk{4 + q4}", "s")
                p.set_ready("yf", "s")
                p.req_ready("scalar", "yf")
                for q4 in range(4):
                    p.req_free("scalar", "sqy")
                    p.op("scalar", (lambda sc, q4=q4: sc.square(
                        sqy, yf[:, q4 * 512:(q4 + 1) * 512])), inc="s")
                    p.set_ready("sqy", "s")
                    p.req_ready("vector", "sqy")
                    for qi in range(4):
                        p.op("vector", (lambda v, q4=q4, qi=qi: v.reduce_sum(
                            ysum[:, q4 * 4 + qi:q4 * 4 + qi + 1],
                            sqy[:, qi * 128:(qi + 1) * 128], axis=AX.X)), inc="v")
                    p.set_free("sqy", "v")
                p.set_ready("ysum", "v")
                p.req_ready("scalar", "ysum")
                p.op("scalar", lambda sc: sc.activation(yrst, ysum, AF.Ln,
                                                        bias=cst[:, CC_EPS6:CC_EPS6 + 1],
                                                        scale=cst[:, CC_I128:CC_I128 + 1]), inc="s")
                p.wait("scalar", "s", p.val("s"))
                p.op("scalar", lambda sc: sc.activation(yrst, yrst, AF.Exp,
                                                        scale=cst[:, CC_NHALF:CC_NHALF + 1]), inc="s")
                p.set_ready("yrst", "s")
                p.req_ready("vector", "yrst")
                p.req_free("vector", "ynb")
                for bh in range(16):
                    g, h2, rs, csl = bh_slices(bh)
                    vi = g * 2 + h2
                    p.op("vector", (lambda v, bh=bh, vi=vi: v.scalar_tensor_tensor(
                        ynb[:, vi * 128:(vi + 1) * 128], yf[:, bh * 128:(bh + 1) * 128],
                        yrst[:, bh:bh + 1], whnbhB[:, vi * 128:(vi + 1) * 128],
                        ALU.mult, ALU.mult)), inc="v")
                p.set_free("yf", "v")
                p.set_ready("ynb", "v")
                p.req_ready("gpsimd", "ynb")
                c0 = ci * C
                for hp in range(4):
                    for b_ in range(2):
                        g_ = hp * 2 + b_
                        p.dma("gpsimd",
                              y_sp[b_, 2 * hp:2 * hp + 2, c0:c0 + C, :].rearrange("h2 t dv -> t h2 dv"),
                              ynb[:, g_ * 256:(g_ + 1) * 256].rearrange("t (h2 dv) -> t h2 dv", h2=2),
                              "dynb")
                p.set_free("ynb", "dynb")
                if B_STAGE < 10:
                    continue
                p.req_ready("tensor", "aTb")
                for bh in range(16):
                    g, h2, rs, csl = bh_slices(bh)
                    bkix = g // 4
                    if bh == 0 or bh == 4:
                        p.req_free("tensor", f"bk{bkix}")
                    p.op("tensor", (lambda t, g=g, h2=h2, bh=bh, bkix=bkix, uc=ucur: t.matmul(
                        banks[bkix][h2 * 64:(h2 + 1) * 64, (g % 4) * 128:((g % 4) + 1) * 128],
                        aTb[:, g * 128 + h2 * 64: g * 128 + h2 * 64 + 64],
                        Ub[uc][:, bh * 128:(bh + 1) * 128],
                        start=True, stop=True)), inc="t")
                    if bh == 11 or bh == 15:
                        p.set_ready(f"bk{bkix}", "t")
                p.req_ready("vector", "rhog")
                for g in range(8):
                    bkix = g // 4
                    if g % 4 == 0:
                        p.req_ready("vector", f"bk{bkix}")
                    p.op("vector", (lambda v, g=g, bkix=bkix, sc_=sc_, sn=sn: v.scalar_tensor_tensor(
                        sn[:, g * 128:(g + 1) * 128], sc_[:, g * 128:(g + 1) * 128],
                        rhog[:, g:g + 1], banks[bkix][:, (g % 4) * 128:((g % 4) + 1) * 128],
                        ALU.mult, ALU.add)), inc="v")
                    if g % 4 == 3:
                        p.set_free(f"bk{bkix}", "v")
                p.set_ready("Sb", "v")
                if ci == 0 and "bdbg" in TEST_TAPS:
                    dumps = [(spk, "v"), (alf2, "v"), (Graw, "v"), (Gi, "s"),
                             (brow, "v"), (bcolT, "s"), (abuf, "v"), (qtb, "v"),
                             (bhn, "v"), (Ub[ucur], "s"), (XS[0], "v"), (XTS[0], "v"),
                             (g1t, "v"), (S0pb, "v"), (aTb, "s"), (yf, "s"),
                             (ynb, "v"), (yrst, "s"), (rhot, "v"), (rhog, "v"),
                             (Sb[1 - pb], "v"), (vtm[pb], f"dvtm{pb}"), (abcast, "s"), (bbcn, "s")]
                    for di, (tile, dep) in enumerate(dumps):
                        if dep in ("v", "s", "t"):
                            p.wait("gpsimd", dep, p.val(dep))
                        else:
                            p.wait("gpsimd", dep)
                        sh = tile.shape
                        p.dma("gpsimd", bdbg[di, 0:sh[0], 0:sh[1]], tile, "st")
                p.set_free("spk", "t")
                p.set_free("spk", "v")
                for nm in ("abuf", "qtb", "bhn", "ahfw", "S0pb", "g1t"):
                    p.set_free(nm, "t")
                p.set_free("aTb", "t")

        bctx.close()
        barrier()

        CT = 512
        cctx = ExitStack()
        if "C" in PHASES:
            sbC = lambda n, sh, dt=F32: cctx.enter_context(nc.sbuf_tensor(n, sh, dt, side="right"))[:]
            wu1B = sbC("wu1B", [128, 8 * R], BF16)
            wu2B = sbC("wu2B", [R, D], BF16)
            woutB = sbC("woutB", [128, 8 * 1024], BF16)
            wgB = sbC("wgB", [128, 8 * 8], BF16)
            for ks in range(8):
                p.dma("gpsimd", wu1B[:, ks * R:(ks + 1) * R], w["w_u1"][ks * 128:(ks + 1) * 128, :], "qW")
                p.dma("gpsimd", woutB[:, ks * 1024:(ks + 1) * 1024], w["w_out"][ks * 128:(ks + 1) * 128, :], "qW")
                p.dma("gpsimd", wgB[:, ks * 8:(ks + 1) * 8], w["moe_Wg"][ks * 128:(ks + 1) * 128, :], "qW")
            p.dma("gpsimd", wu2B, w["w_u2"], "qW")
            for eng in ("tensor", "gpsimd"):
                p.wait(eng, "qW")
            p.wait("gpsimd", "c0")
            p.wait("gpsimd", "c1")
            p.wait("sync", "c0")
            p.wait("sync", "c1")

            xc = sbC("xc", [128, 8 * CT])
            xcb = [sbC(f"xcb{i}", [128, 8 * CT], BF16) for i in range(2)]
            yc = [sbC(f"yc{i}", [128, 8 * CT], BF16) for i in range(2)]
            pan = [sbC(f"pan{i}", [128, 12288], BF16) for i in range(2)]
            f13p = [sbC(f"f13p{i}", [128, 2048], BF16) for i in range(2)]
            f1c = sbC("f1c", [128, 2048], BF16)
            g1c = sbC("g1c", [R, CT], BF16)
            gatec = sbC("gatec", [128, 8 * CT])
            y2c = sbC("y2c", [128, 8 * CT])
            h2c = sbC("h2c", [128, 8 * CT], BF16)
            sq1 = sbC("sq1", [128, CT])
            rowc = sbC("rowc", [1, CT])
            rowc2 = sbC("rowc2", [1, CT])
            rowmu = sbC("rowmu", [1, CT])
            z1cb = sbC("z1cb", [128, 8 * CT], BF16)
            lgc = sbC("lgc", [8, CT])
            lgt = sbC("lgt", [128, 8])
            lgt2 = sbC("lgt2", [128, 8])
            prb = sbC("prb", [128, 8])
            rowm = sbC("rowm", [128, 1])
            rowsum = sbC("rowsum", [128, 1])
            rrec = sbC("rrec", [128, 1])
            coefc = sbC("coefc", [128, 32])
            coefT = sbC("coefT", [8, CT], BF16)
            z1c = xc

            ff1r = ff1b.rearrange("(ks p) f -> p ks f", p=128)
            ff3r = ff3b.rearrange("(ks p) f -> p ks f", p=128)
            ff2r = ff2b.rearrange("(fs p) m -> p fs m", p=128)
            panr1 = [pan[i][:, 0:4096].rearrange("p (ks c) -> p ks c", ks=8) for i in range(2)]
            panr3 = [pan[i][:, 4096:8192].rearrange("p (ks c) -> p ks c", ks=8) for i in range(2)]
            panr2 = [pan[i][:, 8192:12288].rearrange("p (ks2 m) -> p ks2 m", ks2=4) for i in range(2)]

            def pan_load(fp):
                fpb = fp % 2
                p.req_free("sync", f"pan{fpb}")
                for kh in range(2):
                    p.dma("sync", panr1[fpb][:, kh * 4:(kh + 1) * 4, :],
                          ff1r[:, kh * 4:(kh + 1) * 4, fp * 512:(fp + 1) * 512], f"dpan{fpb}")
                    p.dma("sync", panr3[fpb][:, kh * 4:(kh + 1) * 4, :],
                          ff3r[:, kh * 4:(kh + 1) * 4, fp * 512:(fp + 1) * 512], f"dpan{fpb}")
                p.dma("sync", panr2[fpb], ff2r[:, fp * 4:(fp + 1) * 4, :], f"dpan{fpb}")
                p.set_ready(f"pan{fpb}", f"dpan{fpb}")

            for b in range(B):
                for ti in range(T // CT):
                    tix = b * (T // CT) + ti
                    if ti >= C_LIMIT:
                        continue
                    pb = tix % 2
                    t0 = ti * CT
                    p.req_free("sync", "xc")
                    for s_ in range(8):
                        p.dma("sync", xc[:, s_ * CT:(s_ + 1) * CT],
                              x_fm[b, s_, :, t0:t0 + CT], "dx0")
                    p.set_ready("xc", "dx0")
                    p.req_free("gpsimd", f"xcb{pb}")
                    for s_ in range(8):
                        p.dma("gpsimd", xcb[pb][:, s_ * CT:(s_ + 1) * CT],
                              x_fm[b, s_, :, t0:t0 + CT], f"dxcb{pb}")
                    p.set_ready(f"xcb{pb}", f"dxcb{pb}")
                    p.req_free("sync", f"yc{pb}")
                    for h_ in range(8):
                        p.dma("sync", yc[pb][:, h_ * CT:(h_ + 1) * CT],
                              y_sp[b, h_, t0:t0 + CT, :], f"dyc{pb}", transpose=True)
                    p.set_ready(f"yc{pb}", f"dyc{pb}")
                    pan_load(0)
                    pan_load(1)
                    if C_STAGE < 1:
                        continue
                    p.req_ready("tensor", f"xcb{pb}")
                    p.req_free("tensor", "bk6")
                    for ks in range(8):
                        p.op("tensor", (lambda t, ks=ks, pb=pb: t.matmul(
                            banks[6][0:R, :], wu1B[:, ks * R:(ks + 1) * R],
                            xcb[pb][:, ks * CT:(ks + 1) * CT],
                            start=(ks == 0), stop=(ks == 7))), inc="t")
                    p.set_free(f"xcb{pb}", "t")
                    p.set_ready("bk6", "t")
                    p.req_ready("scalar", "bk6")
                    p.req_free("scalar", "g1c")
                    p.op("scalar", lambda sc: sc.activation(g1c, banks[6][0:R, :],
                                                            AF.Silu, bias=bu1B), inc="s")
                    p.set_free("bk6", "s")
                    p.set_ready("g1c", "s")
                    p.req_ready("tensor", "g1c")
                    p.req_free("scalar", "gatec")
                    for m in range(8):
                        bki = 6 + (m % 2)
                        p.req_free("tensor", f"bk{bki}")
                        p.op("tensor", (lambda t, m=m, bki=bki: t.matmul(
                            banks[bki], wu2B[:, m * 128:(m + 1) * 128], g1c,
                            start=True, stop=True)), inc="t")
                        mv = p.val("t")
                        p.wait("scalar", "t", mv)
                        p.op("scalar", (lambda sc, m=m, bki=bki: sc.activation(
                            gatec[:, m * CT:(m + 1) * CT], banks[bki],
                            AF.Sigmoid, bias=bu2B[:, m:m + 1])), inc="s")
                        p.set_free(f"bk{bki}", "s")
                    p.set_free("g1c", "t")
                    p.set_ready("gatec", "s")
                    if C_STAGE < 2:
                        continue
                    p.req_ready("tensor", f"yc{pb}")
                    p.req_ready("vector", "gatec")
                    p.req_ready("vector", "xc")
                    p.req_free("vector", "y2c")
                    for m in range(8):
                        bki = 6 + (m % 2)
                        p.req_free("tensor", f"bk{bki}")
                        for ks in range(8):
                            p.op("tensor", (lambda t, m=m, ks=ks, pb=pb, bki=bki: t.matmul(
                                banks[bki],
                                woutB[:, ks * 1024 + m * 128: ks * 1024 + (m + 1) * 128],
                                yc[pb][:, ks * CT:(ks + 1) * CT],
                                start=(ks == 0), stop=(ks == 7))), inc="t")
                        p.set_ready(f"bk{bki}", "t")
                        p.req_ready("vector", f"bk{bki}")
                        p.op("vector", (lambda v, m=m, bki=bki: v.tensor_mul(
                            y2c[:, m * CT:(m + 1) * CT], banks[bki],
                            gatec[:, m * CT:(m + 1) * CT])), inc="v")
                        p.wait("vector", "v", p.val("v"))
                        p.op("vector", (lambda v, m=m: v.tensor_add(
                            y2c[:, m * CT:(m + 1) * CT], y2c[:, m * CT:(m + 1) * CT],
                            xc[:, m * CT:(m + 1) * CT])), inc="v")
                        p.wait("vector", "v", p.val("v"))
                        p.set_free(f"bk{bki}", "v")
                    p.set_ready("y2c", "v")
                    p.set_free("gatec", "v")
                    p.set_free(f"yc{pb}", "t")
                    if C_STAGE < 3:
                        continue
                    p.req_ready("vector", "y2c")
                    p.req_free("tensor", "bk6")
                    for s_ in range(8):
                        p.req_free("vector", "sq1")
                        p.op("vector", (lambda v, s=s_: v.tensor_mul(
                            sq1, y2c[:, s * CT:(s + 1) * CT], y2c[:, s * CT:(s + 1) * CT])), inc="v")
                        p.set_ready("sq1", "v")
                        p.req_ready("tensor", "sq1")
                        p.op("tensor", (lambda t, s=s_: t.matmul(
                            banks[6][0:1, :], onescolf, sq1,
                            start=(s == 0), stop=(s == 7))), inc="t")
                        p.set_free("sq1", "t")
                    p.set_ready("bk6", "t")
                    p.req_ready("scalar", "bk6")
                    p.req_free("scalar", "rowc")
                    p.op("scalar", lambda sc: sc.activation(
                        rowc, banks[6][0:1, :], AF.Ln, bias=cst[0:1, CC_EPS6:CC_EPS6 + 1],
                        scale=cst[0:1, CC_I1024:CC_I1024 + 1]), inc="s")
                    p.set_free("bk6", "s")
                    p.wait("scalar", "s", p.val("s"))
                    p.op("scalar", lambda sc: sc.activation(
                        rowc, rowc, AF.Exp, scale=cst[0:1, CC_NHALF:CC_NHALF + 1]), inc="s")
                    p.set_ready("rowc", "s")
                    p.req_ready("tensor", "rowc")
                    p.req_free("tensor", "bk7")
                    p.op("tensor", lambda t: t.matmul(banks[7], ones1f, rowc,
                                                      start=True, stop=True), inc="t")
                    p.set_free("rowc", "t")
                    p.set_ready("bk7", "t")
                    p.req_ready("vector", "bk7")
                    p.req_free("vector", "h2c")
                    for s_ in range(8):
                        p.op("vector", (lambda v, s=s_: v.scalar_tensor_tensor(
                            h2c[:, s * CT:(s + 1) * CT], banks[7], wffnB[:, s:s + 1],
                            y2c[:, s * CT:(s + 1) * CT], ALU.mult, ALU.mult)), inc="v")
                    p.set_free("bk7", "v")
                    p.set_ready("h2c", "v")
                    if C_STAGE < 4:
                        continue
                    p.req_ready("tensor", "h2c")
                    for fp in range(8):
                        fpb = fp % 2
                        if fp >= 2:
                            pan_load(fp)
                        p.req_ready("tensor", f"pan{fpb}")
                        for sl in range(4):
                            bki = 4 + (sl % 2)
                            p.req_free("tensor", f"bk{bki}")
                            for ks in range(8):
                                p.op("tensor", (lambda t, sl=sl, ks=ks, fpb=fpb, bki=bki: t.matmul(
                                    banks[bki],
                                    pan[fpb][:, ks * 512 + sl * 128: ks * 512 + (sl + 1) * 128],
                                    h2c[:, ks * CT:(ks + 1) * CT],
                                    start=(ks == 0), stop=(ks == 7))), inc="t")
                            p.set_ready(f"bk{bki}", "t")
                            p.req_ready("scalar", f"bk{bki}")
                            if sl == 0:
                                p.req_free("scalar", "f1c")
                            p.op("scalar", (lambda sc, sl=sl, bki=bki: sc.activation(
                                f1c[:, sl * 512:(sl + 1) * 512], banks[bki], AF.Silu)), inc="s")
                            p.set_free(f"bk{bki}", "s")
                        p.set_ready("f1c", "s")
                        p.req_free("vector", f"f13p{fpb}")
                        p.req_ready("vector", "f1c")
                        for sl in range(4):
                            bki = 6 + (sl % 2)
                            p.req_free("tensor", f"bk{bki}")
                            for ks in range(8):
                                p.op("tensor", (lambda t, sl=sl, ks=ks, fpb=fpb, bki=bki: t.matmul(
                                    banks[bki],
                                    pan[fpb][:, 4096 + ks * 512 + sl * 128: 4096 + ks * 512 + (sl + 1) * 128],
                                    h2c[:, ks * CT:(ks + 1) * CT],
                                    start=(ks == 0), stop=(ks == 7))), inc="t")
                            p.set_ready(f"bk{bki}", "t")
                            p.req_ready("vector", f"bk{bki}")
                            p.op("vector", (lambda v, sl=sl, fpb=fpb, bki=bki: v.tensor_mul(
                                f13p[fpb][:, sl * 512:(sl + 1) * 512], banks[bki],
                                f1c[:, sl * 512:(sl + 1) * 512])), inc="v")
                            p.set_free(f"bk{bki}", "v")
                        p.set_free("f1c", "v")
                        p.set_ready(f"f13p{fpb}", "v")
                        p.req_ready("tensor", f"f13p{fpb}")
                        for m in range(8):
                            bki = m % 4
                            p.req_free("tensor", f"bk{bki}")
                            for ks2 in range(4):
                                p.op("tensor", (lambda t, m=m, ks2=ks2, fpb=fpb, bki=bki: t.matmul(
                                    banks[bki],
                                    pan[fpb][:, 8192 + ks2 * 1024 + m * 128: 8192 + ks2 * 1024 + (m + 1) * 128],
                                    f13p[fpb][:, ks2 * 512:(ks2 + 1) * 512],
                                    start=(ks2 == 0), stop=(ks2 == 3))), inc="t")
                            p.set_ready(f"bk{bki}", "t")
                            p.req_ready("vector", f"bk{bki}")
                            p.op("vector", (lambda v, m=m, bki=bki: v.tensor_add(
                                y2c[:, m * CT:(m + 1) * CT], banks[bki],
                                y2c[:, m * CT:(m + 1) * CT])), inc="v")
                            p.set_free(f"bk{bki}", "v")
                        p.wait("vector", "v", p.val("v"))
                        p.set_free(f"f13p{fpb}", "t")
                        p.set_free(f"pan{fpb}", "t")
                    p.set_free("h2c", "t")
                    for m in range(8):
                        p.op("vector", (lambda v, m=m: v.tensor_add(
                            y2c[:, m * CT:(m + 1) * CT], y2c[:, m * CT:(m + 1) * CT],
                            xc[:, m * CT:(m + 1) * CT])), inc="v")
                    p.wait("vector", "v", p.val("v"))
                    p.set_ready("zpre", "v")
                    if C_STAGE < 5:
                        continue
                    p.req_ready("tensor", "zpre")
                    p.req_free("tensor", "bk6")
                    for s_ in range(8):
                        p.op("tensor", (lambda t, s=s_: t.matmul(
                            banks[6][0:1, :], onescolf, y2c[:, s * CT:(s + 1) * CT],
                            start=(s == 0), stop=(s == 7))), inc="t")
                    p.set_ready("bk6mu", "t")
                    p.req_free("tensor", "bk7")
                    for s_ in range(8):
                        p.req_free("vector", "sq1")
                        p.op("vector", (lambda v, s=s_: v.tensor_mul(
                            sq1, y2c[:, s * CT:(s + 1) * CT], y2c[:, s * CT:(s + 1) * CT])), inc="v")
                        p.set_ready("sq1", "v")
                        p.req_ready("tensor", "sq1")
                        p.op("tensor", (lambda t, s=s_: t.matmul(
                            banks[7][0:1, :], onescolf, sq1,
                            start=(s == 0), stop=(s == 7))), inc="t")
                        p.set_free("sq1", "t")
                    p.set_ready("bk7s", "t")
                    p.req_ready("vector", "bk6mu")
                    p.req_ready("vector", "bk7s")
                    p.req_free("vector", "rowmu")
                    p.op("vector", lambda v: v.tensor_scalar(rowmu, banks[6][0:1, :],
                                                             1.0 / 1024.0, None, ALU.mult), inc="v")
                    p.wait("vector", "v", p.val("v"))
                    p.req_free("vector", "rowc2")
                    p.op("vector", lambda v: v.tensor_mul(rowc2, rowmu, rowmu), inc="v")
                    p.wait("vector", "v", p.val("v"))
                    p.op("vector", lambda v: v.scalar_tensor_tensor(
                        rowc2, banks[7][0:1, :], 1.0 / 1024.0, rowc2, ALU.mult,
                        ALU.subtract), inc="v")
                    p.wait("vector", "v", p.val("v"))
                    p.set_free("bk6", "v")
                    p.set_free("bk7", "v")
                    p.set_ready("var", "v")
                    p.req_ready("scalar", "var")
                    p.op("scalar", lambda sc: sc.activation(rowc2, rowc2, AF.Ln,
                                                            bias=cst[0:1, CC_EPS5:CC_EPS5 + 1]), inc="s")
                    p.wait("scalar", "s", p.val("s"))
                    p.op("scalar", lambda sc: sc.activation(rowc2, rowc2, AF.Exp,
                                                            scale=cst[0:1, CC_NHALF:CC_NHALF + 1]), inc="s")
                    p.set_ready("rstd1", "s")
                    p.req_ready("vector", "rstd1")
                    p.op("vector", lambda v: v.tensor_mul(rowmu, rowmu, rowc2), inc="v")
                    p.set_ready("murstd", "v")
                    p.req_ready("tensor", "murstd")
                    p.req_free("tensor", "bk6")
                    p.req_free("tensor", "bk7")
                    p.op("tensor", lambda t: t.matmul(banks[6], ones1f, rowc2,
                                                      start=True, stop=True), inc="t")
                    p.op("tensor", lambda t: t.matmul(banks[7], ones1f, rowmu,
                                                      start=True, stop=True), inc="t")
                    p.set_ready("bk6", "t")
                    p.set_ready("bk7", "t")
                    p.req_ready("vector", "bk6")
                    p.req_ready("vector", "bk7")
                    p.req_free("vector", "xc")
                    for s_ in range(8):
                        p.op("vector", (lambda v, s=s_: v.tensor_mul(
                            z1c[:, s * CT:(s + 1) * CT], y2c[:, s * CT:(s + 1) * CT],
                            banks[6])), inc="v")
                        p.wait("vector", "v", p.val("v"))
                        p.op("vector", (lambda v, s=s_: v.tensor_sub(
                            z1c[:, s * CT:(s + 1) * CT], z1c[:, s * CT:(s + 1) * CT],
                            banks[7])), inc="v")
                        p.wait("vector", "v", p.val("v"))
                    p.set_free("bk6", "v")
                    p.set_free("bk7", "v")
                    p.set_free("y2c", "v")
                    p.set_ready("z1c", "v")
                    p.req_ready("gpsimd", "z1c")
                    for s_ in range(8):
                        p.dma("gpsimd", z1_sp[b, s_, :, t0:t0 + CT],
                              z1c[:, s_ * CT:(s_ + 1) * CT], "dxf")
                    if C_STAGE < 6:
                        p.set_free("xc", "dxf")
                        continue
                    p.req_free("vector", "z1cb")
                    for s_ in range(8):
                        p.op("vector", (lambda v, s=s_: v.tensor_copy(
                            z1cb[:, s * CT:(s + 1) * CT], z1c[:, s * CT:(s + 1) * CT])), inc="v")
                    p.set_ready("z1cb", "v")
                    p.set_free("xc", "dxf")
                    p.set_free("xc", "v")
                    p.req_ready("tensor", "z1cb")
                    p.req_free("tensor", "bk7")
                    for ks in range(8):
                        p.op("tensor", (lambda t, ks=ks: t.matmul(
                            banks[7][0:8, :], wgB[:, ks * 8:(ks + 1) * 8],
                            z1cb[:, ks * CT:(ks + 1) * CT],
                            start=(ks == 0), stop=(ks == 7))), inc="t")
                    p.set_free("z1cb", "t")
                    p.set_ready("bk7", "t")
                    p.req_ready("scalar", "bk7")
                    p.req_free("scalar", "lgc")
                    p.op("scalar", lambda sc: sc.activation(lgc, banks[7][0:8, :],
                                                            AF.Identity, bias=bgB), inc="s")
                    p.set_free("bk7", "s")
                    p.set_ready("lgc", "s")
                    for sub in range(4):
                        p.req_ready("tensor", "lgc")
                        p.req_free("tensor", "bk7")
                        p.op("tensor", (lambda t, sub=sub: t.transpose(
                            banks[7][:, 0:8], lgc[0:8, sub * 128:(sub + 1) * 128],
                            identf[0:8, 0:8])), inc="t")
                        p.set_ready("bk7", "t")
                        p.req_ready("vector", "bk7")
                        p.req_free("vector", "lgt")
                        p.op("vector", (lambda v: v.tensor_copy(lgt, banks[7][:, 0:8])), inc="v")
                        p.set_free("bk7", "v")
                        p.wait("vector", "v", p.val("v"))
                        p.op("vector", lambda v: v.reduce_max(rowm, lgt, axis=AX.X), inc="v")
                        p.wait("vector", "v", p.val("v"))
                        p.op("vector", lambda v: v.tensor_scalar(rowm, rowm, -1.0, None, ALU.mult), inc="v")
                        p.set_ready("negm", "v")
                        p.req_ready("scalar", "negm")
                        p.op("scalar", lambda sc: sc.activation(lgt2, lgt, AF.Exp,
                                                                bias=rowm[:, 0:1]), inc="s")
                        p.set_ready("expd", "s")
                        p.req_ready("vector", "expd")
                        p.op("vector", lambda v: v.reduce_sum(rowsum, lgt2, axis=AX.X), inc="v")
                        p.wait("vector", "v", p.val("v"))
                        p.op("vector", lambda v: v.reciprocal(rrec, rowsum), inc="v")
                        p.wait("vector", "v", p.val("v"))
                        p.op("vector", lambda v: v.tensor_scalar(prb, lgt2, rrec[:, 0:1], None, ALU.mult), inc="v")
                        p.wait("vector", "v", p.val("v"))
                        p.op("vector", lambda v: v.reduce_max(rowm, prb, axis=AX.X), inc="v")
                        p.wait("vector", "v", p.val("v"))
                        p.op("vector", lambda v: v.tensor_scalar(lgt2, prb, rowm[:, 0:1], None, ALU.is_ge), inc="v")
                        p.wait("vector", "v", p.val("v"))
                        if sub == 0:
                            p.req_free("vector", "coefc")
                        p.op("vector", (lambda v, sub=sub: v.tensor_mul(
                            coefc[:, sub * 8:(sub + 1) * 8], lgt2, prb)), inc="v")
                        p.wait("vector", "v", p.val("v"))
                        p.set_free("lgt", "v")
                    p.set_free("lgc", "t")
                    p.set_ready("coefc", "v")
                    p.set_free("coefc", "t")
                    for sub in range(4):
                        p.req_ready("tensor", "coefc")
                        p.req_free("tensor", "bk7")
                        p.op("tensor", (lambda t, sub=sub: t.transpose(
                            banks[7][0:8, 0:128], coefc[:, sub * 8:(sub + 1) * 8],
                            identf)), inc="t")
                        p.set_ready("bk7", "t")
                        p.req_ready("scalar", "bk7")
                        p.req_free("scalar", "coefT") if sub == 0 else None
                        p.op("scalar", (lambda sc, sub=sub: sc.copy(
                            coefT[:, sub * 128:(sub + 1) * 128], banks[7][0:8, 0:128])), inc="s")
                        p.set_free("bk7", "s")
                    p.set_ready("coefT", "s")
                    p.req_ready("gpsimd", "coefT")
                    p.dma("gpsimd", coef_sp[b, :, t0:t0 + CT], coefT, "dbbst")
                    p.set_free("coefT", "dbbst")
        cctx.close()
        barrier()

        dctx = ExitStack()
        if "D" in PHASES:
            sbD = lambda n, sh, dt=F32: dctx.enter_context(nc.sbuf_tensor(n, sh, dt, side="right"))[:]
            Wb = [sbD(f"Wb{i}", [128, 8192], BF16) for i in range(2)]
            z1b = [sbD(f"z1b{i}", [128, 8 * 1024], BF16) for i in range(2)]
            zacc = sbD("zacc", [128, 8 * 1024])
            hsD = sbD("hsD", [128, 8 * 512], BF16)
            htmp = sbD("htmp", [128, 512], BF16)
            coefD1 = [sbD(f"coefD1{i}", [1, 8 * 1024], BF16) for i in range(2)]
            coefD8 = [sbD(f"coefD8{i}", [8, 1024], BF16) for i in range(2)]
            z1fD = sbD("z1fD", [128, 8 * 512])
            z2D = z1fD
            outst = sbD("outst", [128, 1024])
            sqD = sbD("sqD", [128, 512])
            rowD = sbD("rowD", [1, 512])
            rowD2 = sbD("rowD2", [1, 512])
            rowDmu = sbD("rowDmu", [1, 512])

            def wload(unit):
                ui = unit % 2
                rem = unit % 32
                e_, half, kind = rem // 4, (rem // 2) % 2, rem % 2
                p.req_free("gpsimd", f"Wb{ui}")
                if kind == 0:
                    for ks in range(8):
                        p.dma("gpsimd", Wb[ui][:, ks * 1024:(ks + 1) * 1024],
                              w1b[e_][ks * 128:(ks + 1) * 128, half * 1024:(half + 1) * 1024],
                              f"devb{ui}")
                else:
                    for ks in range(8):
                        p.dma("gpsimd", Wb[ui][:, ks * 1024:(ks + 1) * 1024],
                              w2b[e_][half * 1024 + ks * 128:half * 1024 + (ks + 1) * 128, :],
                              f"devb{ui}")
                p.set_ready(f"Wb{ui}", f"devb{ui}")

            NDT = T // 1024
            nunit = [0]
            wload(0)
            wload(1)
            for b in range(B):
                for td in range(NDT):
                    tl = b * NDT + td
                    if td >= D_LIMIT:
                        continue
                    pb = tl % 2
                    t0 = td * 1024
                    p.req_free("gpsimd", f"z1b{pb}")
                    for s_ in range(8):
                        p.dma("gpsimd", z1b[pb][:, s_ * 1024:(s_ + 1) * 1024],
                              z1_sp[b, s_, :, t0:t0 + 1024], f"dqc{pb}")
                    p.set_ready(f"z1b{pb}", f"dqc{pb}")
                    p.req_free("sync", f"coefD{pb}")
                    p.dma("sync", coefD1[pb][0:1, :].rearrange("o (e t) -> o e t", e=8, t=1024),
                          coef_sp[b:b + 1, :, t0:t0 + 1024], f"dcf{pb}")
                    p.dma("sync", coefD8[pb], coef_sp[b, :, t0:t0 + 1024], f"dcf{pb}")
                    p.set_ready(f"coefD{pb}", f"dcf{pb}")
                    p.req_free("vector", "zacc")
                    for s_ in range(8):
                        p.op("vector", (lambda v, s=s_: v.memset(zacc[:, s * 1024:(s + 1) * 1024], 0.0)), inc="v")
                    p.wait("vector", "v", p.val("v"))
                    p.set_ready("zacc0", "v")
                    for e_ in range(E):
                        for half in range(2):
                            u1 = tl * 32 + e_ * 4 + half * 2
                            b1i = u1 % 2
                            b2i = (u1 + 1) % 2
                            for sub in range(2):
                                s0 = sub * 512
                                p.req_ready("tensor", f"coefD{pb}")
                                p.req_free("tensor", "bk6")
                                p.op("tensor", (lambda t, e_=e_, s0=s0, pb=pb: t.matmul(
                                    banks[6][:, 0:512], ones1b,
                                    coefD1[pb][0:1, e_ * 1024 + s0:e_ * 1024 + s0 + 512],
                                    start=True, stop=True)), inc="t")
                                p.set_ready("bk6", "t")
                                p.req_ready("tensor", f"Wb{b1i}")
                                p.req_ready("tensor", f"z1b{pb}")
                                for mo in range(8):
                                    bki = 4 + (mo % 2)
                                    p.req_free("tensor", f"bk{bki}")
                                    for ks in range(8):
                                        p.op("tensor", (lambda t, mo=mo, ks=ks, bki=bki, b1i=b1i, pb=pb, s0=s0: t.matmul(
                                            banks[bki][:, 0:512],
                                            Wb[b1i][:, ks * 1024 + mo * 128:ks * 1024 + (mo + 1) * 128],
                                            z1b[pb][:, ks * 1024 + s0:ks * 1024 + s0 + 512],
                                            start=(ks == 0), stop=(ks == 7))), inc="t")
                                    p.set_ready(f"bk{bki}", "t")
                                    p.req_ready("scalar", f"bk{bki}")
                                    p.req_free("scalar", "htmp")
                                    p.op("scalar", (lambda sc, mo=mo, bki=bki, e_=e_, half=half: sc.activation(
                                        htmp, banks[bki][:, 0:512], AF.Gelu,
                                        bias=b1B[:, e_ * 16 + half * 8 + mo:e_ * 16 + half * 8 + mo + 1])), inc="s")
                                    p.set_free(f"bk{bki}", "s")
                                    p.set_ready("htmp", "s")
                                    p.req_ready("vector", "htmp")
                                    p.req_ready("vector", "bk6")
                                    p.req_free("vector", "hsD_" + str(mo))
                                    p.op("vector", (lambda v, mo=mo: v.tensor_mul(
                                        hsD[:, mo * 512:(mo + 1) * 512], htmp, banks[6][:, 0:512])), inc="v")
                                    p.set_free("htmp", "v")
                                    p.set_ready("hsD_" + str(mo), "v")
                                p.set_free("bk6", "v")
                                p.req_ready("tensor", f"Wb{b2i}")
                                first_u = (e_ == 0 and half == 0)
                                last_u = (e_ == E - 1 and half == 1)
                                for mg in range(2):
                                    for m in range(4):
                                        mm_ = mg * 4 + m
                                        p.req_free("tensor", f"bkw{sub}_{mm_}" if False else f"bk{m}")
                                        for ks2 in range(8):
                                            p.req_ready("tensor", "hsD_" + str(ks2))
                                            p.op("tensor", (lambda t, mm_=mm_, ks2=ks2, m=m, b2i=b2i: t.matmul(
                                                banks[m][:, 0:512],
                                                Wb[b2i][:, ks2 * 1024 + mm_ * 128:ks2 * 1024 + (mm_ + 1) * 128],
                                                hsD[:, ks2 * 512:(ks2 + 1) * 512],
                                                start=(ks2 == 0), stop=(ks2 == 7))), inc="t")
                                        p.set_ready(f"bk{m}", "t")
                                        p.req_ready("vector", f"bk{m}")
                                        p.req_ready("vector", "zacc0")
                                        p.op("vector", (lambda v, mm_=mm_, m=m, s0=s0: v.tensor_add(
                                            zacc[:, mm_ * 1024 + s0:mm_ * 1024 + s0 + 512],
                                            zacc[:, mm_ * 1024 + s0:mm_ * 1024 + s0 + 512],
                                            banks[m][:, 0:512])), inc="v")
                                        p.wait("vector", "v", p.val("v"))
                                        p.set_free(f"bk{m}", "v")
                            p.set_free(f"Wb{b1i}", "t")
                            p.set_free(f"Wb{b2i}", "t")
                            if u1 + 2 < 32 * NDT * B:
                                wload(u1 + 2)
                            if u1 + 3 < 32 * NDT * B:
                                wload(u1 + 3)
                    for mo in range(8):
                        p.set_free("hsD_" + str(mo), "t")
                    for sub in range(2):
                        s0 = sub * 512
                        p.req_free("sync", "z1fD")
                        for s_ in range(8):
                            p.dma("sync", z1fD[:, s_ * 512:(s_ + 1) * 512],
                                  z1_sp[b, s_, :, t0 + s0:t0 + s0 + 512], f"dzf{pb}")
                        p.set_ready("z1fD", f"dzf{pb}")
                        for m in range(8):
                            bki = 6 + (m % 2)
                            p.req_free("tensor", f"bk{bki}")
                            p.op("tensor", (lambda t, m=m, bki=bki, pb=pb, s0=s0: t.matmul(
                                banks[bki][:, 0:512], b2B[:, m * 128:(m + 1) * 128],
                                coefD8[pb][:, s0:s0 + 512], start=True, stop=True)), inc="t")
                            p.set_ready(f"bk{bki}", "t")
                            p.req_ready("vector", f"bk{bki}")
                            p.req_ready("vector", "z1fD")
                            p.op("vector", (lambda v, m=m, bki=bki, s0=s0: v.tensor_add(
                                zacc[:, m * 1024 + s0:m * 1024 + s0 + 512],
                                zacc[:, m * 1024 + s0:m * 1024 + s0 + 512],
                                banks[bki][:, 0:512])), inc="v")
                            p.wait("vector", "v", p.val("v"))
                            p.op("vector", (lambda v, m=m, s0=s0: v.tensor_add(
                                zacc[:, m * 1024 + s0:m * 1024 + s0 + 512],
                                zacc[:, m * 1024 + s0:m * 1024 + s0 + 512],
                                z1fD[:, m * 512:(m + 1) * 512])), inc="v")
                            p.wait("vector", "v", p.val("v"))
                            p.set_free(f"bk{bki}", "v")
                        p.set_ready("zpreD", "v")
                        p.req_ready("tensor", "zpreD")
                        p.req_free("tensor", "bk6")
                        for s_ in range(8):
                            p.op("tensor", (lambda t, s=s_, s0=s0: t.matmul(
                                banks[6][0:1, 0:512], onescolf,
                                zacc[:, s * 1024 + s0:s * 1024 + s0 + 512],
                                start=(s == 0), stop=(s == 7))), inc="t")
                        p.set_ready("bk6mu", "t")
                        p.req_free("tensor", "bk7")
                        for s_ in range(8):
                            p.req_free("vector", "sqD")
                            p.op("vector", (lambda v, s=s_, s0=s0: v.tensor_mul(
                                sqD, zacc[:, s * 1024 + s0:s * 1024 + s0 + 512],
                                zacc[:, s * 1024 + s0:s * 1024 + s0 + 512])), inc="v")
                            p.set_ready("sqD", "v")
                            p.req_ready("tensor", "sqD")
                            p.op("tensor", (lambda t, s=s_: t.matmul(
                                banks[7][0:1, 0:512], onescolf, sqD,
                                start=(s == 0), stop=(s == 7))), inc="t")
                            p.set_free("sqD", "t")
                        p.set_ready("bk7s", "t")
                        p.req_ready("vector", "bk6mu")
                        p.req_ready("vector", "bk7s")
                        p.req_free("vector", "rowDmu")
                        p.op("vector", lambda v: v.tensor_scalar(rowDmu, banks[6][0:1, 0:512],
                                                                 1.0 / 1024.0, None, ALU.mult), inc="v")
                        p.wait("vector", "v", p.val("v"))
                        p.req_free("vector", "rowD2")
                        p.op("vector", lambda v: v.tensor_mul(rowD2, rowDmu, rowDmu), inc="v")
                        p.wait("vector", "v", p.val("v"))
                        p.op("vector", lambda v: v.scalar_tensor_tensor(
                            rowD2, banks[7][0:1, 0:512], 1.0 / 1024.0, rowD2, ALU.mult,
                            ALU.subtract), inc="v")
                        p.wait("vector", "v", p.val("v"))
                        p.set_free("bk6", "v")
                        p.set_free("bk7", "v")
                        p.set_ready("varD", "v")
                        p.req_ready("scalar", "varD")
                        p.op("scalar", lambda sc: sc.activation(rowD2, rowD2, AF.Ln,
                                                                bias=cst[0:1, CC_EPS5:CC_EPS5 + 1]), inc="s")
                        p.wait("scalar", "s", p.val("s"))
                        p.op("scalar", lambda sc: sc.activation(rowD2, rowD2, AF.Exp,
                                                                scale=cst[0:1, CC_NHALF:CC_NHALF + 1]), inc="s")
                        p.set_ready("rstd2", "s")
                        p.req_ready("vector", "rstd2")
                        p.op("vector", lambda v: v.tensor_mul(rowDmu, rowDmu, rowD2), inc="v")
                        p.set_ready("murstd2", "v")
                        p.req_ready("tensor", "murstd2")
                        p.req_free("tensor", "bk6")
                        p.req_free("tensor", "bk7")
                        p.op("tensor", lambda t: t.matmul(banks[6][:, 0:512], ones1f, rowD2,
                                                          start=True, stop=True), inc="t")
                        p.op("tensor", lambda t: t.matmul(banks[7][:, 0:512],
                                                          ones1f, rowDmu, start=True, stop=True), inc="t")
                        p.set_ready("bk6", "t")
                        p.req_ready("vector", "bk6")
                        p.req_free("vector", "z2D")
                        for s_ in range(8):
                            p.op("vector", (lambda v, s=s_, s0=s0: v.tensor_mul(
                                z2D[:, s * 512:(s + 1) * 512],
                                zacc[:, s * 1024 + s0:s * 1024 + s0 + 512],
                                banks[6][:, 0:512])), inc="v")
                            p.wait("vector", "v", p.val("v"))
                            p.op("vector", (lambda v, s=s_: v.tensor_sub(
                                z2D[:, s * 512:(s + 1) * 512], z2D[:, s * 512:(s + 1) * 512],
                                banks[7][:, 0:512])), inc="v")
                            p.wait("vector", "v", p.val("v"))
                        p.set_free("bk6", "v")
                        p.set_free("bk7", "v")
                        p.set_ready("z2D", "v")
                        p.req_ready("tensor", "z2D")
                        for tb in range(4):
                            p.req_free("scalar", "outst")
                            for s_ in range(8):
                                bki = 4 + (s_ % 2)
                                p.req_free("tensor", f"bk{bki}")
                                p.op("tensor", (lambda t, s=s_, tb=tb, bki=bki: t.transpose(
                                    banks[bki][:, 0:128],
                                    z2D[:, s * 512 + tb * 128:s * 512 + (tb + 1) * 128],
                                    identf)), inc="t")
                                p.set_ready(f"bk{bki}", "t")
                                p.req_ready("scalar", f"bk{bki}")
                                p.op("scalar", (lambda sc, s=s_, bki=bki: sc.copy(
                                    outst[:, s * 128:(s + 1) * 128], banks[bki][:, 0:128])), inc="s")
                                p.set_free(f"bk{bki}", "s")
                            p.set_ready("outst", "s")
                            p.req_ready("gpsimd", "outst")
                            p.dma("gpsimd", out[b, t0 + s0 + tb * 128:t0 + s0 + (tb + 1) * 128, :],
                                  outst, "st")
                            p.set_free("outst", "st")
                        p.set_free("z1fD", "t")
                    p.set_free("zacc", "v")
                    p.set_free(f"z1b{pb}", "t")
                    p.set_free(f"coefD{pb}", "t")

        for qs in sorted(p.qsems):
            p.wait("sync", qs)

        p.verify()
        p.playback(ctx.enter_context(nc.Block()))
    return nc


def kernel(**inputs):
    percore = host_inputs(inputs)
    nc = build()
    res = run_bass_kernel_spmd(nc, percore, core_ids=list(range(8)))
    return np.concatenate([r["out"] for r in res.results], axis=0)
